# version 19
# baseline (speedup 1.0000x reference)
import contextlib
import functools
import numpy as np
import ml_dtypes
import concourse.bass as bass
import concourse.mybir as mybir
from concourse.bass_utils import run_bass_kernel_spmd

F32 = mybir.dt.float32
BF16 = mybir.dt.bfloat16
AF = mybir.ActivationFunctionType
ALU = mybir.AluOpType
AX = mybir.AxisListType
ENGINES = ("sync", "tensor", "vector", "scalar", "gpsimd")

NT = 32
SLOT_HD = (0, 2, 1, 3)
D = 1024
H = 16
DK = 32
DV = 64
SCALE = DK ** -0.5
EPS = 1e-5


class _CntView:

    def __init__(self, p):
        self.p = p

    def __getitem__(self, name):
        h = self.p.sem(name)
        base = self.p.name2h[name][1]
        return self.p._hcnt[h.name] - base

    def get(self, name, default=0):
        if name in self.p.name2h:
            h, base = self.p.name2h[name]
            return self.p._hcnt[h.name] - base
        return default


class P:

    def __init__(self, nc):
        self.nc = nc
        self.stack = contextlib.ExitStack()
        self.scopes = []
        self.tape = {e: [] for e in ENGINES}
        self.name2h = {}
        self.free_h = []
        self.sems = {}
        self.cnt = _CntView(self)
        self._hcnt = {}
        self._nphys = 0
        self.pending = {e: [] for e in ENGINES}
        self._snap = {}
        self._nbar = 0

    def push_scope(self):
        self.scopes.append(contextlib.ExitStack())
        self._snap = dict(self._hcnt)

    def pop_scope(self):
        self.scopes.pop().close()

    def barrier(self):
        self._nbar += 1
        bar = f"_bar{self._nbar}"
        changed = [(pn, self._hcnt[pn]) for pn in self._hcnt
                   if self._hcnt[pn] != self._snap.get(pn, 0)]
        for eng in ENGINES:
            if self.tape[eng]:
                self.fence(eng)
            self.op(eng, "nop", wait_phys=changed, inc=(bar, 1))
        barh = self.name2h[bar][0]
        for eng in ENGINES:
            self.pending[eng].append((barh, 5))
        for n, (h, base) in self.name2h.items():
            self.free_h.append(h)
        self.name2h = {}
        self._snap = dict(self._hcnt)

    def fence(self, eng):
        tape = self.tape[eng]
        assert tape, "fence with empty tape"
        ws, method, args, kw, inc = tape[-1]
        if inc is None:
            h = self.sem(f"_f_{eng}")
            self._hcnt[h.name] += 1
            inc = (h, 1)
            tape[-1] = (ws, method, args, kw, inc)
        self.pending[eng].append((inc[0], self._hcnt[inc[0].name]))

    def sem(self, name):
        if name not in self.name2h:
            if self.free_h:
                h = self.free_h.pop()
            else:
                pn = f"S{self._nphys}"
                self._nphys += 1
                h = self.stack.enter_context(self.nc.semaphore(pn))
                self.sems[pn] = h
                self._hcnt[pn] = 0
            self.name2h[name] = (h, self._hcnt[h.name])
        return self.name2h[name][0]

    def sb(self, name, shape, dtype):
        st = self.scopes[-1] if self.scopes else self.stack
        return st.enter_context(self.nc.sbuf_tensor(name, shape, dtype))

    def ps(self, name, shape, dtype):
        st = self.scopes[-1] if self.scopes else self.stack
        return st.enter_context(self.nc.psum_tensor(name, shape, dtype))

    def op(self, eng, method, *args, wait=(), inc=None, dma=False,
           wait_phys=(), **kw):
        ws = []
        if self.pending[eng]:
            ws.extend(self.pending[eng])
            self.pending[eng] = []
        for w in wait:
            s, v = w
            if int(v) <= 0:
                continue
            h = self.sem(s)
            base = self.name2h[s][1]
            ws.append((h, base + int(v)))
        for pn, v in wait_phys:
            ws.append((self.sems[pn], int(v)))
        if inc is not None:
            if isinstance(inc, str):
                inc = (inc, 16 if dma else 1)
            if isinstance(inc[0], str):
                inc = (self.sem(inc[0]), inc[1])
            self._hcnt[inc[0].name] += inc[1]
        self.tape[eng].append((ws, method, args, kw, inc))

    def build(self):
        self.check_deadlock()
        with self.nc.Block() as block:
            for eng in ENGINES:
                if self.tape[eng]:
                    getattr(block, eng)(functools.partial(self._run, eng))

    def _run(self, eng, e):
        for ws, method, args, kw, inc in self.tape[eng]:
            for (h, v) in ws:
                if v > 0:
                    e.wait_ge(h, v)
            inst = getattr(e, method)(*args, **kw)
            if inc is not None:
                inst.then_inc(inc[0], inc[1])

    def check_deadlock(self):
        pos = {e: 0 for e in ENGINES}
        val = {pn: 0 for pn in self.sems}
        total = sum(len(t) for t in self.tape.values())
        done = 0
        progress = True
        while progress:
            progress = False
            for e in ENGINES:
                while pos[e] < len(self.tape[e]):
                    ws, method, args, kw, inc = self.tape[e][pos[e]]
                    if any(val[h.name] < v for h, v in ws):
                        break
                    if inc is not None:
                        val[inc[0].name] += inc[1]
                    pos[e] += 1
                    done += 1
                    progress = True
        if done != total:
            stuck = {e: (pos[e], len(self.tape[e]),
                         [(h.name, v) for h, v in self.tape[e][pos[e]][0]]
                         if pos[e] < len(self.tape[e]) else None)
                     for e in ENGINES}
            raise RuntimeError(f"deadlock: retired {done}/{total}; heads={stuck}")


def np_bf16(x):
    return np.asarray(x, np.float32).astype(ml_dtypes.bfloat16)


def dbg(p, env, name, ap, T, wait):
    if not env.get("debug"):
        return
    d = env.setdefault("dbg_tensors", {})
    if name not in d:
        shape = [NT] + list(ap.shape)
        d[name] = p.nc.dram_tensor("dbg_" + name, shape, ap.dtype,
                                   kind="ExternalOutput").ap()
    p.op("sync", "dma_start", out=d[name][T], in_=ap, wait=wait, inc="dbgs",
         dma=True)



def emit_consts(p, env):
    nc = p.nc
    ident = np.eye(128, dtype=ml_dtypes.bfloat16)
    s_idx = np.arange(128)[:, None]
    t_idx = np.arange(128)[None, :]
    ut = (s_idx <= t_idx)
    lt = (s_idx >= t_idx)
    env["ident_b"] = p.sb("ident_b", [128, 128], BF16)
    env["UT_b"] = p.sb("UT_b", [128, 128], BF16)
    env["LT_b"] = p.sb("LT_b", [128, 128], BF16)
    env["UT_f"] = p.sb("UT_f", [128, 128], F32)
    env["LT_f"] = p.sb("LT_f", [128, 128], F32)
    c_ident = nc.inline_tensor(ident, name="c_ident").ap()
    c_utb = nc.inline_tensor(ut.astype(ml_dtypes.bfloat16), name="c_utb").ap()
    c_ltb = nc.inline_tensor(lt.astype(ml_dtypes.bfloat16), name="c_ltb").ap()
    c_nw0 = nc.inline_tensor((np.arange(128, dtype=np.float32) + 1.0
                              ).reshape(128, 1), name="c_nw0").ap()
    c_utf = nc.inline_tensor(ut.astype(np.float32), name="c_utf").ap()
    c_ltf = nc.inline_tensor(lt.astype(np.float32), name="c_ltf").ap()
    p.op("sync", "dma_start", out=env["ident_b"][:, :], in_=c_ident, inc="cld", dma=True)
    p.op("sync", "dma_start", out=env["UT_b"][:, :], in_=c_utb, inc="cld", dma=True)
    p.op("sync", "dma_start", out=env["UT_f"][:, :], in_=c_utf, inc="cld", dma=True)
    p.op("sync", "dma_start", out=env["LT_f"][:, :], in_=c_ltf, inc="cld", dma=True)
    env["nw0"] = p.sb("nw0", [128, 1], F32)
    p.op("sync", "dma_start", out=env["LT_b"][:, :], in_=c_ltb, inc="cld", dma=True)
    p.op("sync", "dma_start", out=env["nw0"][:, :], in_=c_nw0, inc="cld", dma=True)
    env["cld_target"] = 96

    env["ones1_b"] = p.sb("ones1_b", [1, 128], BF16)
    env["onesc_f"] = p.sb("onesc_f", [128, 1], F32)
    env["c_one"] = p.sb("c_one", [128, 1], F32)
    env["c_eps"] = p.sb("c_eps", [128, 1], F32)
    env["ones1_f"] = p.sb("ones1_f", [1, 128], F32)
    p.op("vector", "memset", env["ones1_f"][:, :], 1.0)
    p.op("vector", "memset", env["ones1_b"][:, :], 1.0)
    p.op("vector", "memset", env["onesc_f"][:, :], 1.0)
    p.op("vector", "memset", env["c_one"][:, :], 1.0)
    p.op("vector", "memset", env["c_eps"][:, :], EPS, inc="cms")
    env["cms_target"] = p.cnt["cms"]


def emit_weight_prep(p, env, specs):
    wst = env.setdefault("wst", p.sb("wst", [128, 1024], F32))
    i = env.setdefault("wp_i", 0)
    for (src, rows, cols, dst, coff) in specs:
        nchunk = (rows + 127) // 128
        for k in range(nchunk):
            r0 = k * 128
            r1 = min(rows, r0 + 128)
            nr = r1 - r0
            stg = wst[0:nr, 0:cols] if (i % 2 == 0) else wst[0:nr, 0:cols]
            p.op("sync", "dma_start", out=wst[0:nr, 0:cols], in_=src[r0:r1, :],
                 wait=[("wcast", i)], inc="wld", dma=True)
            p.op("vector", "tensor_copy",
                 out=dst[0:nr, coff + k * cols: coff + (k + 1) * cols],
                 in_=wst[0:nr, 0:cols],
                 wait=[("wld", 16 * (i + 1))], inc="wcast")
            i += 1
    env["wp_i"] = i
    return i



def emit_phase_a(p, env, io):
    nc = p.nc

    Wq_b = p.sb("Wq_b", [128, 8 * 512], BF16)
    Wk_b = p.sb("Wk_b", [128, 8 * 512], BF16)
    Wv_b = p.sb("Wv_b", [128, 8 * 1024], BF16)
    Wg_b = p.sb("Wg_b", [128, 8 * 1024], BF16)
    Wo_b = p.sb("Wo_b", [128, 8 * 1024], BF16)
    Wg1_b = p.sb("Wg1_b", [128, 8 * 16], BF16)
    Wg2_b = p.sb("Wg2_b", [16, 512], BF16)
    bg_b = p.sb("bg_b", [1, 1024], BF16)
    emit_weight_prep(p, env, [
        (io["Wq"], 1024, 512, Wq_b, 0),
        (io["Wk"], 1024, 512, Wk_b, 0),
        (io["Wv"], 1024, 1024, Wv_b, 0),
        (io["Wg"], 1024, 1024, Wg_b, 0),
        (io["Wo"], 1024, 1024, Wo_b, 0),
        (io["Wg1"], 1024, 16, Wg1_b, 0),
        (io["Wg2"], 16, 512, Wg2_b, 0),
        (io["bg"].rearrange("(o d) -> o d", o=1), 1, 1024, bg_b, 0),
    ])
    NW = env["wp_i"]

    PT = p.ps("PT", [128, 1024], BF16)
    PA = p.ps("PA", [128, 512], F32)
    PB = p.ps("PB", [128, 512], F32)
    PC = p.ps("PC", [128, 512], F32)
    PD = p.ps("PD", [128, 512], F32)
    PVO = p.ps("PVO", [128, 1024], F32)
    PK = p.ps("PK", [128, 512], F32)

    xt = [p.sb(f"xt{i}", [128, 1024], F32) for i in range(2)]
    xsq = p.sb("xsq", [128, 1024], F32)
    ssq = p.sb("ssq", [128, 1], F32)
    sumx = p.sb("sumx", [128, 1], F32)
    mean = p.sb("mean", [128, 1], F32)
    msq = p.sb("msq", [128, 1], F32)
    var = p.sb("var", [128, 1], F32)
    tv1 = p.sb("tv1", [128, 1], F32)
    lnv = p.sb("lnv", [128, 1], F32)
    rstd = p.sb("rstd", [128, 1], F32)
    nm = p.sb("nm", [128, 1], F32)
    xnb = p.sb("xnb", [128, 1024], BF16)
    xnf = p.sb("xnf", [128, 1024], F32)
    xnT = p.sb("xnT", [128, 1024], BF16)
    t1b = p.sb("t1b", [16, 128], BF16)
    e1 = p.sb("e1", [128, 512], F32)
    u_tok = p.sb("u_tok", [128, 512], F32)
    A_b = p.sb("A_b", [128, 512], BF16)
    iA_b = p.sb("iA_b", [128, 512], BF16)
    A_end = p.sb("A_end", [128, 4], F32)
    qt_b = p.sb("qt_b", [128, 512], BF16)
    kt_b = p.sb("kt_b", [128, 512], BF16)
    qfm = p.sb("qfm", [128, 512], BF16)
    kfm = p.sb("kfm", [128, 512], BF16)
    v_b = p.sb("v_b", [128, 1024], BF16)
    P_b = [p.sb(f"P_b{i}", [128, 128], BF16) for i in range(2)]
    S_sb = p.sb("S_sb", [128, 256], F32)
    S_bb = p.sb("S_bb", [128, 256], BF16)
    t_s = p.sb("t_s", [128, 256], F32)
    o_mu = p.sb("o_mu", [128, 16], F32)
    o_s2 = p.sb("o_s2", [128, 16], F32)
    om_m = p.sb("om_m", [128, 16], F32)
    msq2 = p.sb("msq2", [128, 16], F32)
    var2 = p.sb("var2", [128, 16], F32)
    tv2 = p.sb("tv2", [128, 16], F32)
    lnv2 = p.sb("lnv2", [128, 16], F32)
    rstd2 = p.sb("rstd2", [128, 16], F32)
    nm2 = p.sb("nm2", [128, 16], F32)
    osq_sb = p.sb("osq_sb", [128, 1024], F32)
    o_n = p.sb("o_n", [128, 1024], F32)
    o_raw = p.sb("o_raw", [128, 1024], F32)
    sg = p.sb("sg", [128, 1024], F32)
    eg = p.sb("eg", [128, 512], F32)
    gate_f = p.sb("gate_f", [128, 1024], F32)
    go_b = p.sb("go_b", [128, 1024], BF16)
    goT_sb = p.sb("goT_sb", [128, 1024], BF16)
    r1_sb = [p.sb(f"r1_sb{i}", [128, 1024], F32) for i in range(2)]
    bmax = p.sb("bmax", [128, 64], F32)
    scl = p.sb("scl", [128, 64], F32)
    rscl = p.sb("rscl", [128, 64], F32)
    a_n = p.sb("a_n", [128, 1024], F32)
    accq = p.sb("accq", [128, 1024], F32)
    qtmp = p.sb("qtmp", [128, 1024], F32)
    qt2 = p.sb("qt2", [128, 1024], F32)
    stair = p.sb("stair", [128, 1024], F32)
    sgn = p.sb("sgn", [128, 1024], F32)
    sscl = p.sb("sscl", [128, 1024], F32)
    xq_b = [p.sb(f"xq_b{i}", [128, 1024], BF16) for i in range(2)]
    m8A = p.sb("m8A", [128, 1024], mybir.dt.uint8)
    QA = {"bmax": bmax, "scl": scl, "rscl": rscl,
          "a": a_n, "acc2": accq, "tmp2": qtmp, "t22": qt2, "sscl": sscl,
          "m8": m8A}

    ident_b = env["ident_b"]
    UT_b = env["UT_b"]
    UT_f = env["UT_f"]
    ones1_b = env["ones1_b"]
    onesc_f = env["onesc_f"]
    c_one = env["c_one"]
    c_eps = env["c_eps"]

    p.op("vector", "memset", S_sb[:, :], 0.0, inc="cms")

    first_waits = [("wcast", NW), ("cld", env["cld_target"]), ("cms", p.cnt["cms"])]
    fw = {e: list(first_waits) for e in ENGINES}

    def W(eng, extra=()):
        ws = fw[eng] + list(extra)
        fw[eng] = []
        return ws

    x0 = io["x0"]
    r1_d = io["r1"]
    xq_d = io["xq"]

    for T in range(NT):
        t0 = T * 128
        dbgw = [("dbgs", p.cnt.get("dbgs", 0))] if env.get("debug") else []
        p.op("sync", "dma_start", out=xt[T % 2][:, :], in_=x0[t0:t0 + 128, :],
             wait=W("sync", [("r1a", T - 1)] + dbgw), inc="ldx", dma=True)

        p.op("scalar", "activation", xsq[:, :], xt[T % 2][:, :], AF.Square,
             accum_out=ssq[:, :], wait=W("scalar", [("ldx", 16 * (T + 1))]), inc="lnq")
        p.op("vector", "tensor_reduce", sumx[:, :], xt[T % 2][:, :], AX.X, ALU.add,
             wait=W("vector", [("ldx", 16 * (T + 1))]))
        p.fence("vector")
        p.op("vector", "tensor_scalar", mean[:, :], sumx[:, :], 1.0 / D, None, ALU.mult)
        p.op("vector", "tensor_tensor", msq[:, :], sumx[:, :], sumx[:, :], ALU.mult)
        p.op("vector", "tensor_scalar", tv1[:, :], ssq[:, :], 1.0 / D, None, ALU.mult,
             wait=[("lnq", T + 1)])
        p.fence("vector")
        p.op("vector", "scalar_tensor_tensor", var[:, :], msq[:, :],
             -1.0 / (D * D), tv1[:, :], ALU.mult, ALU.add, inc="lns")
        p.op("scalar", "activation", lnv[:, :], var[:, :], AF.Ln, bias=c_eps[:, :],
             wait=[("lns", T + 1)])
        p.fence("scalar")
        p.op("scalar", "activation", rstd[:, :], lnv[:, :], AF.Exp, scale=-0.5,
             inc="rsd")
        p.op("vector", "scalar_tensor_tensor", nm[:, :], mean[:, :], -1.0,
             rstd[:, :], ALU.mult, ALU.mult, wait=[("rsd", T + 1)], inc="nmr")
        p.op("scalar", "activation", xnb[:, :], xt[T % 2][:, :], AF.Identity,
             bias=nm[:, :], scale=rstd[:, :],
             wait=[("nmr", T + 1), ("tTn", T)], inc="xnr")
        p.op("scalar", "activation", xnf[:, :], xt[T % 2][:, :], AF.Identity,
             bias=nm[:, :], scale=rstd[:, :], wait=[("qst", T)], inc="xfr")

        dbg(p, env, "xnb", xnb[:, :], T, [("xnr", T + 1)])
        dbg(p, env, "xnf", xnf[:, :], T, [("xfr", T + 1)])
        for j in range(8):
            p.op("tensor", "transpose", PT[:, 128 * j:128 * (j + 1)],
                 xnb[:, 128 * j:128 * (j + 1)], ident_b[:, :],
                 wait=(W("tensor", [("xnr", T + 1), ("goe", T)]) if j == 0 else ()),
                 inc=("tTn" if j == 7 else None))
        p.op("vector", "tensor_copy", out=xnT[:, :], in_=PT[:, :],
             wait=[("tTn", T + 1), ("pgt", T)], inc="xne")

        for k in range(8):
            p.op("tensor", "matmul", PC[0:16, 0:128], Wg1_b[:, 16 * k:16 * (k + 1)],
                 xnT[:, 128 * k:128 * (k + 1)],
                 start=(k == 0), stop=(k == 7),
                 wait=([("xne", T + 1), ("r1a", T)] if k == 0 else ()),
                 inc=("pt1" if k == 7 else None))
        p.op("vector", "tensor_copy", out=t1b[:, :], in_=PC[0:16, 0:128],
             wait=[("pt1", T + 1), ("plg", T)], inc="t1e")
        p.op("tensor", "matmul", PA[:, 0:512], t1b[:, :], Wg2_b[:, :],
             start=True, stop=True,
             wait=[("t1e", T + 1), ("gfr", T), ("sg_", T)], inc="plg")
        p.op("scalar", "activation", e1[:, :], PA[:, 0:512], AF.Exp, scale=-1.0,
             wait=[("plg", T + 1)], inc="ae1")
        p.op("scalar", "activation", u_tok[:, :], e1[:, :], AF.Ln, bias=c_one[:, :],
             wait=[("pce", T)], inc="ulg")
        p.op("tensor", "matmul", PB[:, 0:512], UT_f[:, :], u_tok[:, :],
             start=True, stop=True,
             wait=[("ulg", T + 1), ("gfr", T), ("sg_", T)], inc="pcm")
        for c in range(4):
            p.op("tensor", "matmul", PC[:, 128 + c:129 + c],
                 u_tok[:, 128 * c:128 * (c + 1)], onesc_f[:, :],
                 start=True, stop=True, inc=("pce" if c == 3 else None))
        p.op("scalar", "activation", A_b[:, :], PB[:, 0:512], AF.Exp, scale=-1.0 / 16,
             wait=[("pcm", T + 1), ("qte", T), ("kte", T), ("sup", T)])
        p.op("scalar", "activation", iA_b[:, :], PB[:, 0:512], AF.Exp, scale=1.0 / 16)
        p.op("scalar", "activation", A_end[:, :], PC[:, 128:132], AF.Exp,
             scale=-1.0 / 16, wait=[("pce", T + 1)], inc="gA")

        dbg(p, env, "u_tok", u_tok[:, :], T, [("ulg", T + 1)])
        dbg(p, env, "A_b", A_b[:, :], T, [("gA", T + 1)])
        dbg(p, env, "iA_b", iA_b[:, :], T, [("gA", T + 1)])
        dbg(p, env, "A_end", A_end[:, :], T, [("gA", T + 1)])
        for k in range(8):
            p.op("tensor", "matmul", PA[:, 0:512], xnT[:, 128 * k:128 * (k + 1)],
                 Wq_b[:, 512 * k:512 * (k + 1)], start=(k == 0), stop=(k == 7),
                 wait=([("ae1", T + 1)] if k == 0 else ()),
                 inc=("pq" if k == 7 else None))
        p.op("vector", "scalar_tensor_tensor", qt_b[:, :], PA[:, 0:512], 1.0,
             A_b[:, :], ALU.mult, ALU.mult,
             wait=[("pq", T + 1), ("gA", T + 1), ("tqk", T)], inc="qte")
        for k in range(8):
            p.op("tensor", "matmul", PA[:, 0:512], xnT[:, 128 * k:128 * (k + 1)],
                 Wk_b[:, 512 * k:512 * (k + 1)], start=(k == 0), stop=(k == 7),
                 wait=([("qte", T + 1)] if k == 0 else ()),
                 inc=("pk" if k == 7 else None))
        p.op("vector", "scalar_tensor_tensor", kt_b[:, :], PA[:, 0:512], SCALE,
             iA_b[:, :], ALU.mult, ALU.mult,
             wait=[("pk", T + 1), ("kvd", T)], inc="kte")
        for half in range(2):
            for k in range(8):
                p.op("tensor", "matmul", PVO[:, 512 * half:512 * (half + 1)],
                     xnT[:, 128 * k:128 * (k + 1)],
                     Wv_b[:, 1024 * k + 512 * half:1024 * k + 512 * (half + 1)],
                     start=(k == 0), stop=(k == 7),
                     wait=([("oln", T)] if (half == 0 and k == 0) else ()),
                     inc=("pv" if (half == 1 and k == 7) else None))
        p.op("vector", "tensor_copy", out=v_b[:, :], in_=PVO[:, :],
             wait=[("pv", T + 1), ("kvd", T)], inc="ve")

        dbg(p, env, "qt_b", qt_b[:, :], T, [("qte", T + 1)])
        dbg(p, env, "kt_b", kt_b[:, :], T, [("kte", T + 1)])
        dbg(p, env, "v_b", v_b[:, :], T, [("ve", T + 1)])
        for j in range(4):
            p.op("tensor", "transpose", PT[:, 128 * j:128 * (j + 1)],
                 qt_b[:, 128 * j:128 * (j + 1)], ident_b[:, :],
                 wait=([("kte", T + 1)] if j == 0 else ()))
        for j in range(4):
            p.op("tensor", "transpose", PT[:, 512 + 128 * j:512 + 128 * (j + 1)],
                 kt_b[:, 128 * j:128 * (j + 1)], ident_b[:, :],
                 inc=("tqk" if j == 3 else None))
        p.op("vector", "tensor_copy", out=qfm[:, :], in_=PT[:, 0:512],
             wait=[("tqk", T + 1), ("ohd", T)])
        p.op("vector", "tensor_copy", out=kfm[:, :], in_=PT[:, 512:1024],
             inc="fme")
        p.op("vector", "tensor_copy", out=S_bb[:, :], in_=S_sb[:, :], inc="sbc")

        for h in range(16):
            bank = PC if h % 2 == 0 else PD
            pb = 32 * (h % 4)
            fc = 128 * (h // 4)
            hw = [("pbm", 16 * T + h - 1)] if h >= 2 else []
            if h == 0:
                hw += [("fme", T + 1), ("ve", T + 1), ("sbc", T + 1),
                       ("gA", T + 1), ("sup", T), ("pbm", 16 * T)]
            p.op("tensor", "matmul", bank[:, 0:128],
                 kfm[pb:pb + 32, fc:fc + 128], qfm[pb:pb + 32, fc:fc + 128],
                 start=True, stop=True, tile_position=(pb, 0),
                 wait=hw, inc=("scd", 1))
            p.op("vector", "tensor_tensor", P_b[h % 2][:, :], bank[:, 0:128],
                 UT_b[:, :], ALU.mult,
                 wait=[("scd", 16 * T + h + 1)], inc=("pbm", 1))
            p.op("tensor", "matmul", PVO[:, 64 * h:64 * (h + 1)], P_b[h % 2][:, :],
                 v_b[:, 64 * h:64 * (h + 1)], start=True, stop=False,
                 wait=[("pbm", 16 * T + h + 1)])
            p.op("tensor", "matmul", PVO[:, 64 * h:64 * (h + 1)],
                 qfm[pb:pb + 32, fc:fc + 128],
                 S_bb[pb:pb + 32, 64 * (h // 4):64 * (h // 4) + 64],
                 start=False, stop=True, tile_position=(pb, 0),
                 inc=("ohd" if h == 15 else None))
            p.op("tensor", "matmul", PK[pb:pb + 32, 64 * (h // 4):64 * (h // 4) + 64],
                 kt_b[:, 32 * h:32 * (h + 1)], v_b[:, 64 * h:64 * (h + 1)],
                 start=True, stop=True, tile_position=(0, pb),
                 inc=("kvd" if h == 15 else None))

        p.op("vector", "tensor_tensor", t_s[:, :], PK[:, 0:256], S_sb[:, :], ALU.add,
             wait=[("kvd", T + 1)])
        for c in range(4):
            p.op("vector", "tensor_scalar", S_sb[:, 64 * c:64 * (c + 1)],
                 t_s[:, 64 * c:64 * (c + 1)], A_end[:, c:c + 1], None, ALU.mult,
                 inc=("sup" if c == 3 else None))

        dbg(p, env, "S_sb", S_sb[:, :], T, [("sup", T + 1)])
        if env.get("debug"):
            p.op("vector", "tensor_copy", out=o_raw[:, :], in_=PVO[:, :],
                 wait=[("ohd", T + 1)], inc="oraw")
            dbg(p, env, "o_raw", o_raw[:, :], T, [("oraw", T + 1)])
        dbg(p, env, "qfm", qfm[:, :], T, [("fme", T + 1)])
        dbg(p, env, "kfm", kfm[:, :], T, [("fme", T + 1)])
        dbg(p, env, "S_bb", S_bb[:, :], T, [("sbc", T + 1)])
        for half, bank in ((0, PA), (1, PB)):
            for k in range(8):
                p.op("tensor", "matmul", bank[:, 0:512],
                     xnT[:, 128 * k:128 * (k + 1)],
                     Wg_b[:, 1024 * k + 512 * half:1024 * k + 512 * (half + 1)],
                     start=(k == 0), stop=False)
            p.op("tensor", "matmul", bank[:, 0:512], ones1_b[:, :],
                 bg_b[0:1, 512 * half:512 * (half + 1)], start=False, stop=True,
                 inc=("pgt" if half == 1 else None))

        p.op("vector", "tensor_reduce", o_mu[:, :],
             PVO[:, :].rearrange("p (h e) -> p h e", h=16), AX.X, ALU.add,
             wait=[("ohd", T + 1)])
        p.op("scalar", "activation", osq_sb[:, :], PVO[:, :], AF.Square,
             wait=[("ohd", T + 1), ("oln", T)], inc="osq")
        p.op("vector", "tensor_reduce", o_s2[:, :],
             osq_sb[:, :].rearrange("p (h e) -> p h e", h=16), AX.X, ALU.add,
             wait=[("osq", T + 1)])
        p.fence("vector")
        p.op("vector", "tensor_tensor", msq2[:, :], o_mu[:, :], o_mu[:, :], ALU.mult)
        p.op("vector", "tensor_scalar", tv2[:, :], o_s2[:, :], 1.0 / DV, None,
             ALU.mult)
        p.fence("vector")
        p.op("vector", "scalar_tensor_tensor", var2[:, :], msq2[:, :],
             -1.0 / (DV * DV), tv2[:, :], ALU.mult, ALU.add, inc="ovr")
        p.op("scalar", "activation", lnv2[:, :], var2[:, :], AF.Ln, bias=c_eps[:, :],
             wait=[("ovr", T + 1)])
        p.fence("scalar")
        p.op("scalar", "activation", rstd2[:, :], lnv2[:, :], AF.Exp, scale=-0.5,
             inc="rs2")
        p.op("vector", "scalar_tensor_tensor", nm2[:, :], o_mu[:, :], -1.0 / DV,
             rstd2[:, :], ALU.mult, ALU.mult, wait=[("rs2", T + 1)])
        p.fence("vector")
        p.op("vector", "tensor_tensor",
             o_n[:, :].rearrange("p (h e) -> p h e", h=16),
             PVO[:, :].rearrange("p (h e) -> p h e", h=16),
             rstd2[:, :].rearrange("p (h e) -> p h e", e=1).broadcast_to((128, 16, 64)),
             ALU.mult)
        p.op("vector", "tensor_tensor",
             o_n[:, :].rearrange("p (h e) -> p h e", h=16),
             o_n[:, :].rearrange("p (h e) -> p h e", h=16),
             nm2[:, :].rearrange("p (h e) -> p h e", e=1).broadcast_to((128, 16, 64)),
             ALU.add, inc="oln")

        dbg(p, env, "o_n", o_n[:, :], T, [("oln", T + 1)])
        dbg(p, env, "o_mu", o_mu[:, :], T, [("oln", T + 1)])
        dbg(p, env, "var2", var2[:, :], T, [("oln", T + 1)])
        for half, bank in ((0, PA), (1, PB)):
            p.op("scalar", "activation", eg[:, :], bank[:, 0:512], AF.Exp, scale=-1.0,
                 wait=([("pgt", T + 1), ("gfr", T)] if half == 0 else ()))
            p.op("scalar", "activation", sg[:, 512 * half:512 * (half + 1)],
                 eg[:, :], AF.Ln, bias=c_one[:, :])
        p.op("scalar", "activation", sg[:, :], sg[:, :], AF.Exp, scale=-1.0,
             inc="sg_")
        for half, bank in ((0, PA), (1, PB)):
            p.op("vector", "tensor_tensor", gate_f[:, 512 * half:512 * (half + 1)],
                 bank[:, 0:512], sg[:, 512 * half:512 * (half + 1)], ALU.mult,
                 wait=([("sg_", T + 1), ("tgo", T)] if half == 0 else ()))
        p.op("vector", "tensor_tensor", go_b[:, :], o_n[:, :], gate_f[:, :],
             ALU.mult, inc="gfr")

        dbg(p, env, "sg", sg[:, :], T, [("sg_", T + 1)])
        dbg(p, env, "gate_f", gate_f[:, :], T, [("gfr", T + 1)])
        dbg(p, env, "go_b", go_b[:, :], T, [("gfr", T + 1)])
        for j in range(8):
            p.op("tensor", "transpose", PT[:, 128 * j:128 * (j + 1)],
                 go_b[:, 128 * j:128 * (j + 1)], ident_b[:, :],
                 wait=([("gfr", T + 1)] if j == 0 else ()),
                 inc=("tgo" if j == 7 else None))
        p.op("vector", "tensor_copy", out=goT_sb[:, :], in_=PT[:, :],
             wait=[("tgo", T + 1), ("pa0", T)], inc="goe")
        for half, bank in ((0, PC), (1, PD)):
            for k in range(8):
                p.op("tensor", "matmul", bank[:, 0:512],
                     goT_sb[:, 128 * k:128 * (k + 1)],
                     Wo_b[:, 1024 * k + 512 * half:1024 * k + 512 * (half + 1)],
                     start=(k == 0), stop=(k == 7),
                     wait=([("goe", T + 1)] if (half == 0 and k == 0) else ()),
                     inc=("pa0" if (half == 1 and k == 7) else None))
        p.op("vector", "tensor_tensor", r1_sb[T % 2][:, 0:512], PC[:, 0:512],
             xt[T % 2][:, 0:512], ALU.add,
             wait=[("pa0", T + 1), ("str", 16 * (T - 1))])
        p.op("vector", "tensor_tensor", r1_sb[T % 2][:, 512:1024], PD[:, 0:512],
             xt[T % 2][:, 512:1024], ALU.add, inc="r1a")
        p.op("gpsimd", "dma_start", out=r1_d[t0:t0 + 128, :], in_=r1_sb[T % 2][:, :],
             wait=W("gpsimd", [("r1a", T + 1)]), inc="str", dma=True)

        p.op("scalar", "activation", qt2[:, :], xnf[:, :], AF.Abs,
             wait=[("xfr", T + 1), ("qst", T)])
        p.op("scalar", "activation", sgn[:, :], xnf[:, :], AF.Sign,
             wait=[("qpl", T)], inc="qsg")
        emit_quant_stair(p, env, "vector", "gpsimd", "q", xnf, sgn,
                         xq_b[T % 2], 1024, QA,
                         pool_extra_wait=[("qsg", T + 1), ("stq", 16 * (T - 1))],
                         src_wait=[("xfr", T + 1)], abs_wait=[("qsg", T + 1)])
        p.op("gpsimd", "dma_start", out=xq_d[t0:t0 + 128, :], in_=xq_b[T % 2][:, :],
             wait=[("qpl", T + 1)], inc="stq", dma=True)

        dbg(p, env, "scl", scl[:, :], T, [("qst", T + 1)])
        dbg(p, env, "a_n", a_n[:, :], T, [("qst", T + 1)])
        dbg(p, env, "stair", stair[:, :], T, [("qst", T + 1)])

    fin = [("str", 16 * NT), ("stq", 16 * NT)]
    if env.get("debug"):
        fin.append(("dbgs", p.cnt.get("dbgs", 0)))
    p.op("gpsimd", "nop", wait=fin)



FFI = 2816
ICHUNKS = [(0, 512), (512, 512), (1024, 512), (1536, 512), (2048, 512), (2560, 256)]


def emit_quant_stair(p, env, dve, pool, pref, src_signed, src_sgn, out_bf, C, Q,
                     pool_extra_wait=(), src_wait=(), qlow_st_wait=None,
                     abs_wait=()):
    bmax = Q["bmax"]
    scl, rscl, a_n, m8 = Q["scl"], Q["rscl"], Q["a"], Q["m8"]
    tA, tB, tC = Q["acc2"], Q["tmp2"], Q["t22"]
    P1 = Q["sscl"]
    stair = Q["a"]
    NB = C // 16
    p.sem(pref + "pl")
    p.sem(pref + "st")
    p.sem(pref + "ql")
    p.op(dve, "tensor_reduce", bmax[:, 0:NB],
         src_signed[:, 0:C].rearrange("p (g w) -> p g w", w=16),
         AX.X, ALU.max, apply_absolute_value=True, wait=list(src_wait))
    p.fence(dve)
    p.op(dve, "tensor_scalar", scl[:, 0:NB], bmax[:, 0:NB], 1.0 / 6.0, 1e-12,
         ALU.mult, ALU.max, wait=[(pref + "pl", p.cnt[pref + "pl"])])
    p.fence(dve)
    p.op(dve, "reciprocal", rscl[:, 0:NB], scl[:, 0:NB])
    p.fence(dve)
    p.op(dve, "tensor_tensor",
         a_n[:, 0:C].rearrange("p (g w) -> p g w", w=16),
         tC[:, 0:C].rearrange("p (g w) -> p g w", w=16),
         rscl[:, 0:NB].rearrange("p (g w) -> p g w", w=1).broadcast_to((128, NB, 16)),
         ALU.mult, wait=list(abs_wait))
    p.fence(dve)
    p.op(dve, "nop", inc=pref + "qa")
    p.op(pool, "tensor_scalar", P1[:, 0:C], a_n[:, 0:C], 2.0, 8388608.0,
         ALU.mult, ALU.add,
         wait=[(pref + "qa", p.cnt[pref + "qa"]),
               (pref + "st", qlow_st_wait if qlow_st_wait is not None
                else p.cnt[pref + "st"])])
    p.fence(pool)
    p.op(pool, "tensor_scalar", P1[:, 0:C], P1[:, 0:C], -8388608.0, 0.5,
         ALU.add, ALU.mult, inc=pref + "ql")
    p.op(dve, "tensor_single_scalar", m8[:, 0:C], a_n[:, 0:C], 1.25, ALU.is_gt)
    p.op(dve, "tensor_scalar", tA[:, 0:C], a_n[:, 0:C], 4194305.0, None, ALU.mult)
    p.op(dve, "tensor_tensor", tB[:, 0:C], tA[:, 0:C], a_n[:, 0:C], ALU.subtract)
    p.op(dve, "tensor_tensor", tC[:, 0:C], tA[:, 0:C], tB[:, 0:C], ALU.subtract)
    p.op(dve, "select", stair[:, 0:C], m8[:, 0:C], tC[:, 0:C], P1[:, 0:C],
         wait=[(pref + "ql", p.cnt[pref + "ql"])], inc=pref + "st")
    p.op(pool, "tensor_tensor",
         P1[:, 0:C].rearrange("p (g w) -> p g w", w=16),
         src_sgn[:, 0:C].rearrange("p (g w) -> p g w", w=16),
         scl[:, 0:NB].rearrange("p (g w) -> p g w", w=1).broadcast_to((128, NB, 16)),
         ALU.mult,
         wait=[(pref + "st", p.cnt[pref + "st"])] + list(pool_extra_wait))
    p.op(pool, "tensor_tensor", out_bf[:, 0:C], stair[:, 0:C], P1[:, 0:C],
         ALU.mult, inc=pref + "pl")


def emit_phase_b(p, env, io):
    wgq = p.sb("wgq", [128, 8 * FFI], BF16)
    wuq = p.sb("wuq", [128, 8 * FFI], BF16)
    wdq = p.sb("wdq", [128, 22 * 1024], BF16)

    PTB = p.ps("PTB", [128, 1024], BF16)
    PG0 = p.ps("PG0", [128, 512], F32)
    PU0 = p.ps("PU0", [128, 512], F32)
    PF1 = p.ps("PF1", [128, 512], F32)
    PF2 = p.ps("PF2", [128, 512], F32)
    PTH = p.ps("PTH", [128, 1024], BF16)
    PG1 = p.ps("PG1B", [128, 512], F32)
    PU1 = p.ps("PU1B", [128, 512], F32)

    wstg = p.sb("wstgB", [128, 1024], F32)
    qabs = p.sb("qabsB", [128, 1024], F32)
    qsgn = p.sb("qsgnB", [128, 1024], F32)
    wq_b = p.sb("wq_bB", [128, 1024], BF16)
    Q = {k: p.sb("Qb_" + k, [128, n], F32) for k, n in
         [("bmax", 64), ("scl", 64),
          ("rscl", 64), ("a", 1024), ("acc2", 1024), ("tmp2", 1024), ("t22", 1024),
          ("sscl", 1024)]}
    Q["m8"] = p.sb("Qb_m8", [128, 1024], mybir.dt.uint8)
    eg1 = p.sb("eg1B", [128, 512], F32)
    eg2 = p.sb("eg2B", [128, 512], F32)
    sgc_v = [p.sb("sgcB0", [128, 512], F32), p.sb("sgcB1", [128, 512], F32)]

    ident_b = env["ident_b"]
    c_one = env["c_one"]
    c_eps = env["c_eps"]

    jobs = []
    for c in range(22):
        jobs.append((io["w_gate"], c, 0, 1024, wgq,
                     lambda k, c=c: 2816 * k + 128 * c))
    for c in range(22):
        jobs.append((io["w_up"], c, 0, 1024, wuq,
                     lambda k, c=c: 2816 * k + 128 * c))
    for r in range(8):
        for (p0, plen) in [(0, 1024), (1024, 1024), (2048, 768)]:
            jobs.append((io["w_down"], r, p0, plen, wdq,
                         lambda k, r=r, p0=p0: (p0 // 128 + k) * 1024 + 128 * r))
    wi = 0
    for (srct, rowc, col0, ncol, dstt, colf) in jobs:
        p.op("sync", "dma_start", out=wstg[:, 0:ncol],
             in_=srct[128 * rowc:128 * (rowc + 1), col0:col0 + ncol],
             wait=[("bwq", wi)], inc="bwl", dma=True)
        p.op("scalar", "activation", Q["t22"][:, 0:ncol], wstg[:, 0:ncol],
             AF.Abs, wait=[("bwl", 16 * (wi + 1)), ("bwst", wi)])
        p.op("scalar", "activation", qsgn[:, 0:ncol], wstg[:, 0:ncol], AF.Sign,
             inc="bwsg")
        emit_quant_stair(p, env, "vector", "gpsimd", "bw", wstg, qsgn, wq_b,
                         ncol, Q, pool_extra_wait=[("bwt", p.cnt.get("bwt", 0)),
                                                   ("bwsg", wi + 1)],
                         src_wait=[("bwl", 16 * (wi + 1))],
                         abs_wait=[("bwsg", wi + 1)])
        nk = ncol // 128
        for k in range(nk):
            p.op("tensor", "transpose", PTH[:, 128 * k:128 * (k + 1)],
                 wq_b[:, 128 * k:128 * (k + 1)], ident_b[:, :],
                 wait=([("bwpl", p.cnt["bwpl"]), ("bwe", p.cnt.get("bwe", 0))]
                       if k == 0 else ()),
                 inc=("bwt" if k == nk - 1 else None))
        for k in range(nk):
            p.op("vector", "tensor_copy", out=dstt[:, colf(k):colf(k) + 128],
                 in_=PTH[:, 128 * k:128 * (k + 1)],
                 wait=([("bwt", p.cnt["bwt"])] if k == 0 else ()),
                 inc=("bwe" if k == nk - 1 else None))
        p.op("vector", "nop", wait=[("bwsg", wi + 1)], inc="bwq")
        wi += 1
    NWB = wi

    xqt = [p.sb(f"xqtB{i}", [128, 1024], BF16) for i in range(2)]
    r1t = [p.sb(f"r1tB{i}", [128, 1024], F32) for i in range(2)]
    hg = p.sb("hgB", [128, 512], F32)
    hf = p.sb("hfB", [128, 512], F32)
    hsgn_v = [qsgn[:, 0:512], qsgn[:, 512:1024]]
    hq_b = [wq_b[:, 0:512], wq_b[:, 512:1024]]
    hqT = [p.sb("hqT0B", [128, 512], BF16), p.sb("hqT1B", [128, 512], BF16)]
    x1_sb = [wstg, p.sb("x1B1", [128, 1024], F32)]
    xn1_sb = [p.sb(f"xn1B{i}", [128, 1024], BF16) for i in range(2)]
    xsqB = qabs
    ssqB = p.sb("ssqB", [128, 1], F32)
    sumxB = p.sb("sumxB", [128, 1], F32)
    msqB = p.sb("msqB", [128, 1], F32)
    tv1B = p.sb("tv1B", [128, 1], F32)
    varB = p.sb("varB", [128, 1], F32)
    lnvB = p.sb("lnvB", [128, 1], F32)
    rstdB = p.sb("rstdB", [128, 1], F32)
    nmB = p.sb("nmB", [128, 1], F32)

    first_waits = [("bwq", NWB)]
    fw = {e: list(first_waits) for e in ENGINES}

    def W(eng, extra=()):
        ws = fw[eng] + list(extra)
        fw[eng] = []
        return ws

    xq_d, r1_d, x1_d, xn1_d = io["xq"], io["r1"], io["x1"], io["xn1"]
    nch = len(ICHUNKS)

    for T in range(NT):
        t0 = T * 128
        for j in range(8):
            p.op("sync", "dma_start_transpose",
                 xqt[T % 2][:, 128 * j:128 * (j + 1)],
                 xq_d[t0:t0 + 128, 128 * j:128 * (j + 1)],
                 wait=(W("sync", [("bpg", nch * (T - 1))]) if j == 0 else ()),
                 inc="blq", dma=True)
        p.op("sync", "dma_start", out=r1t[T % 2][:, :], in_=r1_d[t0:t0 + 128, :],
             wait=[("bx1", T - 1)], inc="bld", dma=True)
        xqT = xqt[T % 2]
        dnb = [44 * T]

        def emit_gate_up(ci):
            CI = nch * T + ci
            q = CI % 2
            PG = PG0 if q == 0 else PG1
            PU = PU0 if q == 0 else PU1
            i0, ilen = ICHUNKS[ci]
            for k in range(8):
                p.op("tensor", "matmul", PG[:, 0:ilen],
                     xqT[:, 128 * k:128 * (k + 1)],
                     wgq[:, 2816 * k + i0:2816 * k + i0 + ilen],
                     start=(k == 0), stop=(k == 7),
                     wait=(([("blq", 128 * (T + 1))] if ci == 0 else [])
                           + [("bhg", CI - 1)]) if k == 0 else (),
                     inc=("bpg" if k == 7 else None))
            for k in range(8):
                p.op("tensor", "matmul", PU[:, 0:ilen],
                     xqT[:, 128 * k:128 * (k + 1)],
                     wuq[:, 2816 * k + i0:2816 * k + i0 + ilen],
                     start=(k == 0), stop=(k == 7),
                     wait=([("bhf", CI - 1)] if k == 0 else ()),
                     inc=("bpu" if k == 7 else None))

        emit_gate_up(0)
        for ci, (i0, ilen) in enumerate(ICHUNKS):
            CI = nch * T + ci
            q = CI % 2
            PG = PG0 if q == 0 else PG1
            PU = PU0 if q == 0 else PU1
            TB = PTH if q == 0 else PTB
            if ci + 1 < nch:
                emit_gate_up(ci + 1)
            p.op("scalar", "activation", eg1[:, 0:ilen], PG[:, 0:ilen],
                 AF.Exp, scale=-1.0,
                 wait=W("scalar", [("bpg", CI + 1), ("bhg", CI - 1)]))
            p.op("scalar", "activation", eg2[:, 0:ilen], eg1[:, 0:ilen],
                 AF.Ln, bias=c_one[:, :])
            p.op("scalar", "activation", sgc_v[q][:, 0:ilen], eg2[:, 0:ilen],
                 AF.Exp, scale=-1.0, inc="bsg")
            p.op("vector", "scalar_tensor_tensor", hg[:, 0:ilen], PG[:, 0:ilen],
                 1.0, sgc_v[q][:, 0:ilen], ALU.mult, ALU.mult,
                 wait=[("bsg", CI + 1)], inc="bhg")
            p.op("vector", "tensor_tensor", hf[:, 0:ilen], PU[:, 0:ilen],
                 hg[:, 0:ilen], ALU.mult, wait=[("bpu", CI + 1), ("bhs", CI)],
                 inc="bhf")
            p.op("scalar", "activation", Q["t22"][:, 0:ilen], hf[:, 0:ilen],
                 AF.Abs, wait=[("bhf", CI + 1), ("bhst", CI)])
            p.op("scalar", "activation", hsgn_v[q][:, 0:ilen], hf[:, 0:ilen],
                 AF.Sign, inc="bhs")
            Qq = dict(Q)
            Qq["sscl"] = Q["sscl"][:, 512 * q:512 * (q + 1)]
            emit_quant_stair(p, env, "vector", "gpsimd", "bh", hf,
                             hsgn_v[q], hq_b[q], ilen, Qq,
                             pool_extra_wait=[("bhT", CI - 1), ("bhs", CI + 1)],
                             src_wait=[("bhf", CI + 1)],
                             qlow_st_wait=CI, abs_wait=[("bhs", CI + 1)])
            nk = ilen // 128
            for k in range(nk):
                p.op("tensor", "transpose", TB[:, 128 * k:128 * (k + 1)],
                     hq_b[q][:, 128 * k:128 * (k + 1)], ident_b[:, :],
                     wait=([("bhpl", CI + 1), ("bhTe", CI - 1)]
                           if k == 0 else ()),
                     inc=("bhT" if k == nk - 1 else None))
            p.op("vector", "tensor_copy", out=hqT[q][:, 0:ilen],
                 in_=TB[:, 0:ilen],
                 wait=[("bhT", CI + 1),
                       ("bdn", dnb[max(0, ci - 1)])], inc="bhTe")
            for k in range(nk):
                ic = (i0 // 128) + k
                for half in range(2):
                    p.op("tensor", "matmul",
                         (PF1 if half == 0 else PF2)[:, 0:512],
                         hqT[q][:, 128 * k:128 * (k + 1)],
                         wdq[:, 1024 * ic + 512 * half:1024 * ic + 512 * (half + 1)],
                         start=(ci == 0 and k == 0), stop=(ci == 5 and k == nk - 1),
                         wait=([("bhTe", CI + 1), ("bx1", T)]
                               if (ci == 0 and k == 0 and half == 0) else
                               ([("bhTe", CI + 1)]
                                if (k == 0 and half == 0) else ())),
                         inc=("bdn", 1))
            dnb.append(dnb[-1] + 2 * nk)
        p.op("vector", "tensor_tensor", x1_sb[T % 2][:, 0:512], PF1[:, 0:512],
             r1t[T % 2][:, 0:512], ALU.add,
             wait=[("bdn", 44 * (T + 1)), ("bld", 16 * (T + 1)),
                   ("bso", 32 * (T - 1))])
        p.op("vector", "tensor_tensor", x1_sb[T % 2][:, 512:1024], PF2[:, 0:512],
             r1t[T % 2][:, 512:1024], ALU.add, inc="bx1")
        p.op("scalar", "activation", xsqB[:, :], x1_sb[T % 2][:, :], AF.Square,
             accum_out=ssqB[:, :], wait=[("bx1", T + 1)], inc="blnq")
        p.op("vector", "tensor_reduce", sumxB[:, :], x1_sb[T % 2][:, :], AX.X,
             ALU.add, wait=[("bx1", T + 1)])
        p.fence("vector")
        p.op("vector", "tensor_tensor", msqB[:, :], sumxB[:, :], sumxB[:, :],
             ALU.mult)
        p.op("vector", "tensor_scalar", tv1B[:, :], ssqB[:, :], 1.0 / D, None,
             ALU.mult, wait=[("blnq", T + 1)])
        p.fence("vector")
        p.op("vector", "scalar_tensor_tensor", varB[:, :], msqB[:, :],
             -1.0 / (D * D), tv1B[:, :], ALU.mult, ALU.add, inc="blns")
        p.op("scalar", "activation", lnvB[:, :], varB[:, :], AF.Ln,
             bias=c_eps[:, :], wait=[("blns", T + 1)])
        p.fence("scalar")
        p.op("scalar", "activation", rstdB[:, :], lnvB[:, :], AF.Exp, scale=-0.5,
             inc="brsd")
        p.op("vector", "scalar_tensor_tensor", nmB[:, :], sumxB[:, :], -1.0 / D,
             rstdB[:, :], ALU.mult, ALU.mult, wait=[("brsd", T + 1)], inc="bnmr")
        p.op("scalar", "activation", xn1_sb[T % 2][:, :], x1_sb[T % 2][:, :],
             AF.Identity, bias=nmB[:, :], scale=rstdB[:, :],
             wait=[("bnmr", T + 1), ("bso", 32 * (T - 1))], inc="bxn1")
        p.op("gpsimd", "dma_start", out=x1_d[t0:t0 + 128, :],
             in_=x1_sb[T % 2][:, :], wait=W("gpsimd", [("bxn1", T + 1)]),
             inc="bso", dma=True)
        p.op("gpsimd", "dma_start", out=xn1_d[t0:t0 + 128, :],
             in_=xn1_sb[T % 2][:, :], inc="bso", dma=True)
        if env.get("debug"):
            dbg(p, env, "b_x1", x1_sb[T % 2][:, :], T, [("bx1", T + 1)])
            dbg(p, env, "b_xn1", xn1_sb[T % 2][:, :], T, [("bxn1", T + 1)])

    p.op("gpsimd", "nop", wait=[("bso", 32 * NT)])



def emit_phase_ce(p, env, io):
    WgT_b = p.sb("WgT_b", [128, 8 * 1024], BF16)
    WvT_b = p.sb("WvT_b", [128, 8 * 1024], BF16)
    Wb_b = p.sb("Wb_b", [128, 8 * 1024], BF16)
    Wd1_b = p.sb("Wd1_b", [128, 8 * 2048], BF16)
    Wd2_b = p.sb("Wd2_b", [128, 16 * 1024], BF16)
    bg_r = p.sb("bg_r", [1, 1024], BF16)
    bv_r = p.sb("bv_r", [1, 1024], BF16)
    bb_r = p.sb("bb_r", [1, 1024], BF16)
    bd1_r = p.sb("bd1_r", [1, 2048], BF16)
    bd2_r = p.sb("bd2_r", [1, 8 * 1024], BF16)

    PT = p.ps("PTC", [128, 1024], BF16)
    PG1 = p.ps("PG1C", [128, 512], F32)
    PG2 = p.ps("PG2C", [128, 512], F32)
    PW1 = p.ps("PW1C", [128, 512], F32)
    PW2 = p.ps("PW2C", [128, 512], F32)
    PH = p.ps("PHC", [128, 512], F32)
    PS6 = p.ps("PS6C", [128, 512], F32)

    ident_b = env["ident_b"]
    UT_f = env["UT_f"]
    LT_f = env["LT_f"]
    onesc_f = env["onesc_f"]
    ones1_b = env["ones1_b"]

    wstg = p.sb("wstgC", [128, 1024], F32)
    wcst = p.sb("wcstC", [128, 1024], BF16)
    wi = env.get("cw_i", 0)

    def prep_t(src, dst):
        nonlocal wi
        for c in range(8):
            p.op("sync", "dma_start", out=wstg[:, :],
                 in_=src[128 * c:128 * (c + 1), :],
                 wait=[("cwq", wi)], inc="cwl", dma=True)
            p.op("vector", "tensor_copy", out=wcst[:, :], in_=wstg[:, :],
                 wait=[("cwl", 16 * (wi + 1))], inc="cwc")
            for k in range(8):
                p.op("tensor", "transpose", PT[:, 128 * k:128 * (k + 1)],
                     wcst[:, 128 * k:128 * (k + 1)], ident_b[:, :],
                     wait=([("cwc", p.cnt["cwc"]), ("cwe", p.cnt.get("cwe", 0))]
                           if k == 0 else ()),
                     inc=("cwt" if k == 7 else None))
            for k in range(8):
                p.op("vector", "tensor_copy",
                     out=dst[:, 1024 * k + 128 * c:1024 * k + 128 * (c + 1)],
                     in_=PT[:, 128 * k:128 * (k + 1)],
                     wait=([("cwt", p.cnt["cwt"])] if k == 0 else ()),
                     inc=("cwe" if k == 7 else None))
            p.op("vector", "nop", inc="cwq")
            wi += 1

    def prep_n(src, rows, cols, dst, colf):
        nonlocal wi
        nchunk = rows // 128
        for c in range(nchunk):
            p.op("sync", "dma_start", out=wstg[0:128, 0:cols],
                 in_=src[128 * c:128 * (c + 1), :],
                 wait=[("cwq", wi)], inc="cwl", dma=True)
            p.op("vector", "tensor_copy", out=dst[:, colf(c):colf(c) + cols],
                 in_=wstg[0:128, 0:cols],
                 wait=[("cwl", 16 * (wi + 1))], inc="cwc")
            p.op("vector", "nop", inc="cwq")
            wi += 1

    def prep_row(src_1d, n, dst):
        nonlocal wi
        src2 = src_1d.rearrange("(o d) -> o d", o=1)
        for c0 in range(0, n, 1024):
            w = min(1024, n - c0)
            p.op("sync", "dma_start", out=wstg[0:1, 0:w],
                 in_=src2[0:1, c0:c0 + w],
                 wait=[("cwq", wi)], inc="cwl", dma=True)
            p.op("vector", "tensor_copy", out=dst[0:1, c0:c0 + w],
                 in_=wstg[0:1, 0:w],
                 wait=[("cwl", 16 * (wi + 1))], inc="cwc")
            p.op("vector", "nop", inc="cwq")
            wi += 1

    prep_t(io["L1Wg"], WgT_b)
    prep_t(io["L1Wv"], WvT_b)
    prep_n(io["W_base"], 1024, 1024, Wb_b, lambda c: 1024 * c)
    for e in range(8):
        prep_n(io["Wd1"][e], 1024, 256, Wd1_b, lambda c, e=e: 2048 * c + 256 * e)
    for e in range(8):
        prep_n(io["Wd2"][e], 256, 1024, Wd2_b, lambda c, e=e: (2 * e + c) * 1024)
    prep_row(io["L1bg"], 1024, bg_r)
    prep_row(io["L1bv"], 1024, bv_r)
    prep_row(io["b_base"], 1024, bb_r)
    prep_row(io["bd1"].rearrange("e h -> (e h)"), 2048, bd1_r)
    prep_row(io["bd2"].rearrange("e d -> (e d)"), 8192, bd2_r)
    NWC = wi
    env["cw_i"] = wi

    xn1t = [p.sb(f"xn1tC{i}", [128, 1024], BF16) for i in range(2)]
    xnT = p.sb("xnTC", [128, 1024], BF16)
    gsq = p.sb("gsqC", [128, 1024], F32)
    n2 = p.sb("n2C", [128, 16], F32)
    lnn = p.sb("lnnC", [128, 16], F32)
    rn = p.sb("rnC", [128, 16], F32)
    gh_b = [p.sb(f"gh_bC{i}", [128, 1024], BF16) for i in range(2)]
    ghT = p.sb("ghTC", [128, 1024], BF16)
    v_b = p.sb("v_bC", [128, 1024], BF16)
    base_sb = p.sb("baseC", [128, 1024], F32)
    wmul = p.sb("wmulC", [128, 1024], F32)
    rs16 = p.sb("rs16C", [128, 16], F32)
    dnm = p.sb("dnmC", [128, 16], F32)
    recd = p.sb("recdC", [128, 16], F32)
    denom_all = p.sb("denom_allC", [128, 512], F32)
    he_b = p.sb("he_bC", [128, 512], BF16)
    heT = p.sb("heTC", [128, 512], BF16)
    dsub = p.sb("dsubC", [128, 1024], F32)
    dsq = p.sb("dsqC", [128, 1024], F32)
    lcol8 = p.sb("lcol8C", [128, 8], F32)
    loss_acc = p.sb("loss_accC", [128, 8], F32)
    lrow = p.sb("lrowC", [1, 8], F32)
    lmin = p.sb("lminC", [1, 1], F32)
    d8 = p.sb("d8C", [1, 8], F32)
    s8 = p.sb("s8C", [1, 8], F32)
    ohrow = p.sb("ohrowC", [1, 8], F32)
    prods = [p.sb(f"prodC{i}", [1, 1], F32) for i in range(2)]
    srow = p.sb("srowC", [1, 16], F32)

    p.op("vector", "memset", gh_b[0][:, :], 0.0)
    p.op("vector", "memset", gh_b[1][:, :], 0.0)
    p.op("vector", "memset", loss_acc[:, :], 0.0, inc="cms0")

    first_waits = [("cwq", NWC), ("cms0", 1)]
    fw = {e: list(first_waits) for e in ENGINES}

    def W(eng, extra=()):
        ws = fw[eng] + list(extra)
        fw[eng] = []
        return ws

    xn1_d = io["xn1"]
    for T in range(NT):
        t0 = T * 128
        p.op("sync", "dma_start", out=xn1t[T % 2][:, :], in_=xn1_d[t0:t0 + 128, :],
             wait=W("sync", [("clsq", 8 * (T - 1))]), inc="cld2", dma=True)
        for j in range(8):
            p.op("tensor", "transpose", PT[:, 128 * j:128 * (j + 1)],
                 xn1t[T % 2][:, 128 * j:128 * (j + 1)], ident_b[:, :],
                 wait=(W("tensor", [("cld2", 16 * (T + 1)), ("cheTe", 4 * T)])
                       if j == 0 else ()),
                 inc=("cxT" if j == 7 else None))
        p.op("vector", "tensor_copy", out=xnT[:, :], in_=PT[:, :],
             wait=W("vector", [("cxT", T + 1), ("cph", 4 * T)]), inc="cxe")
        p.op("gpsimd", "dma_start", out=io["xn1T_d"][t0:t0 + 128, :], in_=xnT[:, :],
             wait=W("gpsimd", [("cxe", T + 1)]), inc="cso", dma=True)
        for half, bank in ((0, PG1), (1, PG2)):
            for k in range(8):
                p.op("tensor", "matmul", bank[:, 0:512],
                     xnT[:, 128 * k:128 * (k + 1)],
                     WgT_b[:, 1024 * k + 512 * half:1024 * k + 512 * (half + 1)],
                     start=(k == 0), stop=False,
                     wait=([("cxe", T + 1), ("cdsu", 16 * T)]
                           if (half == 0 and k == 0) else ()))
            p.op("tensor", "matmul", bank[:, 0:512], ones1_b[:, :],
                 bg_r[0:1, 512 * half:512 * (half + 1)], start=False, stop=True,
                 inc=("cpg" if half == 1 else None))
        p.op("scalar", "activation", gsq[:, 0:512], PG1[:, 0:512], AF.Square,
             wait=W("scalar", [("cpg", T + 1), ("cn2", T)]))
        p.op("scalar", "activation", gsq[:, 512:1024], PG2[:, 0:512], AF.Square,
             inc="cgsq")
        p.op("vector", "tensor_reduce", n2[:, :],
             gsq[:, :].rearrange("p (h e) -> p h e", h=16), AX.X, ALU.add,
             wait=[("cgsq", T + 1)], inc="cn2")
        p.op("scalar", "activation", lnn[:, :], n2[:, :], AF.Ln,
             wait=[("cn2", T + 1), ("cghb", T)])
        p.fence("scalar")
        p.op("scalar", "activation", rn[:, :], lnn[:, :], AF.Exp, scale=-0.5,
             inc="crn")
        p.op("vector", "tensor_tensor",
             gh_b[T % 2][:, 0:512].rearrange("p (h e) -> p h e", e=64),
             PG1[:, 0:512].rearrange("p (h e) -> p h e", e=64),
             rn[:, 0:8].rearrange("p (h e) -> p h e", e=1).broadcast_to((128, 8, 64)),
             ALU.mult, wait=[("crn", T + 1), ("cpw", T), ("cghTt", T)])
        p.op("vector", "tensor_tensor",
             gh_b[T % 2][:, 512:1024].rearrange("p (h e) -> p h e", e=64),
             PG2[:, 0:512].rearrange("p (h e) -> p h e", e=64),
             rn[:, 8:16].rearrange("p (h e) -> p h e", e=1).broadcast_to((128, 8, 64)),
             ALU.mult, inc="cghb")
        for j in range(8):
            p.op("tensor", "transpose", PT[:, 128 * j:128 * (j + 1)],
                 gh_b[T % 2][:, 128 * j:128 * (j + 1)], ident_b[:, :],
                 wait=([("cghb", T + 1)] if j == 0 else ()),
                 inc=("cghTt" if j == 7 else None))
        p.op("vector", "tensor_copy", out=ghT[:, :], in_=PT[:, :],
             wait=[("cghTt", T + 1), ("cghs", T)], inc="cghT")
        p.op("gpsimd", "dma_start", out=io["gT_d"][t0:t0 + 128, :], in_=ghT[:, :],
             wait=[("cghT", T + 1)], inc="cghs", dma=True)
        for half, bank in ((0, PW1), (1, PW2)):
            p.op("tensor", "matmul", bank[:, 0:512], env["LT_b"][:, :],
                 gh_b[(T - 1) % 2][:, 512 * half:512 * (half + 1)],
                 start=True, stop=False,
                 wait=([("cghb", T + 1), ("crsm", T)] if half == 0 else ()))
            p.op("tensor", "matmul", bank[:, 0:512], env["UT_b"][:, :],
                 gh_b[T % 2][:, 512 * half:512 * (half + 1)],
                 start=False, stop=True, inc=("cpw" if half == 1 else None))
        p.op("vector", "tensor_tensor", wmul[:, 0:512], PW1[:, 0:512],
             gh_b[T % 2][:, 0:512], ALU.mult, wait=[("cpw", T + 1)])
        p.op("vector", "tensor_tensor", wmul[:, 512:1024], PW2[:, 0:512],
             gh_b[T % 2][:, 512:1024], ALU.mult, inc="crsm")
        p.op("vector", "tensor_reduce", rs16[:, :],
             wmul[:, :].rearrange("p (h e) -> p h e", h=16), AX.X, ALU.add)
        p.fence("vector")
        p.op("vector", "tensor_scalar", dnm[:, :], rs16[:, :], 1.0 / 8192.0, 1.0,
             ALU.mult, ALU.add)
        p.fence("vector")
        p.op("vector", "tensor_copy", out=denom_all[:, 16 * T:16 * (T + 1)],
             in_=dnm[:, :])
        p.op("vector", "reciprocal", recd[:, :], dnm[:, :], inc="crecd")
        p.op("tensor", "matmul", PS6[0:1, 0:16], onesc_f[:, :], recd[:, :],
             start=(T == 0), stop=(T == NT - 1),
             wait=[("crecd", T + 1)], inc="csinv")
        for half, bank in ((0, PG1), (1, PG2)):
            for k in range(8):
                p.op("tensor", "matmul", bank[:, 0:512],
                     xnT[:, 128 * k:128 * (k + 1)],
                     WvT_b[:, 1024 * k + 512 * half:1024 * k + 512 * (half + 1)],
                     start=(k == 0), stop=False,
                     wait=([("cgsq", T + 1)] if (half == 0 and k == 0) else ()))
            p.op("tensor", "matmul", bank[:, 0:512], ones1_b[:, :],
                 bv_r[0:1, 512 * half:512 * (half + 1)], start=False, stop=True,
                 inc=("cpv" if half == 1 else None))
        p.op("vector", "tensor_copy", out=v_b[:, 0:512], in_=PG1[:, 0:512],
             wait=[("cpv", T + 1), ("cvs", T)])
        p.op("vector", "tensor_copy", out=v_b[:, 512:1024], in_=PG2[:, 0:512],
             inc="cvev")
        p.op("gpsimd", "dma_start", out=io["v_d"][t0:t0 + 128, :], in_=v_b[:, :],
             wait=[("cvev", T + 1)], inc="cvs", dma=True)
        for half, bank in ((0, PG1), (1, PG2)):
            for k in range(8):
                p.op("tensor", "matmul", bank[:, 0:512],
                     xnT[:, 128 * k:128 * (k + 1)],
                     Wb_b[:, 1024 * k + 512 * half:1024 * k + 512 * (half + 1)],
                     start=(k == 0), stop=False,
                     wait=([("cvev", T + 1)] if (half == 0 and k == 0) else ()))
            p.op("tensor", "matmul", bank[:, 0:512], ones1_b[:, :],
                 bb_r[0:1, 512 * half:512 * (half + 1)], start=False, stop=True,
                 inc=("cpb" if half == 1 else None))
        p.op("vector", "tensor_copy", out=base_sb[:, 0:512], in_=PG1[:, 0:512],
             wait=[("cpb", T + 1), ("cbs", T)])
        p.op("vector", "tensor_copy", out=base_sb[:, 512:1024], in_=PG2[:, 0:512],
             inc="cbase")
        p.op("gpsimd", "dma_start", out=io["base_d"][t0:t0 + 128, :],
             in_=base_sb[:, :], wait=[("cbase", T + 1)], inc="cbs", dma=True)
        for j in range(4):
            PJ = 4 * T + j
            for k in range(8):
                p.op("tensor", "matmul", PH[:, 0:512],
                     xnT[:, 128 * k:128 * (k + 1)],
                     Wd1_b[:, 2048 * k + 512 * j:2048 * k + 512 * (j + 1)],
                     start=(k == 0), stop=False,
                     wait=([("cheb", PJ)] if k == 0 else ()))
            p.op("tensor", "matmul", PH[:, 0:512], ones1_b[:, :],
                 bd1_r[0:1, 512 * j:512 * (j + 1)], start=False, stop=True,
                 inc=("cph", 1))
            p.op("scalar", "activation", he_b[:, :], PH[:, 0:512], AF.Relu,
                 wait=[("cph", PJ + 1), ("cheT", PJ)], inc=("cheb", 1))
            for c in range(4):
                p.op("tensor", "transpose", PT[:, 128 * c:128 * (c + 1)],
                     he_b[:, 128 * c:128 * (c + 1)], ident_b[:, :],
                     wait=([("cheb", PJ + 1), ("cheTe", PJ), ("cghT", T + 1)]
                           if c == 0 else ()),
                     inc=("cheT", 1) if c == 3 else None)
            p.op("vector", "tensor_copy", out=heT[:, :], in_=PT[:, 0:512],
                 wait=[("cheT", PJ + 1), ("crec", 2 * PJ)],
                 inc=("cheTe", 1))
            for i in range(2):
                e = 2 * j + i
                for half, bank in ((0, PG1), (1, PG2)):
                    for c in range(2):
                        p.op("tensor", "matmul", bank[:, 0:512],
                             heT[:, 256 * i + 128 * c:256 * i + 128 * (c + 1)],
                             Wd2_b[:, (2 * e + c) * 1024 + 512 * half:
                                   (2 * e + c) * 1024 + 512 * (half + 1)],
                             start=(c == 0), stop=False,
                             wait=([("cheTe", PJ + 1), ("cdsu", 2 * (8 * T + e))]
                                   + ([("cbase", T + 1)] if e == 0 else [])
                                   if (half == 0 and c == 0) else ()))
                    p.op("tensor", "matmul", bank[:, 0:512], ones1_b[:, :],
                         bd2_r[0:1, 1024 * e + 512 * half:
                               1024 * e + 512 * (half + 1)],
                         start=False, stop=True,
                         inc=("crec", 1) if half == 1 else None)
                p.op("vector", "tensor_tensor", dsub[:, 0:512], PG1[:, 0:512],
                     xn1t[T % 2][:, 0:512], ALU.subtract,
                     wait=[("crec", 8 * T + e + 1), ("clsq", 8 * T + e)],
                     inc=("cdsu", 1))
                p.op("vector", "tensor_tensor", dsub[:, 512:1024], PG2[:, 0:512],
                     xn1t[T % 2][:, 512:1024], ALU.subtract, inc=("cdsu", 1))
                p.op("scalar", "activation", dsq[:, :], dsub[:, :], AF.Square,
                     accum_out=lcol8[:, e:e + 1],
                     wait=[("cdsu", 2 * (8 * T + e + 1))], inc=("clsq", 1))
        p.op("vector", "tensor_tensor", loss_acc[:, :], loss_acc[:, :],
             lcol8[:, :], ALU.add, wait=[("clsq", 8 * (T + 1))], inc="clac")

    p.op("tensor", "matmul", PS6[0:1, 16:24], onesc_f[:, :], loss_acc[:, :],
         start=True, stop=True,
         wait=[("clac", NT), ("csinv", NT)], inc="cfin")
    p.op("vector", "tensor_copy", out=srow[:, :],
         in_=PS6[0:1, 0:16], wait=[("cfin", 1)])
    p.op("vector", "tensor_copy", out=lrow[:, :], in_=PS6[0:1, 16:24])
    p.fence("vector")
    p.op("vector", "tensor_reduce", lmin[:, :], lrow[:, :], AX.X, ALU.min)
    p.fence("vector")
    p.op("vector", "tensor_scalar", d8[:, :], lrow[:, :], lmin[:, :], None,
         ALU.subtract, inc="cd8")
    p.op("scalar", "activation", s8[:, :], d8[:, :], AF.Sign,
         wait=[("cd8", 1)], inc="cs8")
    p.op("vector", "memset", prods[0][:, :], 1.0, wait=[("cs8", 1)])
    p.fence("vector")
    for i in range(8):
        a, b = prods[i % 2], prods[(i + 1) % 2]
        p.op("vector", "tensor_tensor", b[:, :], a[:, :], s8[:, i:i + 1], ALU.mult)
        p.fence("vector")
        p.op("vector", "tensor_tensor", ohrow[:, i:i + 1], a[:, :], b[:, :],
             ALU.subtract)
        p.fence("vector")
    p.op("vector", "nop", inc="coh")
    p.op("gpsimd", "dma_start", out=io["sinv_d"], in_=srow[:, :],
         wait=[("coh", 1)], inc="cso", dma=True)
    p.op("gpsimd", "dma_start", out=io["oh_d"], in_=ohrow[:, :], inc="cso", dma=True)
    p.op("gpsimd", "dma_start",
         out=io["denom_d"].rearrange("(a p) h -> p a h", p=128),
         in_=denom_all[:, :].rearrange("p (a h) -> p a h", h=16),
         inc="cso", dma=True)
    p.op("gpsimd", "nop",
         wait=[("cso", p.cnt["cso"]), ("cghs", 16 * NT), ("cvs", 16 * NT),
               ("cbs", 16 * NT)])



def emit_phase_d(p, env, io):
    WoT_b = p.sb("WoT_bD", [128, 8 * 1024], BF16)
    bo_r = p.sb("bo_rD", [1, 1024], BF16)

    PTD = p.ps("PTD", [128, 1024], BF16)
    PA1 = p.ps("PA1D", [128, 512], F32)
    PA2 = p.ps("PA2D", [128, 512], F32)
    PSC = p.ps("PSCD", [128, 1024], F32)
    PO = p.ps("POD", [128, 1024], F32)
    PR = p.ps("PRD", [128, 512], F32)

    ident_b = env["ident_b"]
    ones1_b = env["ones1_b"]
    ones1_f = env["ones1_f"]
    zeroreg = p.nc.gpsimd.alloc_register("d_zeroreg")
    p.op("gpsimd", "reg_mov", zeroreg, 0.0)

    wstg = p.sb("wstgD", [128, 1024], F32)
    wcst = p.sb("wcstD", [128, 1024], BF16)
    wi = 0
    for c in range(8):
        p.op("sync", "dma_start", out=wstg[:, :],
             in_=io["L1Wo"][128 * c:128 * (c + 1), :],
             wait=[("dwq", wi)], inc="dwl", dma=True)
        p.op("vector", "tensor_copy", out=wcst[:, :], in_=wstg[:, :],
             wait=[("dwl", 16 * (wi + 1))], inc="dwc")
        for k in range(8):
            p.op("tensor", "transpose", PTD[:, 128 * k:128 * (k + 1)],
                 wcst[:, 128 * k:128 * (k + 1)], ident_b[:, :],
                 wait=([("dwc", p.cnt["dwc"]), ("dwe", p.cnt.get("dwe", 0))]
                       if k == 0 else ()),
                 inc=("dwt" if k == 7 else None))
        for k in range(8):
            p.op("vector", "tensor_copy",
                 out=WoT_b[:, 1024 * k + 128 * c:1024 * k + 128 * (c + 1)],
                 in_=PTD[:, 128 * k:128 * (k + 1)],
                 wait=([("dwt", p.cnt["dwt"])] if k == 0 else ()),
                 inc=("dwe" if k == 7 else None))
        p.op("vector", "nop", inc="dwq")
        wi += 1
    p.op("sync", "dma_start", out=wstg[0:1, 0:1024],
         in_=io["L1bo"].rearrange("(o d) -> o d", o=1),
         wait=[("dwq", wi)], inc="dwl", dma=True)
    p.op("vector", "tensor_copy", out=bo_r[0:1, :], in_=wstg[0:1, 0:1024],
         wait=[("dwl", 16 * (wi + 1))], inc="dwc")
    p.op("vector", "nop", inc="dwq")
    wi += 1
    NWD = wi

    srow = p.sb("srowD", [1, 16], F32)
    rsrow = p.sb("rsrowD", [1, 16], F32)
    rsrep = p.sb("rsrepD", [128, 16], F32)
    p.op("sync", "dma_start", out=srow[:, :], in_=io["sinv_d"], inc="dsl", dma=True)
    p.op("vector", "reciprocal", rsrow[:, :], srow[:, :],
         wait=[("dsl", 16)], inc="drs")
    p.op("tensor", "matmul", PR[0:128, 0:16], ones1_f[:, :], rsrow[:, :],
         start=True, stop=True, wait=[("drs", 1)], inc="drm")
    p.op("vector", "tensor_copy", out=rsrep[:, :], in_=PR[0:128, 0:16],
         wait=[("drm", 1)], inc="drr")

    ghTt = [p.sb(f"ghTtD{i}", [128, 1024], BF16) for i in range(3)]
    vt = [p.sb(f"vtD{i}", [128, 1024], BF16) for i in range(3)]
    x1t = [p.sb(f"x1tD{i}", [128, 1024], F32) for i in range(2)]
    bst = [p.sb(f"bstD{i}", [128, 1024], F32) for i in range(2)]
    dnt = [p.sb(f"dntD{i}", [128, 16], F32) for i in range(2)]
    recd = p.sb("recdD", [128, 16], F32)
    hf = p.sb("hfD", [128, 16], F32)
    u2f = p.sb("u2fD", [128, 1024], F32)
    tp = p.sb("tpD", [128, 1024], F32)
    pp = p.sb("ppD", [128, 1024], F32)
    ppm = p.sb("ppmD", [128, 1024], BF16)
    pTg = p.sb("pTgD", [128, 1024], BF16)
    rs16 = p.sb("rs16D", [128, 16], F32)
    tmp4 = p.sb("tmp4D", [128, 4], F32)
    rnrm = p.sb("rnrmD", [128, 16], F32)
    onb = p.sb("onbD", [128, 1024], BF16)
    onT = p.sb("onTD", [128, 1024], BF16)
    op2 = [p.sb(f"op2D{i}", [128, 1024], F32) for i in range(2)]
    UT_b = env["UT_b"]
    LT_b = env["LT_b"]
    nw0 = env["nw0"]
    c_one = env["c_one"]

    p.op("vector", "memset", ghTt[2][:, :], 0.0)
    p.op("vector", "memset", vt[2][:, :], 0.0, inc="dmz")

    first_waits = [("dwq", NWD), ("drr", 1), ("dmz", 1)]
    fw = {e: list(first_waits) for e in ENGINES}

    def W(eng, extra=()):
        ws = fw[eng] + list(extra)
        fw[eng] = []
        return ws

    def emit_scores(Ts, g):
        GIs = 4 * Ts + g
        for cc in range(4):
            h = 4 * g + SLOT_HD[cc]
            j = h // 2
            pb = 64 * (h % 2)
            sw = []
            if cc == 0:
                sw = [("du2", GIs)]
                if g == 0:
                    sw = sw + [("dld", 80 * Ts + 32)]
                if Ts == 0 and g == 0:
                    sw = W("tensor") + sw
            p.op("tensor", "matmul", PSC[:, 256 * cc:256 * cc + 128],
                 ghTt[Ts % 3][pb:pb + 64, 128 * j:128 * (j + 1)],
                 ghTt[(Ts - 1) % 3][pb:pb + 64, 128 * j:128 * (j + 1)],
                 start=True, stop=True, tile_position=(pb, 0), wait=sw)
            p.op("tensor", "matmul", PSC[:, 256 * cc + 128:256 * cc + 256],
                 ghTt[Ts % 3][pb:pb + 64, 128 * j:128 * (j + 1)],
                 ghTt[Ts % 3][pb:pb + 64, 128 * j:128 * (j + 1)],
                 start=True, stop=True, tile_position=(pb, 0),
                 inc=("dsc", 1) if cc == 3 else None)

    for T in range(NT):
        t0 = T * 128
        p.op("sync", "dma_start", out=ghTt[T % 3][:, :],
             in_=io["gT_d"][t0:t0 + 128, :],
             wait=W("sync", [("dsc", 4 * (T - 1)), ("dpv", 4 * (T - 1))]),
             inc="dld", dma=True)
        p.op("sync", "dma_start", out=vt[T % 3][:, :],
             in_=io["v_d"][t0:t0 + 128, :], inc="dld", dma=True)
        p.op("sync", "dma_start", out=x1t[T % 2][:, :],
             in_=io["x1"][t0:t0 + 128, :],
             wait=[("dop", T - 1)], inc="dld", dma=True)
        p.op("sync", "dma_start", out=bst[T % 2][:, :],
             in_=io["base_d"][t0:t0 + 128, :], inc="dld", dma=True)
        p.op("sync", "dma_start", out=dnt[T % 2][:, :],
             in_=io["denom_d"][t0:t0 + 128, :],
             wait=[("dhf", T - 1)], inc="dld", dma=True)
        p.op("vector", "reciprocal", recd[:, :], dnt[T % 2][:, :],
             wait=W("vector", [("dld", 80 * (T + 1)), ("dhf", T)]))
        p.fence("vector")
        p.op("vector", "scalar_tensor_tensor", hf[:, :], recd[:, :], 0.5,
             rsrep[:, :], ALU.mult, ALU.mult, inc="dhf")
        if T == 0:
            emit_scores(0, 0)
        for g in range(4):
            GI = 4 * T + g
            if g < 3:
                emit_scores(T, g + 1)
            elif T + 1 < NT:
                emit_scores(T + 1, 0)
            for lo in range(2):
                p.op("vector", "scalar_tensor_tensor",
                     u2f[:, 512 * lo:512 * (lo + 1)
                         ].rearrange("p (g w) -> p g w", w=256),
                     PSC[:, 512 * lo:512 * (lo + 1)
                         ].rearrange("p (g w) -> p g w", w=256),
                     1.0,
                     hf[:, 4 * g:4 * g + 4].rearrange("p (k two) -> p k two",
                                                      two=2)[:, :, lo:lo + 1
                                                             ].broadcast_to(
                                                                 (128, 2, 256)),
                     ALU.add, ALU.mult,
                     wait=([("dsc", GI + 1), ("dfm", GI)]
                           + ([("dhf", T + 1)] if g == 0 else [])
                           if lo == 0 else ()),
                     inc=("du2", 1) if lo == 1 else None)
            p.op("scalar", "activation", tp[:, :], u2f[:, :], AF.Identity,
                 scale=0.5, bias=c_one[:, :],
                 wait=W("scalar", [("du2", GI + 1), ("dppr", GI)]), inc=("dtp", 1))
            p.op("vector", "tensor_tensor", pp[:, :], u2f[:, :], tp[:, :],
                 ALU.mult, wait=[("dtp", GI + 1), ("dmsk", GI)], inc=("dfm", 1))
            p.op("gpsimd", "affine_select",
                 ppm[:, :].rearrange("p (g w) -> p g w", w=256)[:, :, 0:128],
                 pp[:, :].rearrange("p (g w) -> p g w", w=256)[:, :, 0:128],
                 pattern=[[0, 4], [1, 128]], compare_op=ALU.is_ge, fill=zeroreg,
                 base=(-100000 if T == 0 else 0), channel_multiplier=-1,
                 wait=W("gpsimd", [("dfm", GI + 1), ("dpT", GI)]))
            p.op("gpsimd", "affine_select",
                 ppm[:, :].rearrange("p (g w) -> p g w", w=256)[:, :, 128:256],
                 pp[:, :].rearrange("p (g w) -> p g w", w=256)[:, :, 128:256],
                 pattern=[[0, 4], [-1, 128]], compare_op=ALU.is_ge, fill=zeroreg,
                 base=0, channel_multiplier=1, inc=("dmsk", 1))
            p.op("vector", "tensor_reduce", tmp4[:, :],
                 ppm[:, :].rearrange("p (g w) -> p g w", w=256), AX.X, ALU.add,
                 wait=[("dmsk", GI + 1)])
            p.fence("vector")
            if T == 0:
                p.op("vector", "tensor_scalar", rs16[:, 4 * g:4 * g + 4],
                     tmp4[:, :], nw0[:, :], None, ALU.add, inc=("dppr", 1))
            else:
                p.op("vector", "tensor_scalar", rs16[:, 4 * g:4 * g + 4],
                     tmp4[:, :], 129.0, None, ALU.add, inc=("dppr", 1))
            for c in range(8):
                p.op("tensor", "transpose", PTD[:, 128 * c:128 * (c + 1)],
                     ppm[:, 128 * c:128 * (c + 1)], ident_b[:, :],
                     wait=([("dmsk", GI + 1), ("dpe", GI), ("doe", T)]
                           if c == 0 else ()),
                     inc=("dpT", 1) if c == 7 else None)
            p.op("vector", "tensor_copy", out=pTg[:, :], in_=PTD[:, :],
                 wait=[("dpT", GI + 1), ("dpv", GI)], inc=("dpe", 1))
            for cc in range(4):
                h = 4 * g + SLOT_HD[cc]
                s = 4 * g + cc
                p.op("tensor", "matmul", PO[:, 64 * s:64 * (s + 1)],
                     LT_b[:, :], vt[(T - 1) % 3][:, 64 * h:64 * (h + 1)],
                     start=True, stop=False,
                     wait=([("dpe", GI + 1)] if cc == 0 else ()))
                p.op("tensor", "matmul", PO[:, 64 * s:64 * (s + 1)],
                     UT_b[:, :], vt[T % 3][:, 64 * h:64 * (h + 1)],
                     start=False, stop=False)
                p.op("tensor", "matmul", PO[:, 64 * s:64 * (s + 1)],
                     pTg[:, 256 * cc:256 * cc + 128],
                     vt[(T - 1) % 3][:, 64 * h:64 * (h + 1)],
                     start=False, stop=False)
                p.op("tensor", "matmul", PO[:, 64 * s:64 * (s + 1)],
                     pTg[:, 256 * cc + 128:256 * cc + 256],
                     vt[T % 3][:, 64 * h:64 * (h + 1)],
                     start=False, stop=True,
                     inc=("dpv", 1) if cc == 3 else None)
        p.op("vector", "reciprocal", rnrm[:, :], rs16[:, :],
             wait=[("dpv", 4 * (T + 1)), ("dppr", 4 * (T + 1))])
        p.fence("vector")
        for a in range(2):
            for b in range(2):
                p.op("vector", "tensor_tensor",
                     onb[:, :].rearrange("p (g e) -> p g e", e=256)[
                         :, :, 128 * a + 64 * b:128 * a + 64 * b + 64],
                     PO[:, :].rearrange("p (g e) -> p g e", e=256)[
                         :, :, 128 * b + 64 * a:128 * b + 64 * a + 64],
                     rnrm[:, :].rearrange("p (g c) -> p g c", c=4)[
                         :, :, 2 * b + a:2 * b + a + 1].broadcast_to((128, 4, 64)),
                     ALU.mult, inc=("donb" if (a, b) == (1, 1) else None))
        for jj in range(8):
            p.op("tensor", "transpose", PTD[:, 128 * jj:128 * (jj + 1)],
                 onb[:, 128 * jj:128 * (jj + 1)], ident_b[:, :],
                 wait=([("donb", T + 1), ("dpe", 4 * (T + 1))] if jj == 0 else ()),
                 inc=("doT" if jj == 7 else None))
        p.op("vector", "tensor_copy", out=onT[:, :],
             in_=PTD[:, :], wait=[("doT", T + 1), ("da1", T)], inc="doe")
        for half, bank in ((0, PA1), (1, PA2)):
            for k in range(8):
                p.op("tensor", "matmul", bank[:, 0:512],
                     onT[:, 128 * k:128 * (k + 1)],
                     WoT_b[:, 1024 * k + 512 * half:1024 * k + 512 * (half + 1)],
                     start=(k == 0), stop=False,
                     wait=([("doe", T + 1), ("dop2", T)]
                           if (half == 0 and k == 0) else ()))
            p.op("tensor", "matmul", bank[:, 0:512], ones1_b[:, :],
                 bo_r[0:1, 512 * half:512 * (half + 1)], start=False, stop=True,
                 inc=("da1" if half == 1 else None))
        for half, bank in ((0, PA1), (1, PA2)):
            p.op("vector", "tensor_tensor", op2[T % 2][:, 512 * half:512 * (half + 1)],
                 bank[:, 0:512], x1t[T % 2][:, 512 * half:512 * (half + 1)],
                 ALU.add,
                 wait=([("da1", T + 1), ("dso", 16 * (T - 1))] if half == 0 else ()))
        p.op("vector", "tensor_tensor", op2[T % 2][:, :], op2[T % 2][:, :],
             bst[T % 2][:, :], ALU.add, inc="dop")
        p.op("vector", "nop", inc="dop2")
        p.op("gpsimd", "dma_start", out=io["op2_d"][t0:t0 + 128, :],
             in_=op2[T % 2][:, :], wait=[("dop", T + 1)], inc="dso", dma=True)
    p.op("gpsimd", "nop", wait=[("dso", 16 * NT)])



def emit_phase_f(p, env, io):
    Wad_b = p.sb("Wad_bF", [128, 8 * 512], BF16)
    Wup_b = p.sb("Wup_bF", [128, 4 * 1024], BF16)
    bad_r = p.sb("bad_rF", [1, 512], BF16)
    bau_r = p.sb("bau_rF", [1, 1024], BF16)
    ohrow = p.sb("ohrowF", [1, 8], F32)
    oh64 = p.sb("oh64F", [1, 512], F32)
    ohcol = p.sb("ohcolF", [128, 4], F32)
    ohT_s = p.sb("ohT_sF", [8, 1], F32)
    bau_all = p.sb("bau_allF", [8, 1024], F32)

    PTF = p.ps("PTF", [128, 1024], BF16)
    PFA = p.ps("PFA", [128, 512], F32)
    PFB1 = p.ps("PFB1", [128, 512], F32)
    PFB2 = p.ps("PFB2", [128, 512], F32)
    PPR = p.ps("PPRF", [128, 512], F32)

    ident_b = env["ident_b"]
    ones1_b = env["ones1_b"]
    onesc_f = env["onesc_f"]

    wstg = p.sb("wstgF", [128, 1024], F32)
    wi = 0
    for e in range(8):
        for k in range(8):
            p.op("sync", "dma_start", out=wstg[0:128, 0:64],
                 in_=io["Wa_down"][e, 128 * k:128 * (k + 1), :],
                 wait=[("fwq", wi)], inc="fwl", dma=True)
            p.op("vector", "tensor_copy",
                 out=Wad_b[:, 512 * k + 64 * e:512 * k + 64 * (e + 1)],
                 in_=wstg[0:128, 0:64],
                 wait=[("fwl", 16 * (wi + 1))], inc="fwc")
            p.op("vector", "nop", inc="fwq")
            wi += 1
    for c in range(4):
        for s in range(2):
            e = 2 * c + s
            p.op("sync", "dma_start", out=wstg[0:64, 0:1024],
                 in_=io["Wa_up"][e], wait=[("fwq", wi)], inc="fwl", dma=True)
            p.op("vector", "tensor_copy",
                 out=Wup_b[64 * s:64 * (s + 1), 1024 * c:1024 * (c + 1)],
                 in_=wstg[0:64, 0:1024],
                 wait=[("fwl", 16 * (wi + 1))], inc="fwc")
            p.op("vector", "nop", inc="fwq")
            wi += 1
    p.op("sync", "dma_start", out=wstg[0:1, 0:512],
         in_=io["ba_down"].rearrange("e r -> (e r)").rearrange("(o d) -> o d", o=1),
         wait=[("fwq", wi)], inc="fwl", dma=True)
    p.op("vector", "tensor_copy", out=bad_r[0:1, :], in_=wstg[0:1, 0:512],
         wait=[("fwl", 16 * (wi + 1))], inc="fwc")
    p.op("vector", "nop", inc="fwq")
    wi += 1
    p.op("sync", "dma_start", out=bau_all[:, :], in_=io["ba_up"],
         wait=[("fwq", wi)], inc="fwl", dma=True)
    p.op("sync", "dma_start", out=ohrow[:, :], in_=io["oh_d"], inc="fwl", dma=True)
    NWF0 = wi
    wi += 1
    p.op("vector", "tensor_copy",
         out=oh64[:, :].rearrange("o (e r) -> o e r", e=8),
         in_=ohrow[:, :].rearrange("o (e r) -> o e r", r=1).broadcast_to((1, 8, 64)),
         wait=[("fwl", 16 * (wi + 1))], inc="foh")
    for c in range(4):
        p.op("tensor", "matmul", PPR[0:128, c:c + 1],
             oh64[0:1, 128 * c:128 * (c + 1)], onesc_f[0:1, 0:1],
             start=True, stop=True,
             wait=([("foh", 1)] if c == 0 else ()),
             inc=("fohm", 1))
    p.op("tensor", "matmul", PPR[0:8, 8:9], ohrow[0:1, :], onesc_f[0:1, 0:1],
         start=True, stop=True, inc=("fohm", 1))
    p.op("vector", "tensor_copy", out=ohcol[:, :], in_=PPR[0:128, 0:4],
         wait=[("fohm", 5)])
    p.op("vector", "tensor_copy", out=ohT_s[:, :], in_=PPR[0:8, 8:9], inc="fohc")
    for c in range(4):
        p.op("vector", "tensor_scalar",
             Wup_b[:, 1024 * c:1024 * (c + 1)], Wup_b[:, 1024 * c:1024 * (c + 1)],
             ohcol[:, c:c + 1], None, ALU.mult,
             wait=([("fohc", 1), ("fwc", p.cnt["fwc"])] if c == 0 else ()),
             inc=("fwup", 1))
    p.op("tensor", "matmul", PPR[0:1, 0:512], ohT_s[:, :], bau_all[:, 0:512],
         start=True, stop=True, wait=[("fohc", 1), ("fwl", 16 * (wi + 1))],
         inc="fbau")
    p.op("vector", "tensor_copy", out=bau_r[0:1, 0:512], in_=PPR[0:1, 0:512],
         wait=[("fbau", 1)], inc="fbaue")
    p.op("tensor", "matmul", PPR[0:1, 0:512], ohT_s[:, :], bau_all[:, 512:1024],
         start=True, stop=True, wait=[("fbaue", 1)], inc="fbau")
    p.op("vector", "tensor_copy", out=bau_r[0:1, 512:1024], in_=PPR[0:1, 0:512],
         wait=[("fbau", 2)], inc="fprep")

    xnTt = [p.sb(f"xnTtF{i}", [128, 1024], BF16) for i in range(2)]
    op2t = [p.sb(f"op2tF{i}", [128, 1024], F32) for i in range(2)]
    a_b = p.sb("a_bF", [128, 512], BF16)
    aT = p.sb("aTF", [128, 512], BF16)
    out_sb = [p.sb(f"out_sbF{i}", [128, 1024], F32) for i in range(2)]

    first_waits = [("fwup", 4), ("fprep", 1)]
    fw = {e: list(first_waits) for e in ENGINES}

    def W(eng, extra=()):
        ws = fw[eng] + list(extra)
        fw[eng] = []
        return ws

    for T in range(NT):
        t0 = T * 128
        p.op("sync", "dma_start", out=xnTt[T % 2][:, :],
             in_=io["xn1T_d"][t0:t0 + 128, :],
             wait=W("sync", [("fpa", T - 1)]), inc="fld", dma=True)
        p.op("sync", "dma_start", out=op2t[T % 2][:, :],
             in_=io["op2_d"][t0:t0 + 128, :],
             wait=[("ffin", T - 1)], inc="fld", dma=True)
        for k in range(8):
            p.op("tensor", "matmul", PFA[:, 0:512],
                 xnTt[T % 2][:, 128 * k:128 * (k + 1)],
                 Wad_b[:, 512 * k:512 * (k + 1)], start=(k == 0), stop=False,
                 wait=(W("tensor", [("fld", 32 * T + 16), ("fab", T)])
                       if k == 0 else ()))
        p.op("tensor", "matmul", PFA[:, 0:512], ones1_b[:, :], bad_r[0:1, :],
             start=False, stop=True, inc="fpa")
        p.op("scalar", "activation", a_b[:, :], PFA[:, 0:512], AF.Relu,
             wait=W("scalar", [("fpa", T + 1), ("faT", T)]), inc="fab")
        for c in range(4):
            p.op("tensor", "transpose", PTF[:, 128 * c:128 * (c + 1)],
                 a_b[:, 128 * c:128 * (c + 1)], ident_b[:, :],
                 wait=([("fab", T + 1), ("faTe", T)] if c == 0 else ()),
                 inc=("faT" if c == 3 else None))
        p.op("vector", "tensor_copy", out=aT[:, :], in_=PTF[:, 0:512],
             wait=W("vector", [("faT", T + 1), ("fup", T)]), inc="faTe")
        for half, bank in ((0, PFB1), (1, PFB2)):
            for c in range(4):
                p.op("tensor", "matmul", bank[:, 0:512],
                     aT[:, 128 * c:128 * (c + 1)],
                     Wup_b[:, 1024 * c + 512 * half:1024 * c + 512 * (half + 1)],
                     start=(c == 0), stop=False,
                     wait=([("faTe", T + 1), ("ffin", T)]
                           if (half == 0 and c == 0) else ()))
            p.op("tensor", "matmul", bank[:, 0:512], ones1_b[:, :],
                 bau_r[0:1, 512 * half:512 * (half + 1)], start=False, stop=True,
                 inc=("fup" if half == 1 else None))
        for half, bank in ((0, PFB1), (1, PFB2)):
            p.op("vector", "tensor_tensor",
                 out_sb[T % 2][:, 512 * half:512 * (half + 1)], bank[:, 0:512],
                 op2t[T % 2][:, 512 * half:512 * (half + 1)], ALU.add,
                 wait=([("fup", T + 1), ("fld", 32 * (T + 1)),
                        ("fso", 16 * (T - 1))] if half == 0 else ()),
                 inc=("ffin" if half == 1 else None))
        p.op("gpsimd", "dma_start", out=io["out_d"][t0:t0 + 128, :],
             in_=out_sb[T % 2][:, :], wait=W("gpsimd", [("ffin", T + 1)]),
             inc="fso", dma=True)
    p.op("gpsimd", "nop", wait=[("fso", 16 * NT)])



def build_fused(phases="abcdf"):
    nc = bass.Bass("TRN2", target_bir_lowering=False)
    E_IN = "ExternalInput"
    INT = "Internal"
    t = lambda n, s, dt, k: nc.dram_tensor(n, s, dt, kind=k).ap()
    g = {}
    g["x0"] = t("x0", [4096, 1024], F32, E_IN)
    for nm, shp in [("Wq", [1024, 512]), ("Wk", [1024, 512]), ("Wg1", [1024, 16]),
                    ("Wg2", [16, 512]), ("Wv", [1024, 1024]), ("Wg", [1024, 1024]),
                    ("Wo", [1024, 1024]), ("bg", [1024]),
                    ("w_gate", [2816, 1024]), ("w_up", [2816, 1024]),
                    ("w_down", [1024, 2816]),
                    ("L1Wg", [1024, 1024]), ("L1bg", [1024]),
                    ("L1Wv", [1024, 1024]), ("L1bv", [1024]),
                    ("L1Wo", [1024, 1024]), ("L1bo", [1024]),
                    ("W_base", [1024, 1024]), ("b_base", [1024]),
                    ("Wd1", [8, 1024, 256]), ("bd1", [8, 256]),
                    ("Wd2", [8, 256, 1024]), ("bd2", [8, 1024]),
                    ("Wa_down", [8, 1024, 64]), ("ba_down", [8, 64]),
                    ("Wa_up", [8, 64, 1024]), ("ba_up", [8, 1024])]:
        g[nm] = t(nm, shp, F32, E_IN)
    g["r1"] = t("r1", [4096, 1024], F32, INT)
    g["xq"] = t("xq_s", [4096, 1024], BF16, INT)
    g["x1"] = t("x1_s", [4096, 1024], F32, INT)
    g["xn1"] = t("xn1_s", [4096, 1024], BF16, INT)
    g["xn1T_d"] = t("xn1T_s", [4096, 1024], BF16, INT)
    g["gT_d"] = t("gT_s", [4096, 1024], BF16, INT)
    g["v_d"] = t("v_s", [4096, 1024], BF16, INT)
    g["base_d"] = t("base_s", [4096, 1024], F32, INT)
    g["denom_d"] = t("denom_s", [4096, 16], F32, INT)
    g["sinv_d"] = t("sinv_s", [1, 16], F32, INT)
    g["oh_d"] = t("oh_s", [1, 8], F32, INT)
    g["op2_d"] = t("op2_s", [4096, 1024], F32, INT)
    g["out_d"] = t("out_d", [4096, 1024], F32, "ExternalOutput")

    p = P(nc)
    env = {}
    emit_consts(p, env)
    emitters = {"a": emit_phase_a, "b": emit_phase_b, "c": emit_phase_ce,
                "d": emit_phase_d, "f": emit_phase_f}
    for ph in phases:
        p.push_scope()
        emitters[ph](p, env, g)
        p.pop_scope()
        p.barrier()
    p.build()
    p.stack.close()
    return nc



_CACHE = {}


def _build_a():
    nc = bass.Bass("TRN2", target_bir_lowering=False)
    io = {}
    io["x0"] = nc.dram_tensor("x0", [4096, 1024], F32, kind="ExternalInput").ap()
    for nm, shp in [("Wq", [1024, 512]), ("Wk", [1024, 512]), ("Wg1", [1024, 16]),
                    ("Wg2", [16, 512]), ("Wv", [1024, 1024]), ("Wg", [1024, 1024]),
                    ("Wo", [1024, 1024])]:
        io[nm] = nc.dram_tensor(nm, shp, F32, kind="ExternalInput").ap()
    io["bg"] = nc.dram_tensor("bg", [1024], F32, kind="ExternalInput").ap()
    io["r1"] = nc.dram_tensor("r1", [4096, 1024], F32, kind="ExternalOutput").ap()
    io["xq"] = nc.dram_tensor("xq", [4096, 1024], BF16, kind="ExternalOutput").ap()
    p = P(nc)
    env = {}
    emit_consts(p, env)
    emit_phase_a(p, env, io)
    p.build()
    p.stack.close()
    return nc


def _build_b():
    nc = bass.Bass("TRN2", target_bir_lowering=False)
    io = {}
    io["xq"] = nc.dram_tensor("xq", [4096, 1024], BF16, kind="ExternalInput").ap()
    io["r1"] = nc.dram_tensor("r1", [4096, 1024], F32, kind="ExternalInput").ap()
    io["w_gate"] = nc.dram_tensor("w_gate", [2816, 1024], F32, kind="ExternalInput").ap()
    io["w_up"] = nc.dram_tensor("w_up", [2816, 1024], F32, kind="ExternalInput").ap()
    io["w_down"] = nc.dram_tensor("w_down", [1024, 2816], F32, kind="ExternalInput").ap()
    io["x1"] = nc.dram_tensor("x1", [4096, 1024], F32, kind="ExternalOutput").ap()
    io["xn1"] = nc.dram_tensor("xn1", [4096, 1024], BF16, kind="ExternalOutput").ap()
    p = P(nc)
    env = {}
    emit_consts(p, env)
    emit_phase_b(p, env, io)
    p.build()
    p.stack.close()
    return nc


def _build_ce():
    nc = bass.Bass("TRN2", target_bir_lowering=False)
    io = {}
    io["xn1"] = nc.dram_tensor("xn1", [4096, 1024], BF16, kind="ExternalInput").ap()
    for nm, shp in [("L1Wg", [1024, 1024]), ("L1bg", [1024]), ("L1Wv", [1024, 1024]),
                    ("L1bv", [1024]), ("W_base", [1024, 1024]), ("b_base", [1024]),
                    ("Wd1", [8, 1024, 256]), ("bd1", [8, 256]),
                    ("Wd2", [8, 256, 1024]), ("bd2", [8, 1024])]:
        io[nm] = nc.dram_tensor(nm, shp, F32, kind="ExternalInput").ap()
    for nm, shp, dt in [("xn1T_d", [4096, 1024], BF16), ("gT_d", [4096, 1024], BF16),
                        ("v_d", [4096, 1024], BF16), ("base_d", [4096, 1024], F32),
                        ("denom_d", [4096, 16], F32), ("sinv_d", [1, 16], F32),
                        ("oh_d", [1, 8], F32)]:
        io[nm] = nc.dram_tensor(nm, shp, dt, kind="ExternalOutput").ap()
    p = P(nc)
    env = {}
    emit_consts(p, env)
    emit_phase_ce(p, env, io)
    p.build()
    p.stack.close()
    return nc


def _build_d():
    nc = bass.Bass("TRN2", target_bir_lowering=False)
    io = {}
    for nm, shp, dt in [("gT_d", [4096, 1024], BF16), ("v_d", [4096, 1024], BF16),
                        ("x1", [4096, 1024], F32), ("base_d", [4096, 1024], F32),
                        ("denom_d", [4096, 16], F32), ("sinv_d", [1, 16], F32),
                        ("L1Wo", [1024, 1024], F32), ("L1bo", [1024], F32)]:
        io[nm] = nc.dram_tensor(nm, shp, dt, kind="ExternalInput").ap()
    io["op2_d"] = nc.dram_tensor("op2_d", [4096, 1024], F32, kind="ExternalOutput").ap()
    p = P(nc)
    env = {}
    emit_consts(p, env)
    emit_phase_d(p, env, io)
    p.build()
    p.stack.close()
    return nc


def _build_f():
    nc = bass.Bass("TRN2", target_bir_lowering=False)
    io = {}
    for nm, shp, dt in [("xn1T_d", [4096, 1024], BF16), ("op2_d", [4096, 1024], F32),
                        ("oh_d", [1, 8], F32), ("Wa_down", [8, 1024, 64], F32),
                        ("ba_down", [8, 64], F32), ("Wa_up", [8, 64, 1024], F32),
                        ("ba_up", [8, 1024], F32)]:
        io[nm] = nc.dram_tensor(nm, shp, dt, kind="ExternalInput").ap()
    io["out_d"] = nc.dram_tensor("out_d", [4096, 1024], F32, kind="ExternalOutput").ap()
    p = P(nc)
    env = {}
    emit_consts(p, env)
    emit_phase_f(p, env, io)
    p.build()
    p.stack.close()
    return nc


def kernel(**inputs):
    cores = list(range(8))
    x0 = np.ascontiguousarray(inputs["L0_attn_inputs"], dtype=np.float32)
    nc = _CACHE.setdefault("fused", build_fused())
    in_maps = [{
        "x0": np.ascontiguousarray(x0[c]),
        "Wq": inputs["L0_attn_Wq"], "Wk": inputs["L0_attn_Wk"],
        "Wg1": inputs["L0_attn_Wg1"], "Wg2": inputs["L0_attn_Wg2"],
        "Wv": inputs["L0_attn_Wv"], "Wg": inputs["L0_attn_Wg"],
        "Wo": inputs["L0_attn_Wo"], "bg": inputs["L0_attn_bg"],
        "w_gate": inputs["L0_ffn_w_gate"], "w_up": inputs["L0_ffn_w_up"],
        "w_down": inputs["L0_ffn_w_down"],
        "L1Wg": inputs["L1_attn_Wg"], "L1bg": inputs["L1_attn_bg"],
        "L1Wv": inputs["L1_attn_Wv"], "L1bv": inputs["L1_attn_bv"],
        "L1Wo": inputs["L1_attn_Wo"], "L1bo": inputs["L1_attn_bo"],
        "W_base": inputs["L1_ffn_W_base"], "b_base": inputs["L1_ffn_b_base"],
        "Wd1": inputs["L1_ffn_Wd1"], "bd1": inputs["L1_ffn_bd1"],
        "Wd2": inputs["L1_ffn_Wd2"], "bd2": inputs["L1_ffn_bd2"],
        "Wa_down": inputs["L1_ffn_Wa_down"], "ba_down": inputs["L1_ffn_ba_down"],
        "Wa_up": inputs["L1_ffn_Wa_up"], "ba_up": inputs["L1_ffn_ba_up"],
    } for c in range(8)]
    res = run_bass_kernel_spmd(nc, in_maps, core_ids=cores).results
    return np.stack([res[c]["out_d"] for c in range(8)]).astype(np.float32)


def kernel_multilaunch(**inputs):
    cores = list(range(8))
    x0 = np.ascontiguousarray(inputs["L0_attn_inputs"], dtype=np.float32)

    nc_a = _CACHE.setdefault("a", _build_a())
    in_a = [{
        "x0": np.ascontiguousarray(x0[c]),
        "Wq": inputs["L0_attn_Wq"], "Wk": inputs["L0_attn_Wk"],
        "Wg1": inputs["L0_attn_Wg1"], "Wg2": inputs["L0_attn_Wg2"],
        "Wv": inputs["L0_attn_Wv"], "Wg": inputs["L0_attn_Wg"],
        "Wo": inputs["L0_attn_Wo"], "bg": inputs["L0_attn_bg"],
    } for c in range(8)]
    ra = run_bass_kernel_spmd(nc_a, in_a, core_ids=cores).results

    nc_b = _CACHE.setdefault("b", _build_b())
    in_b = [{
        "xq": ra[c]["xq"], "r1": ra[c]["r1"],
        "w_gate": inputs["L0_ffn_w_gate"], "w_up": inputs["L0_ffn_w_up"],
        "w_down": inputs["L0_ffn_w_down"],
    } for c in range(8)]
    rb = run_bass_kernel_spmd(nc_b, in_b, core_ids=cores).results

    nc_ce = _CACHE.setdefault("ce", _build_ce())
    in_ce = [{
        "xn1": rb[c]["xn1"],
        "L1Wg": inputs["L1_attn_Wg"], "L1bg": inputs["L1_attn_bg"],
        "L1Wv": inputs["L1_attn_Wv"], "L1bv": inputs["L1_attn_bv"],
        "W_base": inputs["L1_ffn_W_base"], "b_base": inputs["L1_ffn_b_base"],
        "Wd1": inputs["L1_ffn_Wd1"], "bd1": inputs["L1_ffn_bd1"],
        "Wd2": inputs["L1_ffn_Wd2"], "bd2": inputs["L1_ffn_bd2"],
    } for c in range(8)]
    rc = run_bass_kernel_spmd(nc_ce, in_ce, core_ids=cores).results

    nc_d = _CACHE.setdefault("d", _build_d())
    in_d = [{
        "gT_d": rc[c]["gT_d"], "v_d": rc[c]["v_d"], "x1": rb[c]["x1"],
        "base_d": rc[c]["base_d"], "denom_d": rc[c]["denom_d"],
        "sinv_d": rc[c]["sinv_d"],
        "L1Wo": inputs["L1_attn_Wo"], "L1bo": inputs["L1_attn_bo"],
    } for c in range(8)]
    rd = run_bass_kernel_spmd(nc_d, in_d, core_ids=cores).results

    nc_f = _CACHE.setdefault("f", _build_f())
    in_f = [{
        "xn1T_d": rc[c]["xn1T_d"], "op2_d": rd[c]["op2_d"], "oh_d": rc[c]["oh_d"],
        "Wa_down": inputs["L1_ffn_Wa_down"], "ba_down": inputs["L1_ffn_ba_down"],
        "Wa_up": inputs["L1_ffn_Wa_up"], "ba_up": inputs["L1_ffn_ba_up"],
    } for c in range(8)]
    rf = run_bass_kernel_spmd(nc_f, in_f, core_ids=cores).results

    return np.stack([rf[c]["out_d"] for c in range(8)]).astype(np.float32)


# version 20
# speedup vs baseline: 1.2263x; 1.2263x over previous
import contextlib
import functools
import numpy as np
import ml_dtypes
import concourse.bass as bass
import concourse.mybir as mybir
from concourse.bass_utils import run_bass_kernel_spmd

F32 = mybir.dt.float32
BF16 = mybir.dt.bfloat16
AF = mybir.ActivationFunctionType
ALU = mybir.AluOpType
AX = mybir.AxisListType
ENGINES = ("sync", "tensor", "vector", "scalar", "gpsimd")

NT = 32
SLOT_HD = (0, 2, 1, 3)
D = 1024
H = 16
DK = 32
DV = 64
SCALE = DK ** -0.5
EPS = 1e-5


class _CntView:

    def __init__(self, p):
        self.p = p

    def __getitem__(self, name):
        h = self.p.sem(name)
        base = self.p.name2h[name][1]
        return self.p._hcnt[h.name] - base

    def get(self, name, default=0):
        if name in self.p.name2h:
            h, base = self.p.name2h[name]
            return self.p._hcnt[h.name] - base
        return default


class P:

    def __init__(self, nc):
        self.nc = nc
        self.stack = contextlib.ExitStack()
        self.scopes = []
        self.tape = {e: [] for e in ENGINES}
        self.name2h = {}
        self.free_h = []
        self.sems = {}
        self.cnt = _CntView(self)
        self._hcnt = {}
        self._nphys = 0
        self.pending = {e: [] for e in ENGINES}
        self._snap = {}
        self._nbar = 0

    def push_scope(self):
        self.scopes.append(contextlib.ExitStack())
        self._snap = dict(self._hcnt)

    def pop_scope(self):
        self.scopes.pop().close()

    def barrier(self):
        self._nbar += 1
        bar = f"_bar{self._nbar}"
        changed = [(pn, self._hcnt[pn]) for pn in self._hcnt
                   if self._hcnt[pn] != self._snap.get(pn, 0)]
        for eng in ENGINES:
            if self.tape[eng]:
                self.fence(eng)
            self.op(eng, "nop", wait_phys=changed, inc=(bar, 1))
        barh = self.name2h[bar][0]
        for eng in ENGINES:
            self.pending[eng].append((barh, 5))
        for n, (h, base) in self.name2h.items():
            self.free_h.append(h)
        self.name2h = {}
        self._snap = dict(self._hcnt)

    def fence(self, eng):
        tape = self.tape[eng]
        assert tape, "fence with empty tape"
        ws, method, args, kw, inc = tape[-1]
        if inc is None:
            h = self.sem(f"_f_{eng}")
            self._hcnt[h.name] += 1
            inc = (h, 1)
            tape[-1] = (ws, method, args, kw, inc)
        self.pending[eng].append((inc[0], self._hcnt[inc[0].name]))

    def sem(self, name):
        if name not in self.name2h:
            if self.free_h:
                h = self.free_h.pop()
            else:
                pn = f"S{self._nphys}"
                self._nphys += 1
                h = self.stack.enter_context(self.nc.semaphore(pn))
                self.sems[pn] = h
                self._hcnt[pn] = 0
            self.name2h[name] = (h, self._hcnt[h.name])
        return self.name2h[name][0]

    def sb(self, name, shape, dtype):
        st = self.scopes[-1] if self.scopes else self.stack
        return st.enter_context(self.nc.sbuf_tensor(name, shape, dtype))

    def ps(self, name, shape, dtype):
        st = self.scopes[-1] if self.scopes else self.stack
        return st.enter_context(self.nc.psum_tensor(name, shape, dtype))

    def op(self, eng, method, *args, wait=(), inc=None, dma=False,
           wait_phys=(), **kw):
        ws = []
        if self.pending[eng]:
            ws.extend(self.pending[eng])
            self.pending[eng] = []
        for w in wait:
            s, v = w
            if int(v) <= 0:
                continue
            h = self.sem(s)
            base = self.name2h[s][1]
            ws.append((h, base + int(v)))
        for pn, v in wait_phys:
            ws.append((self.sems[pn], int(v)))
        if inc is not None:
            if isinstance(inc, str):
                inc = (inc, 16 if dma else 1)
            if isinstance(inc[0], str):
                inc = (self.sem(inc[0]), inc[1])
            self._hcnt[inc[0].name] += inc[1]
        self.tape[eng].append((ws, method, args, kw, inc))

    def build(self):
        self.check_deadlock()
        with self.nc.Block() as block:
            for eng in ENGINES:
                if self.tape[eng]:
                    getattr(block, eng)(functools.partial(self._run, eng))

    def _run(self, eng, e):
        for ws, method, args, kw, inc in self.tape[eng]:
            for (h, v) in ws:
                if v > 0:
                    e.wait_ge(h, v)
            inst = getattr(e, method)(*args, **kw)
            if inc is not None:
                inst.then_inc(inc[0], inc[1])

    def check_deadlock(self):
        pos = {e: 0 for e in ENGINES}
        val = {pn: 0 for pn in self.sems}
        total = sum(len(t) for t in self.tape.values())
        done = 0
        progress = True
        while progress:
            progress = False
            for e in ENGINES:
                while pos[e] < len(self.tape[e]):
                    ws, method, args, kw, inc = self.tape[e][pos[e]]
                    if any(val[h.name] < v for h, v in ws):
                        break
                    if inc is not None:
                        val[inc[0].name] += inc[1]
                    pos[e] += 1
                    done += 1
                    progress = True
        if done != total:
            stuck = {e: (pos[e], len(self.tape[e]),
                         [(h.name, v) for h, v in self.tape[e][pos[e]][0]]
                         if pos[e] < len(self.tape[e]) else None)
                     for e in ENGINES}
            raise RuntimeError(f"deadlock: retired {done}/{total}; heads={stuck}")


def np_bf16(x):
    return np.asarray(x, np.float32).astype(ml_dtypes.bfloat16)


def dbg(p, env, name, ap, T, wait):
    if not env.get("debug"):
        return
    d = env.setdefault("dbg_tensors", {})
    if name not in d:
        shape = [NT] + list(ap.shape)
        d[name] = p.nc.dram_tensor("dbg_" + name, shape, ap.dtype,
                                   kind="ExternalOutput").ap()
    p.op("sync", "dma_start", out=d[name][T], in_=ap, wait=wait, inc="dbgs",
         dma=True)



def emit_consts(p, env):
    nc = p.nc
    ident = np.eye(128, dtype=ml_dtypes.bfloat16)
    s_idx = np.arange(128)[:, None]
    t_idx = np.arange(128)[None, :]
    ut = (s_idx <= t_idx)
    lt = (s_idx >= t_idx)
    env["ident_b"] = p.sb("ident_b", [128, 128], BF16)
    env["UT_b"] = p.sb("UT_b", [128, 128], BF16)
    env["LT_b"] = p.sb("LT_b", [128, 128], BF16)
    env["UT_f"] = p.sb("UT_f", [128, 128], F32)
    env["LT_f"] = p.sb("LT_f", [128, 128], F32)
    c_ident = nc.inline_tensor(ident, name="c_ident").ap()
    c_utb = nc.inline_tensor(ut.astype(ml_dtypes.bfloat16), name="c_utb").ap()
    c_ltb = nc.inline_tensor(lt.astype(ml_dtypes.bfloat16), name="c_ltb").ap()
    c_nw0 = nc.inline_tensor((np.arange(128, dtype=np.float32) + 1.0
                              ).reshape(128, 1), name="c_nw0").ap()
    c_utf = nc.inline_tensor(ut.astype(np.float32), name="c_utf").ap()
    c_ltf = nc.inline_tensor(lt.astype(np.float32), name="c_ltf").ap()
    p.op("sync", "dma_start", out=env["ident_b"][:, :], in_=c_ident, inc="cld", dma=True)
    p.op("sync", "dma_start", out=env["UT_b"][:, :], in_=c_utb, inc="cld", dma=True)
    p.op("sync", "dma_start", out=env["UT_f"][:, :], in_=c_utf, inc="cld", dma=True)
    p.op("sync", "dma_start", out=env["LT_f"][:, :], in_=c_ltf, inc="cld", dma=True)
    env["nw0"] = p.sb("nw0", [128, 1], F32)
    p.op("sync", "dma_start", out=env["LT_b"][:, :], in_=c_ltb, inc="cld", dma=True)
    p.op("sync", "dma_start", out=env["nw0"][:, :], in_=c_nw0, inc="cld", dma=True)
    env["cld_target"] = 96

    env["ones1_b"] = p.sb("ones1_b", [1, 128], BF16)
    env["onesc_f"] = p.sb("onesc_f", [128, 1], F32)
    env["c_one"] = p.sb("c_one", [128, 1], F32)
    env["c_eps"] = p.sb("c_eps", [128, 1], F32)
    env["ones1_f"] = p.sb("ones1_f", [1, 128], F32)
    p.op("vector", "memset", env["ones1_f"][:, :], 1.0)
    p.op("vector", "memset", env["ones1_b"][:, :], 1.0)
    p.op("vector", "memset", env["onesc_f"][:, :], 1.0)
    p.op("vector", "memset", env["c_one"][:, :], 1.0)
    p.op("vector", "memset", env["c_eps"][:, :], EPS, inc="cms")
    env["cms_target"] = p.cnt["cms"]


def emit_weight_prep(p, env, specs):
    if "wst" not in env:
        env["wst"] = [p.sb(f"wst{j}", [128, 1024], F32) for j in range(2)]
    wst = env["wst"]
    i = env.setdefault("wp_i", 0)
    for (src, rows, cols, dst, coff) in specs:
        nchunk = (rows + 127) // 128
        for k in range(nchunk):
            r0 = k * 128
            r1 = min(rows, r0 + 128)
            nr = r1 - r0
            p.op("sync", "dma_start", out=wst[i % 2][0:nr, 0:cols],
                 in_=src[r0:r1, :], wait=[("wcast", i - 1)], inc="wld", dma=True)
            p.op("vector", "tensor_copy",
                 out=dst[0:nr, coff + k * cols: coff + (k + 1) * cols],
                 in_=wst[i % 2][0:nr, 0:cols],
                 wait=[("wld", 16 * (i + 1))], inc="wcast")
            i += 1
    env["wp_i"] = i
    return i



def emit_phase_a(p, env, io):
    nc = p.nc

    Wq_b = p.sb("Wq_b", [128, 8 * 512], BF16)
    Wk_b = p.sb("Wk_b", [128, 8 * 512], BF16)
    Wv_b = p.sb("Wv_b", [128, 8 * 1024], BF16)
    Wg_b = p.sb("Wg_b", [128, 8 * 1024], BF16)
    Wo_b = p.sb("Wo_b", [128, 8 * 1024], BF16)
    Wg1_b = p.sb("Wg1_b", [128, 8 * 16], BF16)
    Wg2_b = p.sb("Wg2_b", [16, 512], BF16)
    bg_b = p.sb("bg_b", [1, 1024], BF16)
    emit_weight_prep(p, env, [
        (io["Wq"], 1024, 512, Wq_b, 0),
        (io["Wk"], 1024, 512, Wk_b, 0),
        (io["Wv"], 1024, 1024, Wv_b, 0),
        (io["Wg"], 1024, 1024, Wg_b, 0),
        (io["Wo"], 1024, 1024, Wo_b, 0),
        (io["Wg1"], 1024, 16, Wg1_b, 0),
        (io["Wg2"], 16, 512, Wg2_b, 0),
        (io["bg"].rearrange("(o d) -> o d", o=1), 1, 1024, bg_b, 0),
    ])
    NW = env["wp_i"]

    PT = p.ps("PT", [128, 1024], BF16)
    PA = p.ps("PA", [128, 512], F32)
    PB = p.ps("PB", [128, 512], F32)
    PC = p.ps("PC", [128, 512], F32)
    PD = p.ps("PD", [128, 512], F32)
    PVO = p.ps("PVO", [128, 1024], F32)
    PK = p.ps("PK", [128, 512], F32)

    xt = [p.sb(f"xt{i}", [128, 1024], F32) for i in range(2)]
    xsq = p.sb("xsq", [128, 1024], F32)
    ssq = p.sb("ssq", [128, 1], F32)
    sumx = p.sb("sumx", [128, 1], F32)
    mean = p.sb("mean", [128, 1], F32)
    msq = p.sb("msq", [128, 1], F32)
    var = p.sb("var", [128, 1], F32)
    tv1 = p.sb("tv1", [128, 1], F32)
    lnv = p.sb("lnv", [128, 1], F32)
    rstd = p.sb("rstd", [128, 1], F32)
    nm = p.sb("nm", [128, 1], F32)
    xnb = p.sb("xnb", [128, 1024], BF16)
    xnf = p.sb("xnf", [128, 1024], F32)
    xnT = p.sb("xnT", [128, 1024], BF16)
    t1b = p.sb("t1b", [16, 128], BF16)
    e1 = p.sb("e1", [128, 512], F32)
    u_tok = p.sb("u_tok", [128, 512], F32)
    A_b = p.sb("A_b", [128, 512], BF16)
    iA_b = p.sb("iA_b", [128, 512], BF16)
    A_end = p.sb("A_end", [128, 4], F32)
    qt_b = p.sb("qt_b", [128, 512], BF16)
    kt_b = p.sb("kt_b", [128, 512], BF16)
    qfm = p.sb("qfm", [128, 512], BF16)
    kfm = p.sb("kfm", [128, 512], BF16)
    v_b = p.sb("v_b", [128, 1024], BF16)
    P_b = [p.sb(f"P_b{i}", [128, 128], BF16) for i in range(2)]
    S_sb = p.sb("S_sb", [128, 256], F32)
    S_bb = p.sb("S_bb", [128, 256], BF16)
    t_s = p.sb("t_s", [128, 256], F32)
    o_mu = p.sb("o_mu", [128, 16], F32)
    o_s2 = p.sb("o_s2", [128, 16], F32)
    om_m = p.sb("om_m", [128, 16], F32)
    msq2 = p.sb("msq2", [128, 16], F32)
    var2 = p.sb("var2", [128, 16], F32)
    tv2 = p.sb("tv2", [128, 16], F32)
    lnv2 = p.sb("lnv2", [128, 16], F32)
    rstd2 = p.sb("rstd2", [128, 16], F32)
    nm2 = p.sb("nm2", [128, 16], F32)
    osq_sb = p.sb("osq_sb", [128, 1024], F32)
    o_n = p.sb("o_n", [128, 1024], F32)
    o_raw = p.sb("o_raw", [128, 1024], F32)
    sg = p.sb("sg", [128, 1024], F32)
    eg = p.sb("eg", [128, 512], F32)
    gate_f = p.sb("gate_f", [128, 1024], F32)
    go_b = p.sb("go_b", [128, 1024], BF16)
    goT_sb = p.sb("goT_sb", [128, 1024], BF16)
    r1_sb = [p.sb(f"r1_sb{i}", [128, 1024], F32) for i in range(2)]
    bmax = p.sb("bmax", [128, 64], F32)
    scl = p.sb("scl", [128, 64], F32)
    rscl = p.sb("rscl", [128, 64], F32)
    a_n = p.sb("a_n", [128, 1024], F32)
    accq = p.sb("accq", [128, 1024], F32)
    qtmp = p.sb("qtmp", [128, 1024], F32)
    qt2 = p.sb("qt2", [128, 1024], F32)
    stair = p.sb("stair", [128, 1024], F32)
    sgn = p.sb("sgn", [128, 1024], F32)
    sscl = p.sb("sscl", [128, 1024], F32)
    xq_b = [p.sb(f"xq_b{i}", [128, 1024], BF16) for i in range(2)]
    m8A = p.sb("m8A", [128, 1024], mybir.dt.uint8)
    QA = {"bmax": bmax, "scl": scl, "rscl": rscl,
          "a": a_n, "acc2": accq, "tmp2": qtmp, "t22": qt2, "sscl": sscl,
          "m8": m8A}

    ident_b = env["ident_b"]
    UT_b = env["UT_b"]
    UT_f = env["UT_f"]
    ones1_b = env["ones1_b"]
    onesc_f = env["onesc_f"]
    c_one = env["c_one"]
    c_eps = env["c_eps"]

    p.op("vector", "memset", S_sb[:, :], 0.0, inc="cms")

    first_waits = [("wcast", NW), ("cld", env["cld_target"]), ("cms", p.cnt["cms"])]
    fw = {e: list(first_waits) for e in ENGINES}

    def W(eng, extra=()):
        ws = fw[eng] + list(extra)
        fw[eng] = []
        return ws

    x0 = io["x0"]
    r1_d = io["r1"]
    xq_d = io["xq"]

    for T in range(NT):
        t0 = T * 128
        dbgw = [("dbgs", p.cnt.get("dbgs", 0))] if env.get("debug") else []
        p.op("sync", "dma_start", out=xt[T % 2][:, :], in_=x0[t0:t0 + 128, :],
             wait=W("sync", [("r1a", T - 1)] + dbgw), inc="ldx", dma=True)

        p.op("scalar", "activation", xsq[:, :], xt[T % 2][:, :], AF.Square,
             accum_out=ssq[:, :], wait=W("scalar", [("ldx", 16 * (T + 1))]), inc="lnq")
        p.op("vector", "tensor_reduce", sumx[:, :], xt[T % 2][:, :], AX.X, ALU.add,
             wait=W("vector", [("ldx", 16 * (T + 1))]))
        p.fence("vector")
        p.op("vector", "tensor_scalar", mean[:, :], sumx[:, :], 1.0 / D, None, ALU.mult)
        p.op("vector", "tensor_tensor", msq[:, :], sumx[:, :], sumx[:, :], ALU.mult)
        p.op("vector", "tensor_scalar", tv1[:, :], ssq[:, :], 1.0 / D, None, ALU.mult,
             wait=[("lnq", T + 1)])
        p.fence("vector")
        p.op("vector", "scalar_tensor_tensor", var[:, :], msq[:, :],
             -1.0 / (D * D), tv1[:, :], ALU.mult, ALU.add, inc="lns")
        p.op("scalar", "activation", lnv[:, :], var[:, :], AF.Ln, bias=c_eps[:, :],
             wait=[("lns", T + 1)])
        p.fence("scalar")
        p.op("scalar", "activation", rstd[:, :], lnv[:, :], AF.Exp, scale=-0.5,
             inc="rsd")
        p.op("vector", "scalar_tensor_tensor", nm[:, :], mean[:, :], -1.0,
             rstd[:, :], ALU.mult, ALU.mult, wait=[("rsd", T + 1)], inc="nmr")
        p.op("scalar", "activation", xnb[:, :], xt[T % 2][:, :], AF.Identity,
             bias=nm[:, :], scale=rstd[:, :],
             wait=[("nmr", T + 1), ("tTn", T)], inc="xnr")
        p.op("scalar", "activation", xnf[:, :], xt[T % 2][:, :], AF.Identity,
             bias=nm[:, :], scale=rstd[:, :], wait=[("qst", T)], inc="xfr")

        dbg(p, env, "xnb", xnb[:, :], T, [("xnr", T + 1)])
        dbg(p, env, "xnf", xnf[:, :], T, [("xfr", T + 1)])
        for j in range(8):
            p.op("tensor", "transpose", PT[:, 128 * j:128 * (j + 1)],
                 xnb[:, 128 * j:128 * (j + 1)], ident_b[:, :],
                 wait=(W("tensor", [("xnr", T + 1), ("goe", T)]) if j == 0 else ()),
                 inc=("tTn" if j == 7 else None))
        p.op("vector", "tensor_copy", out=xnT[:, :], in_=PT[:, :],
             wait=[("tTn", T + 1), ("pgt", T)], inc="xne")

        for k in range(8):
            p.op("tensor", "matmul", PC[0:16, 0:128], Wg1_b[:, 16 * k:16 * (k + 1)],
                 xnT[:, 128 * k:128 * (k + 1)],
                 start=(k == 0), stop=(k == 7),
                 wait=([("xne", T + 1), ("r1a", T)] if k == 0 else ()),
                 inc=("pt1" if k == 7 else None))
        p.op("vector", "tensor_copy", out=t1b[:, :], in_=PC[0:16, 0:128],
             wait=[("pt1", T + 1), ("plg", T)], inc="t1e")
        p.op("tensor", "matmul", PA[:, 0:512], t1b[:, :], Wg2_b[:, :],
             start=True, stop=True,
             wait=[("t1e", T + 1), ("gfr", T), ("sg_", T)], inc="plg")
        p.op("scalar", "activation", e1[:, :], PA[:, 0:512], AF.Exp, scale=-1.0,
             wait=[("plg", T + 1)], inc="ae1")
        p.op("scalar", "activation", u_tok[:, :], e1[:, :], AF.Ln, bias=c_one[:, :],
             wait=[("pce", T)], inc="ulg")
        p.op("tensor", "matmul", PB[:, 0:512], UT_f[:, :], u_tok[:, :],
             start=True, stop=True,
             wait=[("ulg", T + 1), ("gfr", T), ("sg_", T)], inc="pcm")
        for c in range(4):
            p.op("tensor", "matmul", PC[:, 128 + c:129 + c],
                 u_tok[:, 128 * c:128 * (c + 1)], onesc_f[:, :],
                 start=True, stop=True, inc=("pce" if c == 3 else None))
        p.op("scalar", "activation", A_b[:, :], PB[:, 0:512], AF.Exp, scale=-1.0 / 16,
             wait=[("pcm", T + 1), ("qte", T), ("kte", T), ("sup", T)])
        p.op("scalar", "activation", iA_b[:, :], PB[:, 0:512], AF.Exp, scale=1.0 / 16)
        p.op("scalar", "activation", A_end[:, :], PC[:, 128:132], AF.Exp,
             scale=-1.0 / 16, wait=[("pce", T + 1)], inc="gA")

        dbg(p, env, "u_tok", u_tok[:, :], T, [("ulg", T + 1)])
        dbg(p, env, "A_b", A_b[:, :], T, [("gA", T + 1)])
        dbg(p, env, "iA_b", iA_b[:, :], T, [("gA", T + 1)])
        dbg(p, env, "A_end", A_end[:, :], T, [("gA", T + 1)])
        for k in range(8):
            p.op("tensor", "matmul", PA[:, 0:512], xnT[:, 128 * k:128 * (k + 1)],
                 Wq_b[:, 512 * k:512 * (k + 1)], start=(k == 0), stop=(k == 7),
                 wait=([("ae1", T + 1)] if k == 0 else ()),
                 inc=("pq" if k == 7 else None))
        p.op("vector", "scalar_tensor_tensor", qt_b[:, :], PA[:, 0:512], 1.0,
             A_b[:, :], ALU.mult, ALU.mult,
             wait=[("pq", T + 1), ("gA", T + 1), ("tqk", T)], inc="qte")
        for k in range(8):
            p.op("tensor", "matmul", PA[:, 0:512], xnT[:, 128 * k:128 * (k + 1)],
                 Wk_b[:, 512 * k:512 * (k + 1)], start=(k == 0), stop=(k == 7),
                 wait=([("qte", T + 1)] if k == 0 else ()),
                 inc=("pk" if k == 7 else None))
        p.op("vector", "scalar_tensor_tensor", kt_b[:, :], PA[:, 0:512], SCALE,
             iA_b[:, :], ALU.mult, ALU.mult,
             wait=[("pk", T + 1), ("kvd", T)], inc="kte")
        for half in range(2):
            for k in range(8):
                p.op("tensor", "matmul", PVO[:, 512 * half:512 * (half + 1)],
                     xnT[:, 128 * k:128 * (k + 1)],
                     Wv_b[:, 1024 * k + 512 * half:1024 * k + 512 * (half + 1)],
                     start=(k == 0), stop=(k == 7),
                     wait=([("oln", T)] if (half == 0 and k == 0) else ()),
                     inc=("pv" if (half == 1 and k == 7) else None))
        p.op("vector", "tensor_copy", out=v_b[:, :], in_=PVO[:, :],
             wait=[("pv", T + 1), ("kvd", T)], inc="ve")

        dbg(p, env, "qt_b", qt_b[:, :], T, [("qte", T + 1)])
        dbg(p, env, "kt_b", kt_b[:, :], T, [("kte", T + 1)])
        dbg(p, env, "v_b", v_b[:, :], T, [("ve", T + 1)])
        for j in range(4):
            p.op("tensor", "transpose", PT[:, 128 * j:128 * (j + 1)],
                 qt_b[:, 128 * j:128 * (j + 1)], ident_b[:, :],
                 wait=([("kte", T + 1)] if j == 0 else ()))
        for j in range(4):
            p.op("tensor", "transpose", PT[:, 512 + 128 * j:512 + 128 * (j + 1)],
                 kt_b[:, 128 * j:128 * (j + 1)], ident_b[:, :],
                 inc=("tqk" if j == 3 else None))
        p.op("vector", "tensor_copy", out=qfm[:, :], in_=PT[:, 0:512],
             wait=[("tqk", T + 1), ("ohd", T)])
        p.op("vector", "tensor_copy", out=kfm[:, :], in_=PT[:, 512:1024],
             inc="fme")
        p.op("vector", "tensor_copy", out=S_bb[:, :], in_=S_sb[:, :], inc="sbc")

        for h in range(16):
            bank = PC if h % 2 == 0 else PD
            pb = 32 * (h % 4)
            fc = 128 * (h // 4)
            hw = [("pbm", 16 * T + h - 1)] if h >= 2 else []
            if h == 0:
                hw += [("fme", T + 1), ("ve", T + 1), ("sbc", T + 1),
                       ("gA", T + 1), ("sup", T), ("pbm", 16 * T)]
            p.op("tensor", "matmul", bank[:, 0:128],
                 kfm[pb:pb + 32, fc:fc + 128], qfm[pb:pb + 32, fc:fc + 128],
                 start=True, stop=True, tile_position=(pb, 0),
                 wait=hw, inc=("scd", 1))
            p.op("vector", "tensor_tensor", P_b[h % 2][:, :], bank[:, 0:128],
                 UT_b[:, :], ALU.mult,
                 wait=[("scd", 16 * T + h + 1)], inc=("pbm", 1))
            p.op("tensor", "matmul", PVO[:, 64 * h:64 * (h + 1)], P_b[h % 2][:, :],
                 v_b[:, 64 * h:64 * (h + 1)], start=True, stop=False,
                 wait=[("pbm", 16 * T + h + 1)])
            p.op("tensor", "matmul", PVO[:, 64 * h:64 * (h + 1)],
                 qfm[pb:pb + 32, fc:fc + 128],
                 S_bb[pb:pb + 32, 64 * (h // 4):64 * (h // 4) + 64],
                 start=False, stop=True, tile_position=(pb, 0),
                 inc=("ohd" if h == 15 else None))
            p.op("tensor", "matmul", PK[pb:pb + 32, 64 * (h // 4):64 * (h // 4) + 64],
                 kt_b[:, 32 * h:32 * (h + 1)], v_b[:, 64 * h:64 * (h + 1)],
                 start=True, stop=True, tile_position=(0, pb),
                 inc=("kvd" if h == 15 else None))

        p.op("vector", "tensor_tensor", t_s[:, :], PK[:, 0:256], S_sb[:, :], ALU.add,
             wait=[("kvd", T + 1)])
        for c in range(4):
            p.op("vector", "tensor_scalar", S_sb[:, 64 * c:64 * (c + 1)],
                 t_s[:, 64 * c:64 * (c + 1)], A_end[:, c:c + 1], None, ALU.mult,
                 inc=("sup" if c == 3 else None))

        dbg(p, env, "S_sb", S_sb[:, :], T, [("sup", T + 1)])
        if env.get("debug"):
            p.op("vector", "tensor_copy", out=o_raw[:, :], in_=PVO[:, :],
                 wait=[("ohd", T + 1)], inc="oraw")
            dbg(p, env, "o_raw", o_raw[:, :], T, [("oraw", T + 1)])
        dbg(p, env, "qfm", qfm[:, :], T, [("fme", T + 1)])
        dbg(p, env, "kfm", kfm[:, :], T, [("fme", T + 1)])
        dbg(p, env, "S_bb", S_bb[:, :], T, [("sbc", T + 1)])
        for half, bank in ((0, PA), (1, PB)):
            for k in range(8):
                p.op("tensor", "matmul", bank[:, 0:512],
                     xnT[:, 128 * k:128 * (k + 1)],
                     Wg_b[:, 1024 * k + 512 * half:1024 * k + 512 * (half + 1)],
                     start=(k == 0), stop=False)
            p.op("tensor", "matmul", bank[:, 0:512], ones1_b[:, :],
                 bg_b[0:1, 512 * half:512 * (half + 1)], start=False, stop=True,
                 inc=("pgt" if half == 1 else None))

        p.op("vector", "tensor_reduce", o_mu[:, :],
             PVO[:, :].rearrange("p (h e) -> p h e", h=16), AX.X, ALU.add,
             wait=[("ohd", T + 1)])
        p.op("scalar", "activation", osq_sb[:, :], PVO[:, :], AF.Square,
             wait=[("ohd", T + 1), ("oln", T)], inc="osq")
        p.op("vector", "tensor_reduce", o_s2[:, :],
             osq_sb[:, :].rearrange("p (h e) -> p h e", h=16), AX.X, ALU.add,
             wait=[("osq", T + 1)])
        p.fence("vector")
        p.op("vector", "tensor_tensor", msq2[:, :], o_mu[:, :], o_mu[:, :], ALU.mult)
        p.op("vector", "tensor_scalar", tv2[:, :], o_s2[:, :], 1.0 / DV, None,
             ALU.mult)
        p.fence("vector")
        p.op("vector", "scalar_tensor_tensor", var2[:, :], msq2[:, :],
             -1.0 / (DV * DV), tv2[:, :], ALU.mult, ALU.add, inc="ovr")
        p.op("scalar", "activation", lnv2[:, :], var2[:, :], AF.Ln, bias=c_eps[:, :],
             wait=[("ovr", T + 1)])
        p.fence("scalar")
        p.op("scalar", "activation", rstd2[:, :], lnv2[:, :], AF.Exp, scale=-0.5,
             inc="rs2")
        p.op("vector", "scalar_tensor_tensor", nm2[:, :], o_mu[:, :], -1.0 / DV,
             rstd2[:, :], ALU.mult, ALU.mult, wait=[("rs2", T + 1)])
        p.fence("vector")
        p.op("vector", "tensor_tensor",
             o_n[:, :].rearrange("p (h e) -> p h e", h=16),
             PVO[:, :].rearrange("p (h e) -> p h e", h=16),
             rstd2[:, :].rearrange("p (h e) -> p h e", e=1).broadcast_to((128, 16, 64)),
             ALU.mult)
        p.op("vector", "tensor_tensor",
             o_n[:, :].rearrange("p (h e) -> p h e", h=16),
             o_n[:, :].rearrange("p (h e) -> p h e", h=16),
             nm2[:, :].rearrange("p (h e) -> p h e", e=1).broadcast_to((128, 16, 64)),
             ALU.add, inc="oln")

        dbg(p, env, "o_n", o_n[:, :], T, [("oln", T + 1)])
        dbg(p, env, "o_mu", o_mu[:, :], T, [("oln", T + 1)])
        dbg(p, env, "var2", var2[:, :], T, [("oln", T + 1)])
        for half, bank in ((0, PA), (1, PB)):
            p.op("scalar", "activation", eg[:, :], bank[:, 0:512], AF.Exp, scale=-1.0,
                 wait=([("pgt", T + 1), ("gfr", T)] if half == 0 else ()))
            p.op("scalar", "activation", sg[:, 512 * half:512 * (half + 1)],
                 eg[:, :], AF.Ln, bias=c_one[:, :])
        p.op("scalar", "activation", sg[:, :], sg[:, :], AF.Exp, scale=-1.0,
             inc="sg_")
        for half, bank in ((0, PA), (1, PB)):
            p.op("vector", "tensor_tensor", gate_f[:, 512 * half:512 * (half + 1)],
                 bank[:, 0:512], sg[:, 512 * half:512 * (half + 1)], ALU.mult,
                 wait=([("sg_", T + 1), ("tgo", T)] if half == 0 else ()))
        p.op("vector", "tensor_tensor", go_b[:, :], o_n[:, :], gate_f[:, :],
             ALU.mult, inc="gfr")

        dbg(p, env, "sg", sg[:, :], T, [("sg_", T + 1)])
        dbg(p, env, "gate_f", gate_f[:, :], T, [("gfr", T + 1)])
        dbg(p, env, "go_b", go_b[:, :], T, [("gfr", T + 1)])
        for j in range(8):
            p.op("tensor", "transpose", PT[:, 128 * j:128 * (j + 1)],
                 go_b[:, 128 * j:128 * (j + 1)], ident_b[:, :],
                 wait=([("gfr", T + 1)] if j == 0 else ()),
                 inc=("tgo" if j == 7 else None))
        p.op("vector", "tensor_copy", out=goT_sb[:, :], in_=PT[:, :],
             wait=[("tgo", T + 1), ("pa0", T)], inc="goe")
        for half, bank in ((0, PC), (1, PD)):
            for k in range(8):
                p.op("tensor", "matmul", bank[:, 0:512],
                     goT_sb[:, 128 * k:128 * (k + 1)],
                     Wo_b[:, 1024 * k + 512 * half:1024 * k + 512 * (half + 1)],
                     start=(k == 0), stop=(k == 7),
                     wait=([("goe", T + 1)] if (half == 0 and k == 0) else ()),
                     inc=("pa0" if (half == 1 and k == 7) else None))
        p.op("vector", "tensor_tensor", r1_sb[T % 2][:, 0:512], PC[:, 0:512],
             xt[T % 2][:, 0:512], ALU.add,
             wait=[("pa0", T + 1), ("str", 16 * (T - 1))])
        p.op("vector", "tensor_tensor", r1_sb[T % 2][:, 512:1024], PD[:, 0:512],
             xt[T % 2][:, 512:1024], ALU.add, inc="r1a")
        p.op("gpsimd", "dma_start", out=r1_d[t0:t0 + 128, :], in_=r1_sb[T % 2][:, :],
             wait=W("gpsimd", [("r1a", T + 1)]), inc="str", dma=True)

        p.op("scalar", "activation", qt2[:, :], xnf[:, :], AF.Abs,
             wait=[("xfr", T + 1), ("qst", T)])
        p.op("scalar", "activation", sgn[:, :], xnf[:, :], AF.Sign,
             wait=[("qpl", T)], inc="qsg")
        emit_quant_stair(p, env, "vector", "gpsimd", "q", xnf, sgn,
                         xq_b[T % 2], 1024, QA,
                         pool_extra_wait=[("qsg", T + 1), ("stq", 16 * (T - 1))],
                         src_wait=[("xfr", T + 1)], abs_wait=[("qsg", T + 1)])
        p.op("gpsimd", "dma_start", out=xq_d[t0:t0 + 128, :], in_=xq_b[T % 2][:, :],
             wait=[("qpl", T + 1)], inc="stq", dma=True)

        dbg(p, env, "scl", scl[:, :], T, [("qst", T + 1)])
        dbg(p, env, "a_n", a_n[:, :], T, [("qst", T + 1)])
        dbg(p, env, "stair", stair[:, :], T, [("qst", T + 1)])

    fin = [("str", 16 * NT), ("stq", 16 * NT)]
    if env.get("debug"):
        fin.append(("dbgs", p.cnt.get("dbgs", 0)))
    p.op("gpsimd", "nop", wait=fin)



FFI = 2816
ICHUNKS = [(0, 512), (512, 512), (1024, 512), (1536, 512), (2048, 512), (2560, 256)]


def emit_quant_stair(p, env, dve, pool, pref, src_signed, src_sgn, out_bf, C, Q,
                     pool_extra_wait=(), src_wait=(), qlow_st_wait=None,
                     abs_wait=()):
    bmax = Q["bmax"]
    scl, rscl, a_n, m8 = Q["scl"], Q["rscl"], Q["a"], Q["m8"]
    tA, tB, tC = Q["acc2"], Q["tmp2"], Q["t22"]
    P1 = Q["sscl"]
    stair = Q["a"]
    NB = C // 16
    p.sem(pref + "pl")
    p.sem(pref + "st")
    p.sem(pref + "ql")
    p.op(dve, "tensor_reduce", bmax[:, 0:NB],
         src_signed[:, 0:C].rearrange("p (g w) -> p g w", w=16),
         AX.X, ALU.max, apply_absolute_value=True, wait=list(src_wait))
    p.fence(dve)
    p.op(dve, "tensor_scalar", scl[:, 0:NB], bmax[:, 0:NB], 1.0 / 6.0, 1e-12,
         ALU.mult, ALU.max, wait=[(pref + "pl", p.cnt[pref + "pl"])])
    p.fence(dve)
    p.op(dve, "reciprocal", rscl[:, 0:NB], scl[:, 0:NB])
    p.fence(dve)
    p.op(dve, "tensor_tensor",
         a_n[:, 0:C].rearrange("p (g w) -> p g w", w=16),
         tC[:, 0:C].rearrange("p (g w) -> p g w", w=16),
         rscl[:, 0:NB].rearrange("p (g w) -> p g w", w=1).broadcast_to((128, NB, 16)),
         ALU.mult, wait=list(abs_wait))
    p.fence(dve)
    p.op(dve, "nop", inc=pref + "qa")
    p.op(pool, "tensor_scalar", P1[:, 0:C], a_n[:, 0:C], 2.0, 8388608.0,
         ALU.mult, ALU.add,
         wait=[(pref + "qa", p.cnt[pref + "qa"]),
               (pref + "st", qlow_st_wait if qlow_st_wait is not None
                else p.cnt[pref + "st"])])
    p.fence(pool)
    p.op(pool, "tensor_scalar", P1[:, 0:C], P1[:, 0:C], -8388608.0, 0.5,
         ALU.add, ALU.mult, inc=pref + "ql")
    p.op(dve, "tensor_single_scalar", m8[:, 0:C], a_n[:, 0:C], 1.25, ALU.is_gt)
    p.op(dve, "tensor_scalar", tA[:, 0:C], a_n[:, 0:C], 4194305.0, None, ALU.mult)
    p.op(dve, "tensor_tensor", tB[:, 0:C], tA[:, 0:C], a_n[:, 0:C], ALU.subtract)
    p.op(dve, "tensor_tensor", tC[:, 0:C], tA[:, 0:C], tB[:, 0:C], ALU.subtract)
    p.op(dve, "select", stair[:, 0:C], m8[:, 0:C], tC[:, 0:C], P1[:, 0:C],
         wait=[(pref + "ql", p.cnt[pref + "ql"])], inc=pref + "st")
    p.op(pool, "tensor_tensor",
         P1[:, 0:C].rearrange("p (g w) -> p g w", w=16),
         src_sgn[:, 0:C].rearrange("p (g w) -> p g w", w=16),
         scl[:, 0:NB].rearrange("p (g w) -> p g w", w=1).broadcast_to((128, NB, 16)),
         ALU.mult,
         wait=[(pref + "st", p.cnt[pref + "st"])] + list(pool_extra_wait))
    p.op(pool, "tensor_tensor", out_bf[:, 0:C], stair[:, 0:C], P1[:, 0:C],
         ALU.mult, inc=pref + "pl")


def emit_phase_b(p, env, io):
    wgq = p.sb("wgq", [128, 8 * FFI], BF16)
    wuq = p.sb("wuq", [128, 8 * FFI], BF16)
    wdq = p.sb("wdq", [128, 22 * 1024], BF16)

    PTB = p.ps("PTB", [128, 1024], BF16)
    PG0 = p.ps("PG0", [128, 512], F32)
    PU0 = p.ps("PU0", [128, 512], F32)
    PF1 = p.ps("PF1", [128, 512], F32)
    PF2 = p.ps("PF2", [128, 512], F32)
    PTH = p.ps("PTH", [128, 1024], BF16)
    PG1 = p.ps("PG1B", [128, 512], F32)
    PU1 = p.ps("PU1B", [128, 512], F32)

    wstg = p.sb("wstgB", [128, 1024], F32)
    qabs = p.sb("qabsB", [128, 1024], F32)
    qsgn = p.sb("qsgnB", [128, 1024], F32)
    wq_b = p.sb("wq_bB", [128, 1024], BF16)
    Q = {k: p.sb("Qb_" + k, [128, n], F32) for k, n in
         [("bmax", 64), ("scl", 64),
          ("rscl", 64), ("a", 1024), ("acc2", 1024), ("tmp2", 1024), ("t22", 1024),
          ("sscl", 1024)]}
    Q["m8"] = p.sb("Qb_m8", [128, 1024], mybir.dt.uint8)
    eg1 = p.sb("eg1B", [128, 512], F32)
    eg2 = p.sb("eg2B", [128, 512], F32)
    sgc_v = [p.sb("sgcB0", [128, 512], F32), p.sb("sgcB1", [128, 512], F32)]

    ident_b = env["ident_b"]
    c_one = env["c_one"]
    c_eps = env["c_eps"]

    jobs = []
    for c in range(22):
        jobs.append((io["w_gate"], c, 0, 1024, wgq,
                     lambda k, c=c: 2816 * k + 128 * c))
    for c in range(22):
        jobs.append((io["w_up"], c, 0, 1024, wuq,
                     lambda k, c=c: 2816 * k + 128 * c))
    for r in range(8):
        for (p0, plen) in [(0, 1024), (1024, 1024), (2048, 768)]:
            jobs.append((io["w_down"], r, p0, plen, wdq,
                         lambda k, r=r, p0=p0: (p0 // 128 + k) * 1024 + 128 * r))
    wi = 0
    for (srct, rowc, col0, ncol, dstt, colf) in jobs:
        p.op("sync", "dma_start", out=wstg[:, 0:ncol],
             in_=srct[128 * rowc:128 * (rowc + 1), col0:col0 + ncol],
             wait=[("bwq", wi)], inc="bwl", dma=True)
        p.op("scalar", "activation", Q["t22"][:, 0:ncol], wstg[:, 0:ncol],
             AF.Abs, wait=[("bwl", 16 * (wi + 1)), ("bwst", wi)])
        p.op("scalar", "activation", qsgn[:, 0:ncol], wstg[:, 0:ncol], AF.Sign,
             inc="bwsg")
        emit_quant_stair(p, env, "vector", "gpsimd", "bw", wstg, qsgn, wq_b,
                         ncol, Q, pool_extra_wait=[("bwt", p.cnt.get("bwt", 0)),
                                                   ("bwsg", wi + 1)],
                         src_wait=[("bwl", 16 * (wi + 1))],
                         abs_wait=[("bwsg", wi + 1)])
        nk = ncol // 128
        for k in range(nk):
            p.op("tensor", "transpose", PTH[:, 128 * k:128 * (k + 1)],
                 wq_b[:, 128 * k:128 * (k + 1)], ident_b[:, :],
                 wait=([("bwpl", p.cnt["bwpl"]), ("bwe", p.cnt.get("bwe", 0))]
                       if k == 0 else ()),
                 inc=("bwt" if k == nk - 1 else None))
        for k in range(nk):
            p.op("vector", "tensor_copy", out=dstt[:, colf(k):colf(k) + 128],
                 in_=PTH[:, 128 * k:128 * (k + 1)],
                 wait=([("bwt", p.cnt["bwt"])] if k == 0 else ()),
                 inc=("bwe" if k == nk - 1 else None))
        p.op("vector", "nop", wait=[("bwsg", wi + 1)], inc="bwq")
        wi += 1
    NWB = wi

    xqt = [p.sb(f"xqtB{i}", [128, 1024], BF16) for i in range(2)]
    r1t = [p.sb(f"r1tB{i}", [128, 1024], F32) for i in range(2)]
    hg = p.sb("hgB", [128, 512], F32)
    hf = p.sb("hfB", [128, 512], F32)
    hsgn_v = [qsgn[:, 0:512], qsgn[:, 512:1024]]
    hq_b = [wq_b[:, 0:512], wq_b[:, 512:1024]]
    hqT = [p.sb("hqT0B", [128, 512], BF16), p.sb("hqT1B", [128, 512], BF16)]
    x1_sb = [wstg, p.sb("x1B1", [128, 1024], F32)]
    xn1_sb = [p.sb(f"xn1B{i}", [128, 1024], BF16) for i in range(2)]
    xsqB = qabs
    ssqB = p.sb("ssqB", [128, 1], F32)
    sumxB = p.sb("sumxB", [128, 1], F32)
    msqB = p.sb("msqB", [128, 1], F32)
    tv1B = p.sb("tv1B", [128, 1], F32)
    varB = p.sb("varB", [128, 1], F32)
    lnvB = p.sb("lnvB", [128, 1], F32)
    rstdB = p.sb("rstdB", [128, 1], F32)
    nmB = p.sb("nmB", [128, 1], F32)

    first_waits = [("bwq", NWB)]
    fw = {e: list(first_waits) for e in ENGINES}

    def W(eng, extra=()):
        ws = fw[eng] + list(extra)
        fw[eng] = []
        return ws

    xq_d, r1_d, x1_d, xn1_d = io["xq"], io["r1"], io["x1"], io["xn1"]
    nch = len(ICHUNKS)

    for T in range(NT):
        t0 = T * 128
        for j in range(8):
            p.op("sync", "dma_start_transpose",
                 xqt[T % 2][:, 128 * j:128 * (j + 1)],
                 xq_d[t0:t0 + 128, 128 * j:128 * (j + 1)],
                 wait=(W("sync", [("bpg", nch * (T - 1))]) if j == 0 else ()),
                 inc="blq", dma=True)
        p.op("sync", "dma_start", out=r1t[T % 2][:, :], in_=r1_d[t0:t0 + 128, :],
             wait=[("bx1", T - 1)], inc="bld", dma=True)
        xqT = xqt[T % 2]
        dnb = [44 * T]

        def emit_gate_up(ci):
            CI = nch * T + ci
            q = CI % 2
            PG = PG0 if q == 0 else PG1
            PU = PU0 if q == 0 else PU1
            i0, ilen = ICHUNKS[ci]
            for k in range(8):
                p.op("tensor", "matmul", PG[:, 0:ilen],
                     xqT[:, 128 * k:128 * (k + 1)],
                     wgq[:, 2816 * k + i0:2816 * k + i0 + ilen],
                     start=(k == 0), stop=(k == 7),
                     wait=(([("blq", 128 * (T + 1))] if ci == 0 else [])
                           + [("bhg", CI - 1)]) if k == 0 else (),
                     inc=("bpg" if k == 7 else None))
            for k in range(8):
                p.op("tensor", "matmul", PU[:, 0:ilen],
                     xqT[:, 128 * k:128 * (k + 1)],
                     wuq[:, 2816 * k + i0:2816 * k + i0 + ilen],
                     start=(k == 0), stop=(k == 7),
                     wait=([("bhf", CI - 1)] if k == 0 else ()),
                     inc=("bpu" if k == 7 else None))

        emit_gate_up(0)
        for ci, (i0, ilen) in enumerate(ICHUNKS):
            CI = nch * T + ci
            q = CI % 2
            PG = PG0 if q == 0 else PG1
            PU = PU0 if q == 0 else PU1
            TB = PTH if q == 0 else PTB
            if ci + 1 < nch:
                emit_gate_up(ci + 1)
            p.op("scalar", "activation", eg1[:, 0:ilen], PG[:, 0:ilen],
                 AF.Exp, scale=-1.0,
                 wait=W("scalar", [("bpg", CI + 1), ("bhg", CI - 1)]))
            p.op("scalar", "activation", eg2[:, 0:ilen], eg1[:, 0:ilen],
                 AF.Ln, bias=c_one[:, :])
            p.op("scalar", "activation", sgc_v[q][:, 0:ilen], eg2[:, 0:ilen],
                 AF.Exp, scale=-1.0, inc="bsg")
            p.op("vector", "scalar_tensor_tensor", hg[:, 0:ilen], PG[:, 0:ilen],
                 1.0, sgc_v[q][:, 0:ilen], ALU.mult, ALU.mult,
                 wait=[("bsg", CI + 1)], inc="bhg")
            p.op("vector", "tensor_tensor", hf[:, 0:ilen], PU[:, 0:ilen],
                 hg[:, 0:ilen], ALU.mult, wait=[("bpu", CI + 1), ("bhs", CI)],
                 inc="bhf")
            p.op("scalar", "activation", Q["t22"][:, 0:ilen], hf[:, 0:ilen],
                 AF.Abs, wait=[("bhf", CI + 1), ("bhst", CI)])
            p.op("scalar", "activation", hsgn_v[q][:, 0:ilen], hf[:, 0:ilen],
                 AF.Sign, inc="bhs")
            Qq = dict(Q)
            Qq["sscl"] = Q["sscl"][:, 512 * q:512 * (q + 1)]
            emit_quant_stair(p, env, "vector", "gpsimd", "bh", hf,
                             hsgn_v[q], hq_b[q], ilen, Qq,
                             pool_extra_wait=[("bhT", CI - 1), ("bhs", CI + 1)],
                             src_wait=[("bhf", CI + 1)],
                             qlow_st_wait=CI, abs_wait=[("bhs", CI + 1)])
            nk = ilen // 128
            for k in range(nk):
                p.op("tensor", "transpose", TB[:, 128 * k:128 * (k + 1)],
                     hq_b[q][:, 128 * k:128 * (k + 1)], ident_b[:, :],
                     wait=([("bhpl", CI + 1), ("bhTe", CI - 1)]
                           if k == 0 else ()),
                     inc=("bhT" if k == nk - 1 else None))
            p.op("vector", "tensor_copy", out=hqT[q][:, 0:ilen],
                 in_=TB[:, 0:ilen],
                 wait=[("bhT", CI + 1),
                       ("bdn", dnb[max(0, ci - 1)])], inc="bhTe")
            for k in range(nk):
                ic = (i0 // 128) + k
                for half in range(2):
                    p.op("tensor", "matmul",
                         (PF1 if half == 0 else PF2)[:, 0:512],
                         hqT[q][:, 128 * k:128 * (k + 1)],
                         wdq[:, 1024 * ic + 512 * half:1024 * ic + 512 * (half + 1)],
                         start=(ci == 0 and k == 0), stop=(ci == 5 and k == nk - 1),
                         wait=([("bhTe", CI + 1), ("bx1", T)]
                               if (ci == 0 and k == 0 and half == 0) else
                               ([("bhTe", CI + 1)]
                                if (k == 0 and half == 0) else ())),
                         inc=("bdn", 1))
            dnb.append(dnb[-1] + 2 * nk)
        p.op("vector", "tensor_tensor", x1_sb[T % 2][:, 0:512], PF1[:, 0:512],
             r1t[T % 2][:, 0:512], ALU.add,
             wait=[("bdn", 44 * (T + 1)), ("bld", 16 * (T + 1)),
                   ("bso", 32 * (T - 1))])
        p.op("vector", "tensor_tensor", x1_sb[T % 2][:, 512:1024], PF2[:, 0:512],
             r1t[T % 2][:, 512:1024], ALU.add, inc="bx1")
        p.op("scalar", "activation", xsqB[:, :], x1_sb[T % 2][:, :], AF.Square,
             accum_out=ssqB[:, :], wait=[("bx1", T + 1)], inc="blnq")
        p.op("vector", "tensor_reduce", sumxB[:, :], x1_sb[T % 2][:, :], AX.X,
             ALU.add, wait=[("bx1", T + 1)])
        p.fence("vector")
        p.op("vector", "tensor_tensor", msqB[:, :], sumxB[:, :], sumxB[:, :],
             ALU.mult)
        p.op("vector", "tensor_scalar", tv1B[:, :], ssqB[:, :], 1.0 / D, None,
             ALU.mult, wait=[("blnq", T + 1)])
        p.fence("vector")
        p.op("vector", "scalar_tensor_tensor", varB[:, :], msqB[:, :],
             -1.0 / (D * D), tv1B[:, :], ALU.mult, ALU.add, inc="blns")
        p.op("scalar", "activation", lnvB[:, :], varB[:, :], AF.Ln,
             bias=c_eps[:, :], wait=[("blns", T + 1)])
        p.fence("scalar")
        p.op("scalar", "activation", rstdB[:, :], lnvB[:, :], AF.Exp, scale=-0.5,
             inc="brsd")
        p.op("vector", "scalar_tensor_tensor", nmB[:, :], sumxB[:, :], -1.0 / D,
             rstdB[:, :], ALU.mult, ALU.mult, wait=[("brsd", T + 1)], inc="bnmr")
        p.op("scalar", "activation", xn1_sb[T % 2][:, :], x1_sb[T % 2][:, :],
             AF.Identity, bias=nmB[:, :], scale=rstdB[:, :],
             wait=[("bnmr", T + 1), ("bso", 32 * (T - 1))], inc="bxn1")
        p.op("gpsimd", "dma_start", out=x1_d[t0:t0 + 128, :],
             in_=x1_sb[T % 2][:, :], wait=W("gpsimd", [("bxn1", T + 1)]),
             inc="bso", dma=True)
        p.op("gpsimd", "dma_start", out=xn1_d[t0:t0 + 128, :],
             in_=xn1_sb[T % 2][:, :], inc="bso", dma=True)
        if env.get("debug"):
            dbg(p, env, "b_x1", x1_sb[T % 2][:, :], T, [("bx1", T + 1)])
            dbg(p, env, "b_xn1", xn1_sb[T % 2][:, :], T, [("bxn1", T + 1)])

    p.op("gpsimd", "nop", wait=[("bso", 32 * NT)])



def emit_phase_ce(p, env, io):
    WgT_b = p.sb("WgT_b", [128, 8 * 1024], BF16)
    WvT_b = p.sb("WvT_b", [128, 8 * 1024], BF16)
    Wb_b = p.sb("Wb_b", [128, 8 * 1024], BF16)
    Wd1_b = p.sb("Wd1_b", [128, 8 * 2048], BF16)
    Wd2_b = p.sb("Wd2_b", [128, 16 * 1024], BF16)
    bg_r = p.sb("bg_r", [1, 1024], BF16)
    bv_r = p.sb("bv_r", [1, 1024], BF16)
    bb_r = p.sb("bb_r", [1, 1024], BF16)
    bd1_r = p.sb("bd1_r", [1, 2048], BF16)
    bd2_r = p.sb("bd2_r", [1, 8 * 1024], BF16)

    PT = p.ps("PTC", [128, 1024], BF16)
    PG1 = p.ps("PG1C", [128, 512], F32)
    PG2 = p.ps("PG2C", [128, 512], F32)
    PW1 = p.ps("PW1C", [128, 512], F32)
    PW2 = p.ps("PW2C", [128, 512], F32)
    PH = p.ps("PHC", [128, 512], F32)
    PS6 = p.ps("PS6C", [128, 512], F32)

    ident_b = env["ident_b"]
    UT_f = env["UT_f"]
    LT_f = env["LT_f"]
    onesc_f = env["onesc_f"]
    ones1_b = env["ones1_b"]

    wstg = [p.sb(f"wstgC{i}", [128, 1024], F32) for i in range(2)]
    wcst = [p.sb(f"wcstC{i}", [128, 1024], BF16) for i in range(2)]
    wi = env.get("cw_i", 0)
    ti = 0

    def prep_t(src, dst):
        nonlocal wi, ti
        for c in range(8):
            p.op("sync", "dma_start", out=wstg[wi % 2][:, :],
                 in_=src[128 * c:128 * (c + 1), :],
                 wait=[("cwc", wi - 1)], inc="cwl", dma=True)
            p.op("vector", "tensor_copy", out=wcst[ti % 2][:, :],
                 in_=wstg[wi % 2][:, :],
                 wait=[("cwl", 16 * (wi + 1)), ("cwt", ti - 1)], inc="cwc")
            for k in range(8):
                p.op("tensor", "transpose", PT[:, 128 * k:128 * (k + 1)],
                     wcst[ti % 2][:, 128 * k:128 * (k + 1)], ident_b[:, :],
                     wait=([("cwc", wi + 1), ("cwe", ti)] if k == 0 else ()),
                     inc=("cwt" if k == 7 else None))
            for k in range(8):
                p.op("vector", "tensor_copy",
                     out=dst[:, 1024 * k + 128 * c:1024 * k + 128 * (c + 1)],
                     in_=PT[:, 128 * k:128 * (k + 1)],
                     wait=([("cwt", ti + 1)] if k == 0 else ()),
                     inc=("cwe" if k == 7 else None))
            wi += 1
            ti += 1

    def prep_n(src, rows, cols, dst, colf):
        nonlocal wi
        nchunk = rows // 128
        for c in range(nchunk):
            p.op("sync", "dma_start", out=wstg[wi % 2][0:128, 0:cols],
                 in_=src[128 * c:128 * (c + 1), :],
                 wait=[("cwc", wi - 1)], inc="cwl", dma=True)
            p.op("vector", "tensor_copy", out=dst[:, colf(c):colf(c) + cols],
                 in_=wstg[wi % 2][0:128, 0:cols],
                 wait=[("cwl", 16 * (wi + 1))], inc="cwc")
            wi += 1

    def prep_row(src_1d, n, dst):
        nonlocal wi
        src2 = src_1d.rearrange("(o d) -> o d", o=1)
        for c0 in range(0, n, 1024):
            w = min(1024, n - c0)
            p.op("sync", "dma_start", out=wstg[wi % 2][0:1, 0:w],
                 in_=src2[0:1, c0:c0 + w],
                 wait=[("cwc", wi - 1)], inc="cwl", dma=True)
            p.op("vector", "tensor_copy", out=dst[0:1, c0:c0 + w],
                 in_=wstg[wi % 2][0:1, 0:w],
                 wait=[("cwl", 16 * (wi + 1))], inc="cwc")
            wi += 1

    prep_t(io["L1Wg"], WgT_b)
    prep_t(io["L1Wv"], WvT_b)
    prep_n(io["W_base"], 1024, 1024, Wb_b, lambda c: 1024 * c)
    for e in range(8):
        prep_n(io["Wd1"][e], 1024, 256, Wd1_b, lambda c, e=e: 2048 * c + 256 * e)
    for e in range(8):
        prep_n(io["Wd2"][e], 256, 1024, Wd2_b, lambda c, e=e: (2 * e + c) * 1024)
    prep_row(io["L1bg"], 1024, bg_r)
    prep_row(io["L1bv"], 1024, bv_r)
    prep_row(io["b_base"], 1024, bb_r)
    prep_row(io["bd1"].rearrange("e h -> (e h)"), 2048, bd1_r)
    prep_row(io["bd2"].rearrange("e d -> (e d)"), 8192, bd2_r)
    NWC = wi
    env["cw_i"] = wi

    xn1t = [p.sb(f"xn1tC{i}", [128, 1024], BF16) for i in range(2)]
    xnT = p.sb("xnTC", [128, 1024], BF16)
    gsq = p.sb("gsqC", [128, 1024], F32)
    n2 = p.sb("n2C", [128, 16], F32)
    lnn = p.sb("lnnC", [128, 16], F32)
    rn = p.sb("rnC", [128, 16], F32)
    gh_b = [p.sb(f"gh_bC{i}", [128, 1024], BF16) for i in range(2)]
    ghT = p.sb("ghTC", [128, 1024], BF16)
    v_b = p.sb("v_bC", [128, 1024], BF16)
    base_sb = p.sb("baseC", [128, 1024], F32)
    wmul = p.sb("wmulC", [128, 1024], F32)
    rs16 = p.sb("rs16C", [128, 16], F32)
    dnm = p.sb("dnmC", [128, 16], F32)
    recd = p.sb("recdC", [128, 16], F32)
    denom_all = p.sb("denom_allC", [128, 512], F32)
    he_b = p.sb("he_bC", [128, 512], BF16)
    heT = p.sb("heTC", [128, 512], BF16)
    dsub = p.sb("dsubC", [128, 1024], F32)
    dsq = p.sb("dsqC", [128, 1024], F32)
    lcol8 = p.sb("lcol8C", [128, 8], F32)
    loss_acc = p.sb("loss_accC", [128, 8], F32)
    lrow = p.sb("lrowC", [1, 8], F32)
    lmin = p.sb("lminC", [1, 1], F32)
    d8 = p.sb("d8C", [1, 8], F32)
    s8 = p.sb("s8C", [1, 8], F32)
    ohrow = p.sb("ohrowC", [1, 8], F32)
    prods = [p.sb(f"prodC{i}", [1, 1], F32) for i in range(2)]
    srow = p.sb("srowC", [1, 16], F32)

    p.op("vector", "memset", gh_b[0][:, :], 0.0)
    p.op("vector", "memset", gh_b[1][:, :], 0.0)
    p.op("vector", "memset", loss_acc[:, :], 0.0, inc="cms0")

    first_waits = [("cwc", NWC), ("cwe", 16), ("cms0", 1)]
    fw = {e: list(first_waits) for e in ENGINES}

    def W(eng, extra=()):
        ws = fw[eng] + list(extra)
        fw[eng] = []
        return ws

    xn1_d = io["xn1"]
    for T in range(NT):
        t0 = T * 128
        p.op("sync", "dma_start", out=xn1t[T % 2][:, :], in_=xn1_d[t0:t0 + 128, :],
             wait=W("sync", [("clsq", 8 * (T - 1))]), inc="cld2", dma=True)
        for j in range(8):
            p.op("tensor", "transpose", PT[:, 128 * j:128 * (j + 1)],
                 xn1t[T % 2][:, 128 * j:128 * (j + 1)], ident_b[:, :],
                 wait=(W("tensor", [("cld2", 16 * (T + 1)), ("cheTe", 4 * T)])
                       if j == 0 else ()),
                 inc=("cxT" if j == 7 else None))
        p.op("vector", "tensor_copy", out=xnT[:, :], in_=PT[:, :],
             wait=W("vector", [("cxT", T + 1), ("cph", 4 * T)]), inc="cxe")
        p.op("gpsimd", "dma_start", out=io["xn1T_d"][t0:t0 + 128, :], in_=xnT[:, :],
             wait=W("gpsimd", [("cxe", T + 1)]), inc="cso", dma=True)
        for half, bank in ((0, PG1), (1, PG2)):
            for k in range(8):
                p.op("tensor", "matmul", bank[:, 0:512],
                     xnT[:, 128 * k:128 * (k + 1)],
                     WgT_b[:, 1024 * k + 512 * half:1024 * k + 512 * (half + 1)],
                     start=(k == 0), stop=False,
                     wait=([("cxe", T + 1), ("cdsu", 16 * T)]
                           if (half == 0 and k == 0) else ()))
            p.op("tensor", "matmul", bank[:, 0:512], ones1_b[:, :],
                 bg_r[0:1, 512 * half:512 * (half + 1)], start=False, stop=True,
                 inc=("cpg" if half == 1 else None))
        p.op("scalar", "activation", gsq[:, 0:512], PG1[:, 0:512], AF.Square,
             wait=W("scalar", [("cpg", T + 1), ("cn2", T)]))
        p.op("scalar", "activation", gsq[:, 512:1024], PG2[:, 0:512], AF.Square,
             inc="cgsq")
        p.op("vector", "tensor_reduce", n2[:, :],
             gsq[:, :].rearrange("p (h e) -> p h e", h=16), AX.X, ALU.add,
             wait=[("cgsq", T + 1)], inc="cn2")
        p.op("scalar", "activation", lnn[:, :], n2[:, :], AF.Ln,
             wait=[("cn2", T + 1), ("cghb", T)])
        p.fence("scalar")
        p.op("scalar", "activation", rn[:, :], lnn[:, :], AF.Exp, scale=-0.5,
             inc="crn")
        p.op("vector", "tensor_tensor",
             gh_b[T % 2][:, 0:512].rearrange("p (h e) -> p h e", e=64),
             PG1[:, 0:512].rearrange("p (h e) -> p h e", e=64),
             rn[:, 0:8].rearrange("p (h e) -> p h e", e=1).broadcast_to((128, 8, 64)),
             ALU.mult, wait=[("crn", T + 1), ("cpw", T), ("cghTt", T)])
        p.op("vector", "tensor_tensor",
             gh_b[T % 2][:, 512:1024].rearrange("p (h e) -> p h e", e=64),
             PG2[:, 0:512].rearrange("p (h e) -> p h e", e=64),
             rn[:, 8:16].rearrange("p (h e) -> p h e", e=1).broadcast_to((128, 8, 64)),
             ALU.mult, inc="cghb")
        for j in range(8):
            p.op("tensor", "transpose", PT[:, 128 * j:128 * (j + 1)],
                 gh_b[T % 2][:, 128 * j:128 * (j + 1)], ident_b[:, :],
                 wait=([("cghb", T + 1)] if j == 0 else ()),
                 inc=("cghTt" if j == 7 else None))
        p.op("vector", "tensor_copy", out=ghT[:, :], in_=PT[:, :],
             wait=[("cghTt", T + 1), ("cghs", T)], inc="cghT")
        p.op("gpsimd", "dma_start", out=io["gT_d"][t0:t0 + 128, :], in_=ghT[:, :],
             wait=[("cghT", T + 1)], inc="cghs", dma=True)
        for half, bank in ((0, PW1), (1, PW2)):
            p.op("tensor", "matmul", bank[:, 0:512], env["LT_b"][:, :],
                 gh_b[(T - 1) % 2][:, 512 * half:512 * (half + 1)],
                 start=True, stop=False,
                 wait=([("cghb", T + 1), ("crsm", T)] if half == 0 else ()))
            p.op("tensor", "matmul", bank[:, 0:512], env["UT_b"][:, :],
                 gh_b[T % 2][:, 512 * half:512 * (half + 1)],
                 start=False, stop=True, inc=("cpw" if half == 1 else None))
        p.op("vector", "tensor_tensor", wmul[:, 0:512], PW1[:, 0:512],
             gh_b[T % 2][:, 0:512], ALU.mult, wait=[("cpw", T + 1)])
        p.op("vector", "tensor_tensor", wmul[:, 512:1024], PW2[:, 0:512],
             gh_b[T % 2][:, 512:1024], ALU.mult, inc="crsm")
        p.op("vector", "tensor_reduce", rs16[:, :],
             wmul[:, :].rearrange("p (h e) -> p h e", h=16), AX.X, ALU.add)
        p.fence("vector")
        p.op("vector", "tensor_scalar", dnm[:, :], rs16[:, :], 1.0 / 8192.0, 1.0,
             ALU.mult, ALU.add)
        p.fence("vector")
        p.op("vector", "tensor_copy", out=denom_all[:, 16 * T:16 * (T + 1)],
             in_=dnm[:, :])
        p.op("vector", "reciprocal", recd[:, :], dnm[:, :], inc="crecd")
        p.op("tensor", "matmul", PS6[0:1, 0:16], onesc_f[:, :], recd[:, :],
             start=(T == 0), stop=(T == NT - 1),
             wait=[("crecd", T + 1)], inc="csinv")
        for half, bank in ((0, PG1), (1, PG2)):
            for k in range(8):
                p.op("tensor", "matmul", bank[:, 0:512],
                     xnT[:, 128 * k:128 * (k + 1)],
                     WvT_b[:, 1024 * k + 512 * half:1024 * k + 512 * (half + 1)],
                     start=(k == 0), stop=False,
                     wait=([("cgsq", T + 1)] if (half == 0 and k == 0) else ()))
            p.op("tensor", "matmul", bank[:, 0:512], ones1_b[:, :],
                 bv_r[0:1, 512 * half:512 * (half + 1)], start=False, stop=True,
                 inc=("cpv" if half == 1 else None))
        p.op("vector", "tensor_copy", out=v_b[:, 0:512], in_=PG1[:, 0:512],
             wait=[("cpv", T + 1), ("cvs", T)])
        p.op("vector", "tensor_copy", out=v_b[:, 512:1024], in_=PG2[:, 0:512],
             inc="cvev")
        p.op("gpsimd", "dma_start", out=io["v_d"][t0:t0 + 128, :], in_=v_b[:, :],
             wait=[("cvev", T + 1)], inc="cvs", dma=True)
        for half, bank in ((0, PG1), (1, PG2)):
            for k in range(8):
                p.op("tensor", "matmul", bank[:, 0:512],
                     xnT[:, 128 * k:128 * (k + 1)],
                     Wb_b[:, 1024 * k + 512 * half:1024 * k + 512 * (half + 1)],
                     start=(k == 0), stop=False,
                     wait=([("cvev", T + 1)] if (half == 0 and k == 0) else ()))
            p.op("tensor", "matmul", bank[:, 0:512], ones1_b[:, :],
                 bb_r[0:1, 512 * half:512 * (half + 1)], start=False, stop=True,
                 inc=("cpb" if half == 1 else None))
        p.op("vector", "tensor_copy", out=base_sb[:, 0:512], in_=PG1[:, 0:512],
             wait=[("cpb", T + 1), ("cbs", T)])
        p.op("vector", "tensor_copy", out=base_sb[:, 512:1024], in_=PG2[:, 0:512],
             inc="cbase")
        p.op("gpsimd", "dma_start", out=io["base_d"][t0:t0 + 128, :],
             in_=base_sb[:, :], wait=[("cbase", T + 1)], inc="cbs", dma=True)
        for j in range(4):
            PJ = 4 * T + j
            for k in range(8):
                p.op("tensor", "matmul", PH[:, 0:512],
                     xnT[:, 128 * k:128 * (k + 1)],
                     Wd1_b[:, 2048 * k + 512 * j:2048 * k + 512 * (j + 1)],
                     start=(k == 0), stop=False,
                     wait=([("cheb", PJ)] if k == 0 else ()))
            p.op("tensor", "matmul", PH[:, 0:512], ones1_b[:, :],
                 bd1_r[0:1, 512 * j:512 * (j + 1)], start=False, stop=True,
                 inc=("cph", 1))
            p.op("scalar", "activation", he_b[:, :], PH[:, 0:512], AF.Relu,
                 wait=[("cph", PJ + 1), ("cheT", PJ)], inc=("cheb", 1))
            for c in range(4):
                p.op("tensor", "transpose", PT[:, 128 * c:128 * (c + 1)],
                     he_b[:, 128 * c:128 * (c + 1)], ident_b[:, :],
                     wait=([("cheb", PJ + 1), ("cheTe", PJ), ("cghT", T + 1)]
                           if c == 0 else ()),
                     inc=("cheT", 1) if c == 3 else None)
            p.op("vector", "tensor_copy", out=heT[:, :], in_=PT[:, 0:512],
                 wait=[("cheT", PJ + 1), ("crec", 2 * PJ)],
                 inc=("cheTe", 1))
            for i in range(2):
                e = 2 * j + i
                for half, bank in ((0, PG1), (1, PG2)):
                    for c in range(2):
                        p.op("tensor", "matmul", bank[:, 0:512],
                             heT[:, 256 * i + 128 * c:256 * i + 128 * (c + 1)],
                             Wd2_b[:, (2 * e + c) * 1024 + 512 * half:
                                   (2 * e + c) * 1024 + 512 * (half + 1)],
                             start=(c == 0), stop=False,
                             wait=([("cheTe", PJ + 1), ("cdsu", 2 * (8 * T + e))]
                                   + ([("cbase", T + 1)] if e == 0 else [])
                                   if (half == 0 and c == 0) else ()))
                    p.op("tensor", "matmul", bank[:, 0:512], ones1_b[:, :],
                         bd2_r[0:1, 1024 * e + 512 * half:
                               1024 * e + 512 * (half + 1)],
                         start=False, stop=True,
                         inc=("crec", 1) if half == 1 else None)
                p.op("vector", "tensor_tensor", dsub[:, 0:512], PG1[:, 0:512],
                     xn1t[T % 2][:, 0:512], ALU.subtract,
                     wait=[("crec", 8 * T + e + 1), ("clsq", 8 * T + e)],
                     inc=("cdsu", 1))
                p.op("vector", "tensor_tensor", dsub[:, 512:1024], PG2[:, 0:512],
                     xn1t[T % 2][:, 512:1024], ALU.subtract, inc=("cdsu", 1))
                p.op("scalar", "activation", dsq[:, :], dsub[:, :], AF.Square,
                     accum_out=lcol8[:, e:e + 1],
                     wait=[("cdsu", 2 * (8 * T + e + 1))], inc=("clsq", 1))
        p.op("vector", "tensor_tensor", loss_acc[:, :], loss_acc[:, :],
             lcol8[:, :], ALU.add, wait=[("clsq", 8 * (T + 1))], inc="clac")

    p.op("tensor", "matmul", PS6[0:1, 16:24], onesc_f[:, :], loss_acc[:, :],
         start=True, stop=True,
         wait=[("clac", NT), ("csinv", NT)], inc="cfin")
    p.op("vector", "tensor_copy", out=srow[:, :],
         in_=PS6[0:1, 0:16], wait=[("cfin", 1)])
    p.op("vector", "tensor_copy", out=lrow[:, :], in_=PS6[0:1, 16:24])
    p.fence("vector")
    p.op("vector", "tensor_reduce", lmin[:, :], lrow[:, :], AX.X, ALU.min)
    p.fence("vector")
    p.op("vector", "tensor_scalar", d8[:, :], lrow[:, :], lmin[:, :], None,
         ALU.subtract, inc="cd8")
    p.op("scalar", "activation", s8[:, :], d8[:, :], AF.Sign,
         wait=[("cd8", 1)], inc="cs8")
    p.op("vector", "memset", prods[0][:, :], 1.0, wait=[("cs8", 1)])
    p.fence("vector")
    for i in range(8):
        a, b = prods[i % 2], prods[(i + 1) % 2]
        p.op("vector", "tensor_tensor", b[:, :], a[:, :], s8[:, i:i + 1], ALU.mult)
        p.fence("vector")
        p.op("vector", "tensor_tensor", ohrow[:, i:i + 1], a[:, :], b[:, :],
             ALU.subtract)
        p.fence("vector")
    p.op("vector", "nop", inc="coh")
    p.op("gpsimd", "dma_start", out=io["sinv_d"], in_=srow[:, :],
         wait=[("coh", 1)], inc="cso", dma=True)
    p.op("gpsimd", "dma_start", out=io["oh_d"], in_=ohrow[:, :], inc="cso", dma=True)
    p.op("gpsimd", "dma_start",
         out=io["denom_d"].rearrange("(a p) h -> p a h", p=128),
         in_=denom_all[:, :].rearrange("p (a h) -> p a h", h=16),
         inc="cso", dma=True)
    p.op("gpsimd", "nop",
         wait=[("cso", p.cnt["cso"]), ("cghs", 16 * NT), ("cvs", 16 * NT),
               ("cbs", 16 * NT)])



def emit_phase_d(p, env, io):
    WoT_b = p.sb("WoT_bD", [128, 8 * 1024], BF16)
    bo_r = p.sb("bo_rD", [1, 1024], BF16)

    PTD = p.ps("PTD", [128, 1024], BF16)
    PA1 = p.ps("PA1D", [128, 512], F32)
    PA2 = p.ps("PA2D", [128, 512], F32)
    PSC = p.ps("PSCD", [128, 1024], F32)
    PO = p.ps("POD", [128, 1024], F32)
    PR = p.ps("PRD", [128, 512], F32)

    ident_b = env["ident_b"]
    ones1_b = env["ones1_b"]
    ones1_f = env["ones1_f"]
    zeroreg = p.nc.gpsimd.alloc_register("d_zeroreg")
    p.op("gpsimd", "reg_mov", zeroreg, 0.0)

    wstg = [p.sb(f"wstgD{i}", [128, 1024], F32) for i in range(2)]
    wcst = [p.sb(f"wcstD{i}", [128, 1024], BF16) for i in range(2)]
    wi = 0
    for c in range(8):
        p.op("sync", "dma_start", out=wstg[wi % 2][:, :],
             in_=io["L1Wo"][128 * c:128 * (c + 1), :],
             wait=[("dwc", wi - 1)], inc="dwl", dma=True)
        p.op("vector", "tensor_copy", out=wcst[wi % 2][:, :],
             in_=wstg[wi % 2][:, :],
             wait=[("dwl", 16 * (wi + 1)), ("dwt", wi - 1)], inc="dwc")
        for k in range(8):
            p.op("tensor", "transpose", PTD[:, 128 * k:128 * (k + 1)],
                 wcst[wi % 2][:, 128 * k:128 * (k + 1)], ident_b[:, :],
                 wait=([("dwc", wi + 1), ("dwe", wi)] if k == 0 else ()),
                 inc=("dwt" if k == 7 else None))
        for k in range(8):
            p.op("vector", "tensor_copy",
                 out=WoT_b[:, 1024 * k + 128 * c:1024 * k + 128 * (c + 1)],
                 in_=PTD[:, 128 * k:128 * (k + 1)],
                 wait=([("dwt", wi + 1)] if k == 0 else ()),
                 inc=("dwe" if k == 7 else None))
        wi += 1
    p.op("sync", "dma_start", out=wstg[wi % 2][0:1, 0:1024],
         in_=io["L1bo"].rearrange("(o d) -> o d", o=1),
         wait=[("dwc", wi - 1)], inc="dwl", dma=True)
    p.op("vector", "tensor_copy", out=bo_r[0:1, :], in_=wstg[wi % 2][0:1, 0:1024],
         wait=[("dwl", 16 * (wi + 1))], inc="dwc")
    wi += 1
    NWD = wi

    srow = p.sb("srowD", [1, 16], F32)
    rsrow = p.sb("rsrowD", [1, 16], F32)
    rsrep = p.sb("rsrepD", [128, 16], F32)
    p.op("sync", "dma_start", out=srow[:, :], in_=io["sinv_d"], inc="dsl", dma=True)
    p.op("vector", "reciprocal", rsrow[:, :], srow[:, :],
         wait=[("dsl", 16)], inc="drs")
    p.op("tensor", "matmul", PR[0:128, 0:16], ones1_f[:, :], rsrow[:, :],
         start=True, stop=True, wait=[("drs", 1)], inc="drm")
    p.op("vector", "tensor_copy", out=rsrep[:, :], in_=PR[0:128, 0:16],
         wait=[("drm", 1)], inc="drr")

    ghTt = [p.sb(f"ghTtD{i}", [128, 1024], BF16) for i in range(3)]
    vt = [p.sb(f"vtD{i}", [128, 1024], BF16) for i in range(3)]
    x1t = [p.sb(f"x1tD{i}", [128, 1024], F32) for i in range(2)]
    bst = [p.sb(f"bstD{i}", [128, 1024], F32) for i in range(2)]
    dnt = [p.sb(f"dntD{i}", [128, 16], F32) for i in range(2)]
    recd = p.sb("recdD", [128, 16], F32)
    hf = p.sb("hfD", [128, 16], F32)
    u2f = p.sb("u2fD", [128, 1024], F32)
    tp = p.sb("tpD", [128, 1024], F32)
    pp = p.sb("ppD", [128, 1024], F32)
    ppm = p.sb("ppmD", [128, 1024], BF16)
    pTg = p.sb("pTgD", [128, 1024], BF16)
    rs16 = p.sb("rs16D", [128, 16], F32)
    tmp4 = p.sb("tmp4D", [128, 4], F32)
    rnrm = p.sb("rnrmD", [128, 16], F32)
    onb = p.sb("onbD", [128, 1024], BF16)
    onT = p.sb("onTD", [128, 1024], BF16)
    op2 = [p.sb(f"op2D{i}", [128, 1024], F32) for i in range(2)]
    UT_b = env["UT_b"]
    LT_b = env["LT_b"]
    nw0 = env["nw0"]
    c_one = env["c_one"]

    p.op("vector", "memset", ghTt[2][:, :], 0.0)
    p.op("vector", "memset", vt[2][:, :], 0.0, inc="dmz")

    first_waits = [("dwc", NWD), ("dwe", 8), ("drr", 1), ("dmz", 1)]
    fw = {e: list(first_waits) for e in ENGINES}

    def W(eng, extra=()):
        ws = fw[eng] + list(extra)
        fw[eng] = []
        return ws

    def emit_scores(Ts, g):
        GIs = 4 * Ts + g
        for cc in range(4):
            h = 4 * g + SLOT_HD[cc]
            j = h // 2
            pb = 64 * (h % 2)
            sw = []
            if cc == 0:
                sw = [("du2", GIs)]
                if g == 0:
                    sw = sw + [("dld", 80 * Ts + 32)]
                if Ts == 0 and g == 0:
                    sw = W("tensor") + sw
            p.op("tensor", "matmul", PSC[:, 256 * cc:256 * cc + 128],
                 ghTt[Ts % 3][pb:pb + 64, 128 * j:128 * (j + 1)],
                 ghTt[(Ts - 1) % 3][pb:pb + 64, 128 * j:128 * (j + 1)],
                 start=True, stop=True, tile_position=(pb, 0), wait=sw)
            p.op("tensor", "matmul", PSC[:, 256 * cc + 128:256 * cc + 256],
                 ghTt[Ts % 3][pb:pb + 64, 128 * j:128 * (j + 1)],
                 ghTt[Ts % 3][pb:pb + 64, 128 * j:128 * (j + 1)],
                 start=True, stop=True, tile_position=(pb, 0),
                 inc=("dsc", 1) if cc == 3 else None)

    for T in range(NT):
        t0 = T * 128
        p.op("sync", "dma_start", out=ghTt[T % 3][:, :],
             in_=io["gT_d"][t0:t0 + 128, :],
             wait=W("sync", [("dsc", 4 * (T - 1)), ("dpv", 4 * (T - 1))]),
             inc="dld", dma=True)
        p.op("sync", "dma_start", out=vt[T % 3][:, :],
             in_=io["v_d"][t0:t0 + 128, :], inc="dld", dma=True)
        p.op("sync", "dma_start", out=x1t[T % 2][:, :],
             in_=io["x1"][t0:t0 + 128, :],
             wait=[("dop", T - 1)], inc="dld", dma=True)
        p.op("sync", "dma_start", out=bst[T % 2][:, :],
             in_=io["base_d"][t0:t0 + 128, :], inc="dld", dma=True)
        p.op("sync", "dma_start", out=dnt[T % 2][:, :],
             in_=io["denom_d"][t0:t0 + 128, :],
             wait=[("dhf", T - 1)], inc="dld", dma=True)
        p.op("vector", "reciprocal", recd[:, :], dnt[T % 2][:, :],
             wait=W("vector", [("dld", 80 * (T + 1)), ("dhf", T)]))
        p.fence("vector")
        p.op("vector", "scalar_tensor_tensor", hf[:, :], recd[:, :], 0.5,
             rsrep[:, :], ALU.mult, ALU.mult, inc="dhf")
        if T == 0:
            emit_scores(0, 0)
        for g in range(4):
            GI = 4 * T + g
            if g < 3:
                emit_scores(T, g + 1)
            elif T + 1 < NT:
                emit_scores(T + 1, 0)
            for lo in range(2):
                p.op("vector", "scalar_tensor_tensor",
                     u2f[:, 512 * lo:512 * (lo + 1)
                         ].rearrange("p (g w) -> p g w", w=256),
                     PSC[:, 512 * lo:512 * (lo + 1)
                         ].rearrange("p (g w) -> p g w", w=256),
                     1.0,
                     hf[:, 4 * g:4 * g + 4].rearrange("p (k two) -> p k two",
                                                      two=2)[:, :, lo:lo + 1
                                                             ].broadcast_to(
                                                                 (128, 2, 256)),
                     ALU.add, ALU.mult,
                     wait=([("dsc", GI + 1), ("dfm", GI)]
                           + ([("dhf", T + 1)] if g == 0 else [])
                           if lo == 0 else ()),
                     inc=("du2", 1) if lo == 1 else None)
            p.op("scalar", "activation", tp[:, :], u2f[:, :], AF.Identity,
                 scale=0.5, bias=c_one[:, :],
                 wait=W("scalar", [("du2", GI + 1), ("dppr", GI)]), inc=("dtp", 1))
            p.op("vector", "tensor_tensor", pp[:, :], u2f[:, :], tp[:, :],
                 ALU.mult, wait=[("dtp", GI + 1), ("dmsk", GI)], inc=("dfm", 1))
            p.op("gpsimd", "affine_select",
                 ppm[:, :].rearrange("p (g w) -> p g w", w=256)[:, :, 0:128],
                 pp[:, :].rearrange("p (g w) -> p g w", w=256)[:, :, 0:128],
                 pattern=[[0, 4], [1, 128]], compare_op=ALU.is_ge, fill=zeroreg,
                 base=(-100000 if T == 0 else 0), channel_multiplier=-1,
                 wait=W("gpsimd", [("dfm", GI + 1), ("dpT", GI)]))
            p.op("gpsimd", "affine_select",
                 ppm[:, :].rearrange("p (g w) -> p g w", w=256)[:, :, 128:256],
                 pp[:, :].rearrange("p (g w) -> p g w", w=256)[:, :, 128:256],
                 pattern=[[0, 4], [-1, 128]], compare_op=ALU.is_ge, fill=zeroreg,
                 base=0, channel_multiplier=1, inc=("dmsk", 1))
            p.op("vector", "tensor_reduce", tmp4[:, :],
                 ppm[:, :].rearrange("p (g w) -> p g w", w=256), AX.X, ALU.add,
                 wait=[("dmsk", GI + 1)])
            p.fence("vector")
            if T == 0:
                p.op("vector", "tensor_scalar", rs16[:, 4 * g:4 * g + 4],
                     tmp4[:, :], nw0[:, :], None, ALU.add, inc=("dppr", 1))
            else:
                p.op("vector", "tensor_scalar", rs16[:, 4 * g:4 * g + 4],
                     tmp4[:, :], 129.0, None, ALU.add, inc=("dppr", 1))
            for c in range(8):
                p.op("tensor", "transpose", PTD[:, 128 * c:128 * (c + 1)],
                     ppm[:, 128 * c:128 * (c + 1)], ident_b[:, :],
                     wait=([("dmsk", GI + 1), ("dpe", GI), ("doe", T)]
                           if c == 0 else ()),
                     inc=("dpT", 1) if c == 7 else None)
            p.op("vector", "tensor_copy", out=pTg[:, :], in_=PTD[:, :],
                 wait=[("dpT", GI + 1), ("dpv", GI)], inc=("dpe", 1))
            for cc in range(4):
                h = 4 * g + SLOT_HD[cc]
                s = 4 * g + cc
                p.op("tensor", "matmul", PO[:, 64 * s:64 * (s + 1)],
                     LT_b[:, :], vt[(T - 1) % 3][:, 64 * h:64 * (h + 1)],
                     start=True, stop=False,
                     wait=([("dpe", GI + 1)] if cc == 0 else ()))
                p.op("tensor", "matmul", PO[:, 64 * s:64 * (s + 1)],
                     UT_b[:, :], vt[T % 3][:, 64 * h:64 * (h + 1)],
                     start=False, stop=False)
                p.op("tensor", "matmul", PO[:, 64 * s:64 * (s + 1)],
                     pTg[:, 256 * cc:256 * cc + 128],
                     vt[(T - 1) % 3][:, 64 * h:64 * (h + 1)],
                     start=False, stop=False)
                p.op("tensor", "matmul", PO[:, 64 * s:64 * (s + 1)],
                     pTg[:, 256 * cc + 128:256 * cc + 256],
                     vt[T % 3][:, 64 * h:64 * (h + 1)],
                     start=False, stop=True,
                     inc=("dpv", 1) if cc == 3 else None)
        p.op("vector", "reciprocal", rnrm[:, :], rs16[:, :],
             wait=[("dpv", 4 * (T + 1)), ("dppr", 4 * (T + 1))])
        p.fence("vector")
        for a in range(2):
            for b in range(2):
                p.op("vector", "tensor_tensor",
                     onb[:, :].rearrange("p (g e) -> p g e", e=256)[
                         :, :, 128 * a + 64 * b:128 * a + 64 * b + 64],
                     PO[:, :].rearrange("p (g e) -> p g e", e=256)[
                         :, :, 128 * b + 64 * a:128 * b + 64 * a + 64],
                     rnrm[:, :].rearrange("p (g c) -> p g c", c=4)[
                         :, :, 2 * b + a:2 * b + a + 1].broadcast_to((128, 4, 64)),
                     ALU.mult, inc=("donb" if (a, b) == (1, 1) else None))
        for jj in range(8):
            p.op("tensor", "transpose", PTD[:, 128 * jj:128 * (jj + 1)],
                 onb[:, 128 * jj:128 * (jj + 1)], ident_b[:, :],
                 wait=([("donb", T + 1), ("dpe", 4 * (T + 1))] if jj == 0 else ()),
                 inc=("doT" if jj == 7 else None))
        p.op("vector", "tensor_copy", out=onT[:, :],
             in_=PTD[:, :], wait=[("doT", T + 1), ("da1", T)], inc="doe")
        for half, bank in ((0, PA1), (1, PA2)):
            for k in range(8):
                p.op("tensor", "matmul", bank[:, 0:512],
                     onT[:, 128 * k:128 * (k + 1)],
                     WoT_b[:, 1024 * k + 512 * half:1024 * k + 512 * (half + 1)],
                     start=(k == 0), stop=False,
                     wait=([("doe", T + 1), ("dop2", T)]
                           if (half == 0 and k == 0) else ()))
            p.op("tensor", "matmul", bank[:, 0:512], ones1_b[:, :],
                 bo_r[0:1, 512 * half:512 * (half + 1)], start=False, stop=True,
                 inc=("da1" if half == 1 else None))
        for half, bank in ((0, PA1), (1, PA2)):
            p.op("vector", "tensor_tensor", op2[T % 2][:, 512 * half:512 * (half + 1)],
                 bank[:, 0:512], x1t[T % 2][:, 512 * half:512 * (half + 1)],
                 ALU.add,
                 wait=([("da1", T + 1), ("dso", 16 * (T - 1))] if half == 0 else ()))
        p.op("vector", "tensor_tensor", op2[T % 2][:, :], op2[T % 2][:, :],
             bst[T % 2][:, :], ALU.add, inc="dop")
        p.op("vector", "nop", inc="dop2")
        p.op("gpsimd", "dma_start", out=io["op2_d"][t0:t0 + 128, :],
             in_=op2[T % 2][:, :], wait=[("dop", T + 1)], inc="dso", dma=True)
    p.op("gpsimd", "nop", wait=[("dso", 16 * NT)])



def emit_phase_f(p, env, io):
    Wad_b = p.sb("Wad_bF", [128, 8 * 512], BF16)
    Wup_b = p.sb("Wup_bF", [128, 4 * 1024], BF16)
    bad_r = p.sb("bad_rF", [1, 512], BF16)
    bau_r = p.sb("bau_rF", [1, 1024], BF16)
    ohrow = p.sb("ohrowF", [1, 8], F32)
    oh64 = p.sb("oh64F", [1, 512], F32)
    ohcol = p.sb("ohcolF", [128, 4], F32)
    ohT_s = p.sb("ohT_sF", [8, 1], F32)
    bau_all = p.sb("bau_allF", [8, 1024], F32)

    PTF = p.ps("PTF", [128, 1024], BF16)
    PFA = p.ps("PFA", [128, 512], F32)
    PFB1 = p.ps("PFB1", [128, 512], F32)
    PFB2 = p.ps("PFB2", [128, 512], F32)
    PPR = p.ps("PPRF", [128, 512], F32)

    ident_b = env["ident_b"]
    ones1_b = env["ones1_b"]
    onesc_f = env["onesc_f"]

    wstg = [p.sb(f"wstgF{i}", [128, 1024], F32) for i in range(2)]
    wi = 0
    for e in range(8):
        for k in range(8):
            p.op("sync", "dma_start", out=wstg[wi % 2][0:128, 0:64],
                 in_=io["Wa_down"][e, 128 * k:128 * (k + 1), :],
                 wait=[("fwc", wi - 1)], inc="fwl", dma=True)
            p.op("vector", "tensor_copy",
                 out=Wad_b[:, 512 * k + 64 * e:512 * k + 64 * (e + 1)],
                 in_=wstg[wi % 2][0:128, 0:64],
                 wait=[("fwl", 16 * (wi + 1))], inc="fwc")
            wi += 1
    for c in range(4):
        for s in range(2):
            e = 2 * c + s
            p.op("sync", "dma_start", out=wstg[wi % 2][0:64, 0:1024],
                 in_=io["Wa_up"][e], wait=[("fwc", wi - 1)], inc="fwl", dma=True)
            p.op("vector", "tensor_copy",
                 out=Wup_b[64 * s:64 * (s + 1), 1024 * c:1024 * (c + 1)],
                 in_=wstg[wi % 2][0:64, 0:1024],
                 wait=[("fwl", 16 * (wi + 1))], inc="fwc")
            wi += 1
    p.op("sync", "dma_start", out=wstg[wi % 2][0:1, 0:512],
         in_=io["ba_down"].rearrange("e r -> (e r)").rearrange("(o d) -> o d", o=1),
         wait=[("fwc", wi - 1)], inc="fwl", dma=True)
    p.op("vector", "tensor_copy", out=bad_r[0:1, :], in_=wstg[wi % 2][0:1, 0:512],
         wait=[("fwl", 16 * (wi + 1))], inc="fwc")
    wi += 1
    p.op("sync", "dma_start", out=bau_all[:, :], in_=io["ba_up"],
         wait=[("fwc", wi - 1)], inc="fwl", dma=True)
    p.op("sync", "dma_start", out=ohrow[:, :], in_=io["oh_d"], inc="fwl", dma=True)
    NWF0 = wi
    wi += 1
    p.op("vector", "tensor_copy",
         out=oh64[:, :].rearrange("o (e r) -> o e r", e=8),
         in_=ohrow[:, :].rearrange("o (e r) -> o e r", r=1).broadcast_to((1, 8, 64)),
         wait=[("fwl", 16 * (wi + 1))], inc="foh")
    for c in range(4):
        p.op("tensor", "matmul", PPR[0:128, c:c + 1],
             oh64[0:1, 128 * c:128 * (c + 1)], onesc_f[0:1, 0:1],
             start=True, stop=True,
             wait=([("foh", 1)] if c == 0 else ()),
             inc=("fohm", 1))
    p.op("tensor", "matmul", PPR[0:8, 8:9], ohrow[0:1, :], onesc_f[0:1, 0:1],
         start=True, stop=True, inc=("fohm", 1))
    p.op("vector", "tensor_copy", out=ohcol[:, :], in_=PPR[0:128, 0:4],
         wait=[("fohm", 5)])
    p.op("vector", "tensor_copy", out=ohT_s[:, :], in_=PPR[0:8, 8:9], inc="fohc")
    for c in range(4):
        p.op("vector", "tensor_scalar",
             Wup_b[:, 1024 * c:1024 * (c + 1)], Wup_b[:, 1024 * c:1024 * (c + 1)],
             ohcol[:, c:c + 1], None, ALU.mult,
             wait=([("fohc", 1), ("fwc", p.cnt["fwc"])] if c == 0 else ()),
             inc=("fwup", 1))
    p.op("tensor", "matmul", PPR[0:1, 0:512], ohT_s[:, :], bau_all[:, 0:512],
         start=True, stop=True, wait=[("fohc", 1), ("fwl", 16 * (wi + 1))],
         inc="fbau")
    p.op("vector", "tensor_copy", out=bau_r[0:1, 0:512], in_=PPR[0:1, 0:512],
         wait=[("fbau", 1)], inc="fbaue")
    p.op("tensor", "matmul", PPR[0:1, 0:512], ohT_s[:, :], bau_all[:, 512:1024],
         start=True, stop=True, wait=[("fbaue", 1)], inc="fbau")
    p.op("vector", "tensor_copy", out=bau_r[0:1, 512:1024], in_=PPR[0:1, 0:512],
         wait=[("fbau", 2)], inc="fprep")

    xnTt = [p.sb(f"xnTtF{i}", [128, 1024], BF16) for i in range(2)]
    op2t = [p.sb(f"op2tF{i}", [128, 1024], F32) for i in range(2)]
    a_b = p.sb("a_bF", [128, 512], BF16)
    aT = p.sb("aTF", [128, 512], BF16)
    out_sb = [p.sb(f"out_sbF{i}", [128, 1024], F32) for i in range(2)]

    first_waits = [("fwup", 4), ("fprep", 1)]
    fw = {e: list(first_waits) for e in ENGINES}

    def W(eng, extra=()):
        ws = fw[eng] + list(extra)
        fw[eng] = []
        return ws

    for T in range(NT):
        t0 = T * 128
        p.op("sync", "dma_start", out=xnTt[T % 2][:, :],
             in_=io["xn1T_d"][t0:t0 + 128, :],
             wait=W("sync", [("fpa", T - 1)]), inc="fld", dma=True)
        p.op("sync", "dma_start", out=op2t[T % 2][:, :],
             in_=io["op2_d"][t0:t0 + 128, :],
             wait=[("ffin", T - 1)], inc="fld", dma=True)
        for k in range(8):
            p.op("tensor", "matmul", PFA[:, 0:512],
                 xnTt[T % 2][:, 128 * k:128 * (k + 1)],
                 Wad_b[:, 512 * k:512 * (k + 1)], start=(k == 0), stop=False,
                 wait=(W("tensor", [("fld", 32 * T + 16), ("fab", T)])
                       if k == 0 else ()))
        p.op("tensor", "matmul", PFA[:, 0:512], ones1_b[:, :], bad_r[0:1, :],
             start=False, stop=True, inc="fpa")
        p.op("scalar", "activation", a_b[:, :], PFA[:, 0:512], AF.Relu,
             wait=W("scalar", [("fpa", T + 1), ("faT", T)]), inc="fab")
        for c in range(4):
            p.op("tensor", "transpose", PTF[:, 128 * c:128 * (c + 1)],
                 a_b[:, 128 * c:128 * (c + 1)], ident_b[:, :],
                 wait=([("fab", T + 1), ("faTe", T)] if c == 0 else ()),
                 inc=("faT" if c == 3 else None))
        p.op("vector", "tensor_copy", out=aT[:, :], in_=PTF[:, 0:512],
             wait=W("vector", [("faT", T + 1), ("fup", T)]), inc="faTe")
        for half, bank in ((0, PFB1), (1, PFB2)):
            for c in range(4):
                p.op("tensor", "matmul", bank[:, 0:512],
                     aT[:, 128 * c:128 * (c + 1)],
                     Wup_b[:, 1024 * c + 512 * half:1024 * c + 512 * (half + 1)],
                     start=(c == 0), stop=False,
                     wait=([("faTe", T + 1), ("ffin", T)]
                           if (half == 0 and c == 0) else ()))
            p.op("tensor", "matmul", bank[:, 0:512], ones1_b[:, :],
                 bau_r[0:1, 512 * half:512 * (half + 1)], start=False, stop=True,
                 inc=("fup" if half == 1 else None))
        for half, bank in ((0, PFB1), (1, PFB2)):
            p.op("vector", "tensor_tensor",
                 out_sb[T % 2][:, 512 * half:512 * (half + 1)], bank[:, 0:512],
                 op2t[T % 2][:, 512 * half:512 * (half + 1)], ALU.add,
                 wait=([("fup", T + 1), ("fld", 32 * (T + 1)),
                        ("fso", 16 * (T - 1))] if half == 0 else ()),
                 inc=("ffin" if half == 1 else None))
        p.op("gpsimd", "dma_start", out=io["out_d"][t0:t0 + 128, :],
             in_=out_sb[T % 2][:, :], wait=W("gpsimd", [("ffin", T + 1)]),
             inc="fso", dma=True)
    p.op("gpsimd", "nop", wait=[("fso", 16 * NT)])



def build_fused(phases="abcdf"):
    nc = bass.Bass("TRN2", target_bir_lowering=False)
    E_IN = "ExternalInput"
    INT = "Internal"
    t = lambda n, s, dt, k: nc.dram_tensor(n, s, dt, kind=k).ap()
    g = {}
    g["x0"] = t("x0", [4096, 1024], F32, E_IN)
    for nm, shp in [("Wq", [1024, 512]), ("Wk", [1024, 512]), ("Wg1", [1024, 16]),
                    ("Wg2", [16, 512]), ("Wv", [1024, 1024]), ("Wg", [1024, 1024]),
                    ("Wo", [1024, 1024]), ("bg", [1024]),
                    ("w_gate", [2816, 1024]), ("w_up", [2816, 1024]),
                    ("w_down", [1024, 2816]),
                    ("L1Wg", [1024, 1024]), ("L1bg", [1024]),
                    ("L1Wv", [1024, 1024]), ("L1bv", [1024]),
                    ("L1Wo", [1024, 1024]), ("L1bo", [1024]),
                    ("W_base", [1024, 1024]), ("b_base", [1024]),
                    ("Wd1", [8, 1024, 256]), ("bd1", [8, 256]),
                    ("Wd2", [8, 256, 1024]), ("bd2", [8, 1024]),
                    ("Wa_down", [8, 1024, 64]), ("ba_down", [8, 64]),
                    ("Wa_up", [8, 64, 1024]), ("ba_up", [8, 1024])]:
        g[nm] = t(nm, shp, F32, E_IN)
    g["r1"] = t("r1", [4096, 1024], F32, INT)
    g["xq"] = t("xq_s", [4096, 1024], BF16, INT)
    g["x1"] = t("x1_s", [4096, 1024], F32, INT)
    g["xn1"] = t("xn1_s", [4096, 1024], BF16, INT)
    g["xn1T_d"] = t("xn1T_s", [4096, 1024], BF16, INT)
    g["gT_d"] = t("gT_s", [4096, 1024], BF16, INT)
    g["v_d"] = t("v_s", [4096, 1024], BF16, INT)
    g["base_d"] = t("base_s", [4096, 1024], F32, INT)
    g["denom_d"] = t("denom_s", [4096, 16], F32, INT)
    g["sinv_d"] = t("sinv_s", [1, 16], F32, INT)
    g["oh_d"] = t("oh_s", [1, 8], F32, INT)
    g["op2_d"] = t("op2_s", [4096, 1024], F32, INT)
    g["out_d"] = t("out_d", [4096, 1024], F32, "ExternalOutput")

    p = P(nc)
    env = {}
    emit_consts(p, env)
    emitters = {"a": emit_phase_a, "b": emit_phase_b, "c": emit_phase_ce,
                "d": emit_phase_d, "f": emit_phase_f}
    for ph in phases:
        p.push_scope()
        emitters[ph](p, env, g)
        p.pop_scope()
        p.barrier()
    p.build()
    p.stack.close()
    return nc



_CACHE = {}


def _build_a():
    nc = bass.Bass("TRN2", target_bir_lowering=False)
    io = {}
    io["x0"] = nc.dram_tensor("x0", [4096, 1024], F32, kind="ExternalInput").ap()
    for nm, shp in [("Wq", [1024, 512]), ("Wk", [1024, 512]), ("Wg1", [1024, 16]),
                    ("Wg2", [16, 512]), ("Wv", [1024, 1024]), ("Wg", [1024, 1024]),
                    ("Wo", [1024, 1024])]:
        io[nm] = nc.dram_tensor(nm, shp, F32, kind="ExternalInput").ap()
    io["bg"] = nc.dram_tensor("bg", [1024], F32, kind="ExternalInput").ap()
    io["r1"] = nc.dram_tensor("r1", [4096, 1024], F32, kind="ExternalOutput").ap()
    io["xq"] = nc.dram_tensor("xq", [4096, 1024], BF16, kind="ExternalOutput").ap()
    p = P(nc)
    env = {}
    emit_consts(p, env)
    emit_phase_a(p, env, io)
    p.build()
    p.stack.close()
    return nc


def _build_b():
    nc = bass.Bass("TRN2", target_bir_lowering=False)
    io = {}
    io["xq"] = nc.dram_tensor("xq", [4096, 1024], BF16, kind="ExternalInput").ap()
    io["r1"] = nc.dram_tensor("r1", [4096, 1024], F32, kind="ExternalInput").ap()
    io["w_gate"] = nc.dram_tensor("w_gate", [2816, 1024], F32, kind="ExternalInput").ap()
    io["w_up"] = nc.dram_tensor("w_up", [2816, 1024], F32, kind="ExternalInput").ap()
    io["w_down"] = nc.dram_tensor("w_down", [1024, 2816], F32, kind="ExternalInput").ap()
    io["x1"] = nc.dram_tensor("x1", [4096, 1024], F32, kind="ExternalOutput").ap()
    io["xn1"] = nc.dram_tensor("xn1", [4096, 1024], BF16, kind="ExternalOutput").ap()
    p = P(nc)
    env = {}
    emit_consts(p, env)
    emit_phase_b(p, env, io)
    p.build()
    p.stack.close()
    return nc


def _build_ce():
    nc = bass.Bass("TRN2", target_bir_lowering=False)
    io = {}
    io["xn1"] = nc.dram_tensor("xn1", [4096, 1024], BF16, kind="ExternalInput").ap()
    for nm, shp in [("L1Wg", [1024, 1024]), ("L1bg", [1024]), ("L1Wv", [1024, 1024]),
                    ("L1bv", [1024]), ("W_base", [1024, 1024]), ("b_base", [1024]),
                    ("Wd1", [8, 1024, 256]), ("bd1", [8, 256]),
                    ("Wd2", [8, 256, 1024]), ("bd2", [8, 1024])]:
        io[nm] = nc.dram_tensor(nm, shp, F32, kind="ExternalInput").ap()
    for nm, shp, dt in [("xn1T_d", [4096, 1024], BF16), ("gT_d", [4096, 1024], BF16),
                        ("v_d", [4096, 1024], BF16), ("base_d", [4096, 1024], F32),
                        ("denom_d", [4096, 16], F32), ("sinv_d", [1, 16], F32),
                        ("oh_d", [1, 8], F32)]:
        io[nm] = nc.dram_tensor(nm, shp, dt, kind="ExternalOutput").ap()
    p = P(nc)
    env = {}
    emit_consts(p, env)
    emit_phase_ce(p, env, io)
    p.build()
    p.stack.close()
    return nc


def _build_d():
    nc = bass.Bass("TRN2", target_bir_lowering=False)
    io = {}
    for nm, shp, dt in [("gT_d", [4096, 1024], BF16), ("v_d", [4096, 1024], BF16),
                        ("x1", [4096, 1024], F32), ("base_d", [4096, 1024], F32),
                        ("denom_d", [4096, 16], F32), ("sinv_d", [1, 16], F32),
                        ("L1Wo", [1024, 1024], F32), ("L1bo", [1024], F32)]:
        io[nm] = nc.dram_tensor(nm, shp, dt, kind="ExternalInput").ap()
    io["op2_d"] = nc.dram_tensor("op2_d", [4096, 1024], F32, kind="ExternalOutput").ap()
    p = P(nc)
    env = {}
    emit_consts(p, env)
    emit_phase_d(p, env, io)
    p.build()
    p.stack.close()
    return nc


def _build_f():
    nc = bass.Bass("TRN2", target_bir_lowering=False)
    io = {}
    for nm, shp, dt in [("xn1T_d", [4096, 1024], BF16), ("op2_d", [4096, 1024], F32),
                        ("oh_d", [1, 8], F32), ("Wa_down", [8, 1024, 64], F32),
                        ("ba_down", [8, 64], F32), ("Wa_up", [8, 64, 1024], F32),
                        ("ba_up", [8, 1024], F32)]:
        io[nm] = nc.dram_tensor(nm, shp, dt, kind="ExternalInput").ap()
    io["out_d"] = nc.dram_tensor("out_d", [4096, 1024], F32, kind="ExternalOutput").ap()
    p = P(nc)
    env = {}
    emit_consts(p, env)
    emit_phase_f(p, env, io)
    p.build()
    p.stack.close()
    return nc


def kernel(**inputs):
    cores = list(range(8))
    x0 = np.ascontiguousarray(inputs["L0_attn_inputs"], dtype=np.float32)
    nc = _CACHE.setdefault("fused", build_fused())
    in_maps = [{
        "x0": np.ascontiguousarray(x0[c]),
        "Wq": inputs["L0_attn_Wq"], "Wk": inputs["L0_attn_Wk"],
        "Wg1": inputs["L0_attn_Wg1"], "Wg2": inputs["L0_attn_Wg2"],
        "Wv": inputs["L0_attn_Wv"], "Wg": inputs["L0_attn_Wg"],
        "Wo": inputs["L0_attn_Wo"], "bg": inputs["L0_attn_bg"],
        "w_gate": inputs["L0_ffn_w_gate"], "w_up": inputs["L0_ffn_w_up"],
        "w_down": inputs["L0_ffn_w_down"],
        "L1Wg": inputs["L1_attn_Wg"], "L1bg": inputs["L1_attn_bg"],
        "L1Wv": inputs["L1_attn_Wv"], "L1bv": inputs["L1_attn_bv"],
        "L1Wo": inputs["L1_attn_Wo"], "L1bo": inputs["L1_attn_bo"],
        "W_base": inputs["L1_ffn_W_base"], "b_base": inputs["L1_ffn_b_base"],
        "Wd1": inputs["L1_ffn_Wd1"], "bd1": inputs["L1_ffn_bd1"],
        "Wd2": inputs["L1_ffn_Wd2"], "bd2": inputs["L1_ffn_bd2"],
        "Wa_down": inputs["L1_ffn_Wa_down"], "ba_down": inputs["L1_ffn_ba_down"],
        "Wa_up": inputs["L1_ffn_Wa_up"], "ba_up": inputs["L1_ffn_ba_up"],
    } for c in range(8)]
    res = run_bass_kernel_spmd(nc, in_maps, core_ids=cores).results
    return np.stack([res[c]["out_d"] for c in range(8)]).astype(np.float32)


def kernel_multilaunch(**inputs):
    cores = list(range(8))
    x0 = np.ascontiguousarray(inputs["L0_attn_inputs"], dtype=np.float32)

    nc_a = _CACHE.setdefault("a", _build_a())
    in_a = [{
        "x0": np.ascontiguousarray(x0[c]),
        "Wq": inputs["L0_attn_Wq"], "Wk": inputs["L0_attn_Wk"],
        "Wg1": inputs["L0_attn_Wg1"], "Wg2": inputs["L0_attn_Wg2"],
        "Wv": inputs["L0_attn_Wv"], "Wg": inputs["L0_attn_Wg"],
        "Wo": inputs["L0_attn_Wo"], "bg": inputs["L0_attn_bg"],
    } for c in range(8)]
    ra = run_bass_kernel_spmd(nc_a, in_a, core_ids=cores).results

    nc_b = _CACHE.setdefault("b", _build_b())
    in_b = [{
        "xq": ra[c]["xq"], "r1": ra[c]["r1"],
        "w_gate": inputs["L0_ffn_w_gate"], "w_up": inputs["L0_ffn_w_up"],
        "w_down": inputs["L0_ffn_w_down"],
    } for c in range(8)]
    rb = run_bass_kernel_spmd(nc_b, in_b, core_ids=cores).results

    nc_ce = _CACHE.setdefault("ce", _build_ce())
    in_ce = [{
        "xn1": rb[c]["xn1"],
        "L1Wg": inputs["L1_attn_Wg"], "L1bg": inputs["L1_attn_bg"],
        "L1Wv": inputs["L1_attn_Wv"], "L1bv": inputs["L1_attn_bv"],
        "W_base": inputs["L1_ffn_W_base"], "b_base": inputs["L1_ffn_b_base"],
        "Wd1": inputs["L1_ffn_Wd1"], "bd1": inputs["L1_ffn_bd1"],
        "Wd2": inputs["L1_ffn_Wd2"], "bd2": inputs["L1_ffn_bd2"],
    } for c in range(8)]
    rc = run_bass_kernel_spmd(nc_ce, in_ce, core_ids=cores).results

    nc_d = _CACHE.setdefault("d", _build_d())
    in_d = [{
        "gT_d": rc[c]["gT_d"], "v_d": rc[c]["v_d"], "x1": rb[c]["x1"],
        "base_d": rc[c]["base_d"], "denom_d": rc[c]["denom_d"],
        "sinv_d": rc[c]["sinv_d"],
        "L1Wo": inputs["L1_attn_Wo"], "L1bo": inputs["L1_attn_bo"],
    } for c in range(8)]
    rd = run_bass_kernel_spmd(nc_d, in_d, core_ids=cores).results

    nc_f = _CACHE.setdefault("f", _build_f())
    in_f = [{
        "xn1T_d": rc[c]["xn1T_d"], "op2_d": rd[c]["op2_d"], "oh_d": rc[c]["oh_d"],
        "Wa_down": inputs["L1_ffn_Wa_down"], "ba_down": inputs["L1_ffn_ba_down"],
        "Wa_up": inputs["L1_ffn_Wa_up"], "ba_up": inputs["L1_ffn_ba_up"],
    } for c in range(8)]
    rf = run_bass_kernel_spmd(nc_f, in_f, core_ids=cores).results

    return np.stack([rf[c]["out_d"] for c in range(8)]).astype(np.float32)
